# version 05
# baseline (speedup 1.0000x reference)
import numpy as np
import ml_dtypes
from contextlib import ExitStack
import concourse.bass as bass
import concourse.mybir as mybir
from concourse.alu_op_type import AluOpType as OP
from concourse.bass_utils import run_bass_kernel_spmd

F32 = mybir.dt.float32
F32R = mybir.dt.float32r
BF16 = mybir.dt.bfloat16
AF = mybir.ActivationFunctionType
AX = mybir.AxisListType
BF = ml_dtypes.bfloat16

D = 1024
L = 8192
NCH = 8
NT = 512
NTILES = L // NT
MACRO = 2048
NMACRO = L // MACRO
NINNER = MACRO // NT

ENGS = ("sync", "tensor", "vector", "scalar", "gpsimd")
LAST_PB = None
AUTOCHAIN = {"vector": "dv", "scalar": "ac", "gpsimd": "po"}


NPOOL = 32


class PB:
    def __init__(self):
        self.ops = []
        self.cnt = {}
        self.qn = {"qy": 0, "qs": 0, "qg": 0}

    def _emit(self, eng, fn, waits, inc_sem, inc_amt):
        ws = []
        for w in waits:
            if w is None:
                continue
            if isinstance(w, list):
                ws.extend(x for x in w if x is not None)
            else:
                ws.append(w)
        if eng in AUTOCHAIN and fn is not None:
            sem = AUTOCHAIN[eng]
            prev = self.cnt.get(sem, 0)
            if prev > 0:
                ws.append((sem, prev))
            if inc_sem is None:
                inc_sem, inc_amt = sem, 1
        ev = None
        if inc_sem is not None:
            self.cnt[inc_sem] = self.cnt.get(inc_sem, 0) + inc_amt
            ev = (inc_sem, self.cnt[inc_sem])
        self.ops.append((eng, fn, ws, (inc_sem, inc_amt) if inc_sem else None))
        return ev

    def t(self, fn, waits=()):
        return self._emit("tensor", fn, waits, "mm", 1)

    def v(self, fn, waits=()):
        return self._emit("vector", fn, waits, None, 1)

    def s(self, fn, waits=()):
        return self._emit("scalar", fn, waits, None, 1)

    def g(self, fn, waits=()):
        return self._emit("gpsimd", fn, waits, None, 1)

    def _dma_emit(self, queue, fn, waits):
        eng = {"y": "sync", "s": "scalar", "g": "gpsimd"}[queue]
        base = {"y": "qy", "s": "qs", "g": "qg"}[queue]
        i = self.qn[base]
        self.qn[base] = i + 1
        sem = f"{base}{i % NPOOL}"
        ws = list(waits)
        if i >= NPOOL:
            ws.append((sem, 16 * (i // NPOOL)))
        return self._emit(eng, fn, ws, sem, 16)

    def dma(self, queue, out, in_, waits=()):
        fn = lambda e, out=out, in_=in_: e.dma_start(out=out, in_=in_)
        return self._dma_emit(queue, fn, waits)

    def dma_tr(self, queue, out, in_, waits=()):
        fn = lambda e, out=out, in_=in_: e.dma_start_transpose(out, in_)
        return self._dma_emit(queue, fn, waits)

    def last(self, sem):
        c = self.cnt.get(sem, 0)
        return (sem, c) if c > 0 else None

    def queue_events(self, base):
        return [(s, c) for s, c in self.cnt.items() if s.startswith(base) and c > 0]

    def barrier(self):
        evs = [(s, c) for s, c in self.cnt.items()]
        for eng in ENGS:
            self._emit(eng, None, evs, None, 1)

    def finalize(self, block, sems):
        for eng_name in ENGS:
            deco = getattr(block, eng_name)

            def make(eng_name):
                def body(eng):
                    for op_eng, fn, waits, inc in self.ops:
                        if op_eng != eng_name:
                            continue
                        best = {}
                        for s, v in waits:
                            if best.get(s, -1) < v:
                                best[s] = v
                        for s, v in best.items():
                            eng.wait_ge(sems[s], v)
                        if fn is None:
                            continue
                        inst = fn(eng)
                        if inc is not None:
                            inst.then_inc(sems[inc[0]], inc[1])
                return body

            deco(make(eng_name))


class Buf:
    __slots__ = ("wr", "rd")

    def __init__(self):
        self.wr = None
        self.rd = []

    def war(self):
        return self.rd + ([self.wr] if self.wr else [])

    def wrote(self, ev):
        self.wr = ev
        self.rd = []

    def read(self, ev):
        self.rd.append(ev)


class Ctx:
    pass


def fm_dram(ap, t0, n):
    return ap[:, t0:t0 + n].rearrange("(c p) n -> p c n", p=128)


def fm_sbuf(tile, nch=NCH, nt=NT):
    return tile[:, 0:nch * nt].rearrange("p (c n) -> p c n", c=nch)


def make_ctx(es, nc, P):
    C = Ctx()
    C.nc = nc
    C.P = P

    def sb(name, shape, dt):
        return es.enter_context(nc.sbuf_tensor(name, shape, dt))
    C.sb = sb

    C.pb = [es.enter_context(nc.psum_tensor(f"pb{i}", [128, 512], F32)) for i in range(6)]
    C.pt = [es.enter_context(nc.psum_tensor(f"pt{i}", [128, 1024], BF16)) for i in range(2)]
    C.pb_ev = [None] * 6
    C.pt_ev = [None] * 2

    C.ones128 = sb("ones128", [128, 1], BF16)
    C.ident = sb("ident", [128, 128], BF16)
    C.oh8T = sb("oh8T", [8, 8 * 128], BF16)
    C.eps_tiles = {}
    C.const_defs = []

    C.xt = sb("xt", [128, NCH * NT], F32)
    C.xt_b = Buf()
    C.lx = [sb(f"lx{i}", [128, NCH * NT], BF16) for i in range(4)]
    C.lx_b = [Buf() for _ in range(4)]
    C.xnew = C.xt
    C.xnew_b = C.xt_b
    C.xbf = sb("xbf", [128, NCH * NT], BF16)
    C.tc = sb("tc", [128, NCH * NT], BF16)
    C.sqt = C.xbf
    C.murow = sb("murow", [1, NT], F32)
    C.bch = sb("bch", [1, NT], BF16)
    C.bcl = sb("bcl", [1, NT], BF16)
    C.ones1_b = sb("ones1_b", [1, 128], BF16)
    C.lnrow = C.murow
    C.rstdrow = sb("rstdrow", [1, NT], F32)

    C.wA = sb("wA", [128, 8 * 2048], BF16)
    C.wA_b = Buf()
    C.wB = sb("wB", [128, 16 * 1024], BF16)
    C.wB_b = Buf()
    C.gh = sb("gh", [128, 16 * NT], BF16)
    C.acc = sb("acc", [128, NCH * MACRO], BF16)
    C.acc_b = Buf()
    C.scr = C.tc
    C.wTm = sb("wTm", [8, MACRO], BF16)
    C.wTm_b = Buf()
    C.wTst = sb("wTst", [8, NT], BF16)
    C.wTst_b = Buf()
    C.csrow_b = Buf()
    C.gbrow = sb("gbrow", [1, 8], F32)
    C.gbb = sb("gbb", [128, 8], F32)
    C.g_sc = sb("g_sc", [128, 8], F32)
    C.g_sc2 = sb("g_sc2", [128, 8], F32)
    C.g_m = sb("g_m", [128, 1], F32)
    C.g_m2 = sb("g_m2", [128, 1], F32)
    C.g_s = sb("g_s", [128, 1], F32)
    C.g_w = sb("g_w", [128, 8], BF16)
    C.g_wcs = sb("g_wcs", [128, 8], BF16)
    C.wbc = sb("wbc", [128, NT], BF16)
    C.wbc_b = Buf()
    C.kvs = sb("kvs", [128, 512], BF16)
    C.kvs_b = Buf()
    C.bkb = C.gh[:, 6144:7168]
    C.bvb = C.gh[:, 7168:8192]
    C.rst16 = C.gh[0:16, 4096:4608]
    C.wsc4 = sb("wsc4", [128, 512], F32)
    C.rst16f = sb("rst16f", [16, NT], F32)
    C.csst = C.rst16f[0:1, :]
    C.ss16 = sb("ss16", [128, 16], F32)
    C.rr16 = sb("rr16", [128, 16], F32)
    C.l1a_cq = sb("l1a_cq", [128, 4 * 8], BF16)
    C.l1a_ck = sb("l1a_ck", [128, 4 * 8], BF16)
    C.l1a_cv = sb("l1a_cv", [128, 4 * 8], BF16)
    C.l1a_lw = sb("l1a_lw", [128, 7 * 8], BF16)
    C.l1a_mw = sb("l1a_mw", [128, 25 * 8], BF16)
    C.hm4 = sb("hm4", [128, 8 * 4], BF16)
    C.oh4T = sb("oh4T", [4, 4 * 128], BF16)
    C.maskL = sb("maskL", [128, 128], BF16)
    C.maskU = sb("maskU", [128, 128], BF16)
    C.maskT = sb("maskT", [128, 128], BF16)
    C.r4a = C.xt[0:4, 3072:3584]
    C.r4b = C.xt[0:4, 3584:4096]
    C.r4c = C.sqt[0:8, 0:512]
    C.r4d = C.sqt[0:8, 512:1024]
    C.scal = C.xt[:, 2048:3072]
    C.S = [C.xt[:, h * 512:(h + 1) * 512] for h in range(4)]
    C.rw1x = C.acc[0:56, 0:2160]
    C.f56T = C.kvs[0:56, :]
    C.rw2s = C.wA[0:128, 2048:2320]
    C.h1x = C.wA[:, 2320:2832]
    C.mwb = C.wA[:, 0:1024]
    C.mwo = C.wA[:, 1024:2048]
    C.rb1h = sb("rb1h", [128, 17], F32)
    C.rb2b = sb("rb2b", [128, 16], F32)
    C.p16 = sb("p16", [128, 16], F32)
    C.f56 = sb("f56", [128, 64], BF16)
    C.f32sc = sb("f32sc", [128, 16], F32)
    C.hm16 = sb("hm16", [128, 8 * 16], BF16)
    C.hm16T = sb("hm16T", [16, 8 * 128], BF16)
    C.l0a_bq = sb("l0a_bq", [128, 8], F32)
    C.l0a_bo = sb("l0a_bo", [128, 8], F32)
    C.l0a_g1 = sb("l0a_g1", [128, 8], F32)
    C.l0a_be1 = sb("l0a_be1", [128, 8], F32)
    C.l0a_g2 = sb("l0a_g2", [128, 8], F32)
    C.l0a_be2 = sb("l0a_be2", [128, 8], F32)
    C.l0a_b1 = sb("l0a_b1", [128, 32], F32)
    C.l0a_b2 = sb("l0a_b2", [128, 8], F32)
    return C


def emit_bcast_row(C, row_f32, psum_ap, waits):
    P = C.P
    n = row_f32.shape[-1]
    eh = P.v(lambda v: v.tensor_copy(C.bch[0:1, 0:n], row_f32), waits=waits)
    el = P.v(lambda v: v.tensor_tensor(C.bcl[0:1, 0:n], row_f32, C.bch[0:1, 0:n],
                                       OP.subtract))
    P.t(lambda t: t.matmul(psum_ap, C.ones1_b[:], C.bch[0:1, 0:n], start=True, stop=False),
        waits=[el] + waits)
    return P.t(lambda t: t.matmul(psum_ap, C.ones1_b[:], C.bcl[0:1, 0:n],
                                  start=False, stop=True))


def eps_tile(C, value):
    if value not in C.eps_tiles:
        assert not getattr(C, "const_frozen", False), f"const {value} not pre-registered"
        t = C.sb(f"cst{len(C.eps_tiles)}", [128, 1], F32)
        C.eps_tiles[value] = t
        C.const_defs.append((t, value))
    return C.eps_tiles[value]


def emit_const_init(C):
    P = C.P
    for t, val in C.const_defs:
        P.v(lambda v, t=t, val=val: v.memset(t[:], val))



def emit_tail_tile(C, ti, xnew_ev, xdst, lndst, store_x=True):
    P = C.P
    t0 = ti * NT
    if store_x:
        ev = P.dma("y", fm_dram(xdst, t0, NT), fm_sbuf(C.xnew), waits=[xnew_ev])
        C.xnew_b.read(ev)
    if lndst is None:
        return
    ev_cast = P.v(lambda v: v.tensor_copy(C.xbf[:], C.xnew[:]), waits=[xnew_ev])
    C.xnew_b.read(ev_cast)
    ev_mu = None
    for c in range(NCH):
        ev_mu = P.t(lambda t, c=c: t.matmul(C.pb[4][0:1, :], C.ones128[:],
                                            C.xbf[:, c * NT:(c + 1) * NT],
                                            start=(c == 0), stop=(c == NCH - 1)),
                    waits=[ev_cast, C.pb_ev[4]])
    ev_mur = P.s(lambda s: s.activation(C.murow[:], C.pb[4][0:1, :], AF.Copy, scale=1.0 / D),
                 waits=[ev_mu])
    ev_mub = emit_bcast_row(C, C.murow[0:1, :], C.pb[5][:, :],
                            [ev_mur, C.pb_ev[5]])
    ev_tc = None
    for c in range(NCH):
        ev_tc = P.v(lambda v, c=c: v.tensor_tensor(C.tc[:, c * NT:(c + 1) * NT],
                                                   C.xbf[:, c * NT:(c + 1) * NT],
                                                   C.pb[5][:, :], OP.subtract),
                    waits=[ev_mub])
    ev_sq = None
    for c in range(NCH):
        ev_sq = P.v(lambda v, c=c: v.tensor_tensor(C.sqt[:, c * NT:(c + 1) * NT],
                                                   C.tc[:, c * NT:(c + 1) * NT],
                                                   C.tc[:, c * NT:(c + 1) * NT], OP.mult),
                    waits=[ev_tc])
    ev_var = None
    for c in range(NCH):
        ev_var = P.t(lambda t, c=c: t.matmul(C.pb[4][0:1, :], C.ones128[:],
                                             C.sqt[:, c * NT:(c + 1) * NT],
                                             start=(c == 0), stop=(c == NCH - 1)),
                     waits=[ev_sq, ev_mur])
    ev_ln = P.s(lambda s: s.activation(C.lnrow[:], C.pb[4][0:1, :], AF.Ln,
                                       bias=eps_tile(C, 1e-5)[0:1, :], scale=1.0 / D),
                waits=[ev_var])
    C.pb_ev[4] = ev_ln
    ev_rstd = P.s(lambda s: s.activation(C.rstdrow[:], C.lnrow[:], AF.Exp, scale=-0.5),
                  waits=[ev_ln])
    ev_rb = emit_bcast_row(C, C.rstdrow[0:1, :], C.pb[5][:, :],
                           [ev_rstd, ev_tc])
    lxi = ti % 4
    lxb = C.lx[lxi]
    ev_lnx = None
    for c in range(NCH):
        ev_lnx = P.v(lambda v, c=c, lxb=lxb: v.tensor_tensor(
            lxb[:, c * NT:(c + 1) * NT], C.tc[:, c * NT:(c + 1) * NT],
            C.pb[5][:, :], OP.mult),
            waits=[ev_rb] + C.lx_b[lxi].war())
    C.pb_ev[5] = ev_lnx
    C.lx_b[lxi].wrote(ev_lnx)
    ev_st = P.dma("y", fm_dram(lndst, t0, NT), fm_sbuf(lxb), waits=[ev_lnx])
    C.lx_b[lxi].read(ev_st)


def emit_init_pass(C, xsrc, lndst):
    P = C.P
    for ti in range(NTILES):
        ev_ld = P.dma("y", fm_sbuf(C.xnew), fm_dram(xsrc, ti * NT, NT),
                      waits=C.xnew_b.war())
        C.xnew_b.wrote(ev_ld)
        emit_tail_tile(C, ti, ev_ld, None, lndst, store_x=False)



def emit_gates(C, gate_w_dram, gate_b_dram, lnsrc, wTdram, renorm, with_cs=None):
    P = C.P
    ev_gw = P.dma("g", C.wA[:, 0:64].rearrange("p (c e) -> p c e", c=8),
                  gate_w_dram.rearrange("(c p) e -> p c e", p=128),
                  waits=C.wA_b.war())
    C.wA_b.wrote(ev_gw)
    ev_gbb = P.dma("y", C.gbb[:], gate_b_dram)
    last_mm = None
    prev_pe_reads = []
    for ti in range(NTILES):
        t0 = ti * NT
        lxi = ti % 4
        lxb = C.lx[lxi]
        ev_lx = P.dma("y", fm_sbuf(lxb), fm_dram(lnsrc, t0, NT),
                      waits=C.lx_b[lxi].war())
        C.lx_b[lxi].wrote(ev_lx)
        for sub in range(NT // 128):
            tt = t0 + sub * 128
            evm = None
            for c in range(NCH):
                evm = P.t(lambda t, c=c, sub=sub, lxb=lxb: t.matmul(
                    C.pb[4][:, 0:8],
                    lxb[:, c * NT + sub * 128: c * NT + sub * 128 + 128],
                    C.wA[:, c * 8:(c + 1) * 8],
                    start=(c == 0), stop=(c == NCH - 1)),
                    waits=[ev_lx, ev_gw, C.pb_ev[4]])
            last_mm = evm
            e1 = P.v(lambda v: v.tensor_tensor(C.g_sc[:], C.pb[4][:, 0:8], C.gbb[:], OP.add),
                     waits=[evm, ev_gbb] + prev_pe_reads)
            prev_pe_reads = []
            C.pb_ev[4] = e1
            P.v(lambda v: v.tensor_reduce(C.g_m[:], C.g_sc[:], axis=AX.X,
                                          op=OP.max, negate=True))
            e3 = P.s(lambda s: s.activation(C.g_sc2[:], C.g_sc[:], AF.Exp,
                                            bias=C.g_m[:], scale=1.0),
                     waits=[P.last("dv")])
            P.v(lambda v: v.tensor_reduce(C.g_m[:], C.g_sc2[:], axis=AX.X, op=OP.add),
                waits=[e3])
            P.v(lambda v: v.reciprocal(C.g_m2[:], C.g_m[:]))
            P.v(lambda v: v.tensor_scalar(C.g_sc[:], C.g_sc2[:], C.g_m2[:], None,
                                          op0=OP.mult))
            P.v(lambda v: v.tensor_reduce(C.g_m[:], C.g_sc[:], axis=AX.X, op=OP.max))
            P.v(lambda v: v.tensor_scalar(C.g_sc2[:], C.g_sc[:], C.g_m[:], None,
                                          op0=OP.is_ge))
            P.v(lambda v: v.scalar_tensor_tensor(C.g_sc2[:], C.g_sc2[:], -1e30,
                                                 C.g_sc[:], op0=OP.mult, op1=OP.add))
            P.v(lambda v: v.tensor_reduce(C.g_m2[:], C.g_sc2[:], axis=AX.X, op=OP.max))
            P.v(lambda v: v.tensor_scalar(C.g_sc2[:], C.g_sc[:], C.g_m2[:], None,
                                          op0=OP.is_ge))
            ew = P.v(lambda v: v.tensor_tensor(C.g_sc[:], C.g_sc[:], C.g_sc2[:], OP.mult))
            if renorm:
                P.v(lambda v: v.tensor_reduce(C.g_m[:], C.g_sc[:], axis=AX.X, op=OP.add))
                P.v(lambda v: v.tensor_scalar(C.g_m2[:], C.g_m[:], 1e-9, None, op0=OP.add))
                P.v(lambda v: v.reciprocal(C.g_s[:], C.g_m2[:]))
                ew = P.v(lambda v: v.tensor_scalar(C.g_w[:], C.g_sc[:], C.g_s[:], None,
                                                   op0=OP.mult))
                if with_cs:
                    ecs = P.v(lambda v: v.tensor_tensor(
                        C.g_wcs[:], C.g_m[:].broadcast_to((128, 8)), C.g_s[:].broadcast_to((128, 8)),
                        OP.mult))
                    evt2 = P.t(lambda t: t.transpose(C.pt[1][0:8, 0:128], C.g_wcs[:],
                                                     C.ident[:]),
                               waits=[ecs, C.pt_ev[1]])
                    prev_pe_reads.append(evt2)
                    evc2 = P.s(lambda s, sub=sub: s.copy(C.csst[0:1, sub * 128:(sub + 1) * 128],
                                                         C.pt[1][0:1, 0:128]),
                               waits=[evt2] + ([C.csrow_b.wr] if sub == 0 else []))
                    C.pt_ev[1] = evc2
                    if sub == NT // 128 - 1:
                        evd = P.dma("y", with_cs[0:1, t0:t0 + NT], C.csst[:],
                                    waits=[evc2])
                        C.csrow_b.wrote(evd)
            else:
                ew = P.v(lambda v: v.tensor_copy(C.g_w[:], C.g_sc[:]))
            evt = P.t(lambda t: t.transpose(C.pt[0][0:8, 0:128], C.g_w[:], C.ident[:]),
                      waits=[ew, C.pt_ev[0]] + (C.wTst_b.rd if sub == 0 else []))
            prev_pe_reads.append(evt)
            evc = P.s(lambda s, sub=sub: s.copy(C.wTst[0:8, sub * 128:(sub + 1) * 128],
                                                C.pt[0][0:8, 0:128]),
                      waits=[evt])
            C.pt_ev[0] = evc
            if sub == NT // 128 - 1:
                evw = P.dma("y", wTdram[:, t0:t0 + NT], C.wTst[:], waits=[evc])
                C.wTst_b.rd = [evw]
                C.wTst_b.wr = evw
        C.lx_b[lxi].read(last_mm)
    C.wA_b.read(last_mm)



def emit_l0ffn(C, W, xsrc, lnsrc, xdst, lndst, wTdram):
    P = C.P
    emit_gates(C, W["gate_w"], W["gate_b"], lnsrc, wTdram, renorm=False)
    if not hasattr(C, "l0f_b1"):
        C.l0f_b1 = C.sb("l0f_b1", [128, 8 * 16], F32)
        C.l0f_b2 = C.sb("l0f_b2", [8, 1024], BF16)
    b1s, b2s = C.l0f_b1, C.l0f_b2
    ev_b1 = P.dma("y", b1s[:].rearrange("p (e h) -> p e h", e=8),
                  W["b1"].rearrange("e p h -> p e h"))
    ev_b2 = P.dma("g", b2s[:], W["b2"])

    for mi in range(NMACRO):
        m0 = mi * MACRO
        ev_wt = P.dma("y", C.wTm[:], wTdram[:, m0:m0 + MACRO],
                      waits=C.wTm_b.war() + [C.wTst_b.wr])
        C.wTm_b.wrote(ev_wt)
        for ii in range(NINNER):
            for c in range(NCH):
                bk = 4 + (c % 2)
                evi = P.t(lambda t, c=c, ii=ii, bk=bk: t.matmul(
                    C.pb[bk][:, :], b2s[0:8, c * 128:(c + 1) * 128],
                    C.wTm[0:8, ii * NT:(ii + 1) * NT],
                    start=True, stop=True),
                    waits=[ev_b2, ev_wt, C.pb_ev[bk]])
                eva = P.s(lambda s, c=c, ii=ii, bk=bk: s.copy(
                    C.acc[:, c * MACRO + ii * NT: c * MACRO + (ii + 1) * NT],
                    C.pb[bk][:, :]),
                    waits=[evi] + (C.acc_b.war() if (ii == 0 and c == 0) else []))
                C.pb_ev[bk] = eva
        C.acc_b.wrote(P.last("ac"))
        for e in range(8):
            ev_w1 = P.dma("g", C.wA[:].rearrange("p (c n) -> p c n", c=8),
                          W["w1"][e].rearrange("(c p) n -> p c n", p=128),
                          waits=C.wA_b.war())
            C.wA_b.wrote(ev_w1)
            ev_w2 = P.dma("g", C.wB[:].rearrange("p (c n) -> p c n", c=16),
                          W["w2"][e].rearrange("(c p) n -> p c n", p=128),
                          waits=C.wB_b.war())
            C.wB_b.wrote(ev_w2)
            for ii in range(NINNER):
                i0 = m0 + ii * NT
                lxb = C.lx[ii]
                if e == 0:
                    ev_lx = P.dma("y", fm_sbuf(lxb), fm_dram(lnsrc, i0, NT),
                                  waits=C.lx_b[ii].war())
                    C.lx_b[ii].wrote(ev_lx)
                ev_lx = C.lx_b[ii].wr
                ev_h_last = None
                evh = None
                for j in range(16):
                    bk = j % 2
                    for c in range(NCH):
                        evh = P.t(lambda t, j=j, c=c, bk=bk, lxb=lxb: t.matmul(
                            C.pb[bk][:, :],
                            C.wA[:, c * 2048 + j * 128: c * 2048 + (j + 1) * 128],
                            lxb[:, c * NT:(c + 1) * NT],
                            start=(c == 0), stop=(c == NCH - 1)),
                            waits=[ev_w1, ev_lx, C.pb_ev[bk]])
                    evr = P.s(lambda s, j=j, bk=bk, e=e: s.activation(
                        C.gh[:, j * NT:(j + 1) * NT], C.pb[bk][:, :], AF.Relu,
                        bias=b1s[:, e * 16 + j: e * 16 + j + 1], scale=1.0),
                        waits=[evh, ev_b1])
                    C.pb_ev[bk] = evr
                    ev_h_last = evr
                if ii == NINNER - 1:
                    C.wA_b.read(evh)
                if e == 7:
                    C.lx_b[ii].read(evh)
                evgb = P.t(lambda t, e=e, ii=ii: t.matmul(
                    C.pb[2][:, :], C.oh8T[0:8, e * 128:(e + 1) * 128],
                    C.wTm[0:8, ii * NT:(ii + 1) * NT], start=True, stop=True),
                    waits=[ev_wt, C.pb_ev[2]])
                evgbc = P.s(lambda s: s.copy(C.wbc[:], C.pb[2][:, :]),
                            waits=[evgb] + C.wbc_b.war())
                C.wbc_b.wrote(evgbc)
                C.pb_ev[2] = evgbc
                evo = None
                for c in range(NCH):
                    bk = 2 if c % 2 == 0 else 3
                    for j in range(16):
                        evo = P.t(lambda t, c=c, j=j, bk=bk: t.matmul(
                            C.pb[bk][:, :],
                            C.wB[:, j * 1024 + c * 128: j * 1024 + (c + 1) * 128],
                            C.gh[:, j * NT:(j + 1) * NT],
                            start=(j == 0), stop=(j == 15)),
                            waits=[ev_w2, ev_h_last, C.pb_ev[bk]])
                    accsl = C.acc[:, c * MACRO + ii * NT: c * MACRO + (ii + 1) * NT]
                    evt1 = P.v(lambda v, bk=bk: v.tensor_tensor(
                        C.scr[:, 0:NT], C.pb[bk][:, :], C.wbc[:], OP.mult),
                        waits=[evo, evgbc])
                    C.wbc_b.read(evt1)
                    P.v(lambda v, accsl=accsl: v.tensor_tensor(
                        accsl, accsl, C.scr[:, 0:NT], OP.add),
                        waits=[C.acc_b.wr])
                    C.pb_ev[bk] = evt1
                if ii == NINNER - 1:
                    C.wB_b.read(evo)
                    C.wTm_b.read(evgb)
        acc_done = P.last("dv")
        for ii in range(NINNER):
            i0 = m0 + ii * NT
            ev_x = P.dma("y", fm_sbuf(C.xt), fm_dram(xsrc, i0, NT),
                         waits=C.xt_b.war())
            C.xt_b.wrote(ev_x)
            ev_new = None
            for c in range(NCH):
                accsl = C.acc[:, c * MACRO + ii * NT: c * MACRO + (ii + 1) * NT]
                ev_new = P.v(lambda v, c=c, accsl=accsl: v.scalar_tensor_tensor(
                    C.xnew[:, c * NT:(c + 1) * NT], accsl, 0.5,
                    C.xt[:, c * NT:(c + 1) * NT], op0=OP.mult, op1=OP.add),
                    waits=[acc_done, ev_x] + C.xnew_b.war())
            C.xt_b.read(ev_new)
            C.acc_b.read(ev_new)
            C.xnew_b.wrote(ev_new)
            emit_tail_tile(C, i0 // NT, ev_new, xdst, lndst, store_x=True)




HL = 32


def _halo(buf, off):
    return buf[:, off:off + 8 * 544].rearrange("p (c n) -> p c n", c=8)


def emit_l1attn_p1(C, W, lnsrc, bufs):
    P = C.P
    rows = bufs["rows"]
    ev_c = []
    for dst, key in [(C.l1a_cq, "convq"), (C.l1a_ck, "convk"), (C.l1a_cv, "convv"),
                     (C.l1a_lw, "localw"), (C.l1a_mw, "midw"), (C.hm4, "hm4"),
                     (C.oh4T, "oh4T"), (C.maskL, "maskL"), (C.maskU, "maskU"),
                     (C.maskT, "maskT")]:
        ev_c.append(P.dma("y", dst[:], W[key]))
    ev_wq = P.dma("g", C.wA[:, 0:8192].rearrange("p (c n) -> p c n", c=8),
                  W["wq"].rearrange("(c p) n -> p c n", p=128), waits=C.wA_b.war())
    ev_wk = P.dma("g", C.wA[:, 8192:16384].rearrange("p (c n) -> p c n", c=8),
                  W["wk"].rearrange("(c p) n -> p c n", p=128))
    C.wA_b.wrote(ev_wk)
    ev_wv = P.dma("g", C.wB[:, 0:8192].rearrange("p (c n) -> p c n", c=8),
                  W["wv"].rearrange("(c p) n -> p c n", p=128), waits=C.wB_b.war())
    ev_wb = P.dma("g", C.wB[:, 8192:8224].rearrange("p (c n) -> p c n", c=8),
                  W["wb"].rearrange("(c p) n -> p c n", p=128))
    C.wB_b.wrote(ev_wb)
    ev_w = [ev_wq, ev_wk, ev_wv, ev_wb]
    P.v(lambda v: v.memset(C.gh[:, 0:4352], 0.0), waits=C.acc_b.war())
    z1 = P.v(lambda v: v.memset(C.acc[:, 0:13056], 0.0))
    C.acc_b.wrote(z1)
    prev_evac = [z1]
    prev_lm = []
    for ti in range(NTILES):
        t0 = ti * NT
        lxi = ti % 4
        lxb = C.lx[lxi]
        ev_lx = P.dma("y", fm_sbuf(lxb), fm_dram(lnsrc, t0, NT), waits=C.lx_b[lxi].war())
        C.lx_b[lxi].wrote(ev_lx)
        for (woff, wsrc, buf, boff) in [(0, C.wA, C.gh, 0), (8192, C.wA, C.acc, 8704),
                                        (0, C.wB, C.acc, 0)]:
            for f in range(NCH):
                bk = f % 2
                evp = None
                for c in range(NCH):
                    evp = P.t(lambda t, c=c, f=f, bk=bk, woff=woff, wsrc=wsrc, lxb=lxb: t.matmul(
                        C.pb[bk][:, :],
                        wsrc[:, woff + c * 1024 + f * 128: woff + c * 1024 + (f + 1) * 128],
                        lxb[:, c * NT:(c + 1) * NT],
                        start=(c == 0), stop=(c == NCH - 1)),
                        waits=ev_w + [ev_lx, C.pb_ev[bk]])
                ev_e = P.v(lambda v, f=f, bk=bk, buf=buf, boff=boff: v.tensor_copy(
                    buf[:, boff + f * 544 + HL: boff + f * 544 + HL + NT], C.pb[bk][:, :]),
                    waits=[evp] + prev_evac)
                prev_evac = []
                C.pb_ev[bk] = ev_e
        evb = None
        for c in range(NCH):
            evb = P.t(lambda t, c=c, lxb=lxb: t.matmul(
                C.pb[4][0:4, :], C.wB[:, 8192 + c * 4: 8192 + (c + 1) * 4],
                lxb[:, c * NT:(c + 1) * NT], start=(c == 0), stop=(c == NCH - 1)),
                waits=[ev_wb, ev_lx, C.pb_ev[4]])
        ev_t = P.s(lambda s: s.activation(C.r4a[0:4, 0:NT], C.pb[4][0:4, :], AF.Tanh,
                                          scale=0.5), waits=[evb, getattr(C, "r4a_st", None)])
        C.pb_ev[4] = ev_t
        ev_bt = P.v(lambda v: v.tensor_scalar(C.r4a[0:4, 0:NT], C.r4a[0:4, 0:NT], 0.5, 0.5,
                                              op0=OP.mult, op1=OP.add), waits=[ev_t])
        C.r4a_st = P.dma("y", rows[0:4, t0:t0 + NT], C.r4a[0:4, 0:NT], waits=[ev_bt])
        sq_store = None
        for (boff, src_, wcol, dram, roff) in [
                (0, C.gh, C.l1a_cq, bufs["qb"], 4), (8704, C.acc, C.l1a_ck, bufs["kb"], 8),
                (0, C.acc, C.l1a_cv, bufs["vb"], None)]:
            evcv = None
            for j in range(4):
                sh = HL - 3 + j
                wt = wcol[:, j * 8:(j + 1) * 8].broadcast_to((128, 8, NT))
                if j == 0:
                    evcv = P.v(lambda v, src_=src_, boff=boff, sh=sh, wt=wt: v.tensor_tensor(
                        fm_sbuf(C.tc), _halo(src_, boff)[:, :, sh:sh + NT], wt, OP.mult),
                        waits=ev_c + prev_lm)
                    prev_lm = []
                else:
                    evcv = P.v(lambda v, src_=src_, boff=boff, sh=sh, wt=wt: v.tensor_tensor(
                        fm_sbuf(C.sqt), _halo(src_, boff)[:, :, sh:sh + NT], wt, OP.mult),
                        waits=[sq_store])
                    sq_store = None
                    evcv = P.v(lambda v: v.tensor_tensor(C.tc[:], C.tc[:], C.sqt[:], OP.add))
            if roff is not None:
                ev_si = P.s(lambda s: s.activation(C.sqt[:], C.tc[:], AF.Silu, scale=1.0),
                            waits=[evcv, sq_store])
                sq_store = P.dma("y", fm_dram(dram, t0, NT), fm_sbuf(C.sqt), waits=[ev_si])
                ev_sq = P.v(lambda v: v.tensor_tensor(C.tc[:], C.sqt[:], C.sqt[:], OP.mult),
                            waits=[ev_si])
                ev_ss = None
                for c in range(NCH):
                    ev_ss = P.t(lambda t, c=c: t.matmul(
                        C.pb[5][0:4, :], C.hm4[:, c * 4:(c + 1) * 4],
                        C.tc[:, c * NT:(c + 1) * NT],
                        start=(c == 0), stop=(c == NCH - 1)),
                        waits=[ev_sq, C.pb_ev[5]] + ev_c)
                ev_sse = P.v(lambda v: v.tensor_copy(C.r4b[0:4, 0:NT], C.pb[5][0:4, :]),
                             waits=[ev_ss, getattr(C, "r4b_st", None)])
                C.pb_ev[5] = ev_sse
                C.r4b_st = P.dma("y", rows[roff:roff + 4, t0:t0 + NT], C.r4b[0:4, 0:NT],
                                 waits=[ev_sse])
            else:
                ev_si = P.s(lambda s: s.activation(
                    _halo(C.acc, 4352)[:, :, HL:HL + NT], fm_sbuf(C.tc), AF.Silu,
                    scale=1.0), waits=[evcv, P.last("dv"), getattr(C, "vp_st", None)])
                C.vp_st = P.dma("y", fm_dram(dram, t0, NT),
                                _halo(C.acc, 4352)[:, :, HL:HL + NT], waits=[ev_si])
        ev_vp = P.last("ac")
        for (wcol, K_, dram) in [(C.l1a_lw, 7, bufs["lob"]), (C.l1a_mw, 25, bufs["midb"])]:
            evcv = None
            evg = None
            for j in range(K_):
                sh = HL - (K_ - 1) + j
                wt = wcol[:, j * 8:(j + 1) * 8].broadcast_to((128, 8, NT))
                if j == 0:
                    evcv = P.v(lambda v, sh=sh, wt=wt: v.tensor_tensor(
                        fm_sbuf(C.tc), _halo(C.acc, 4352)[:, :, sh:sh + NT], wt,
                        OP.mult), waits=[ev_vp] + ev_c + prev_lm)
                else:
                    evm = P.v(lambda v, sh=sh, wt=wt: v.tensor_tensor(
                        fm_sbuf(C.sqt), _halo(C.acc, 4352)[:, :, sh:sh + NT], wt,
                        OP.mult), waits=[evg, sq_store])
                    sq_store = None
                    evg = P.g(lambda g: g.tensor_tensor(C.tc[:], C.tc[:], C.sqt[:],
                                                        OP.add), waits=[evm, evcv])
                    evcv = evg
            stlm = P.dma("y", fm_dram(dram, t0, NT), fm_sbuf(C.tc),
                         waits=[evcv, P.last("dv"), P.last("po")])
            prev_lm = [stlm, evcv]
        for (buf, off) in [(C.gh, 0), (C.acc, 8704), (C.acc, 0), (C.acc, 4352)]:
            P.v(lambda v, buf=buf, off=off: v.tensor_copy(
                _halo(buf, off)[:, :, 0:HL], _halo(buf, off)[:, :, NT:NT + HL]))
        prev_evac = [P.last("dv")]
        C.lx_b[lxi].read(P.last("mm"))
    C.wA_b.read(P.last("mm"))
    C.wB_b.read(P.last("mm"))
    C.acc_b.wrote(P.last("dv"))


def emit_l1attn_p2(C, W, bufs):
    P = C.P
    rows = bufs["rows"]
    for ti in range(NTILES):
        t0 = ti * NT
        ev_l = P.dma("y", C.r4a[0:4, 0:NT], rows[4:8, t0:t0 + NT],
                     waits=[P.last("dv"), P.last("ac"), getattr(C, "r4a_st", None)])
        ev1 = P.s(lambda s: s.activation(C.r4a[0:4, 0:NT], C.r4a[0:4, 0:NT], AF.Ln,
                                         bias=eps_tile(C, 1e-12)[0:4, :], scale=1.0),
                  waits=[ev_l])
        ev2 = P.s(lambda s: s.activation(C.r4a[0:4, 0:NT], C.r4a[0:4, 0:NT], AF.Exp,
                                         scale=-0.5), waits=[ev1])
        st1 = P.dma("y", rows[12:16, t0:t0 + NT], C.r4a[0:4, 0:NT], waits=[ev2])
        ev_l2 = P.dma("y", C.r4b[0:4, 0:NT], rows[8:12, t0:t0 + NT],
                      waits=[P.last("ac"), ev2, getattr(C, "r4b_st", None)])
        ev3 = P.s(lambda s: s.activation(C.r4b[0:4, 0:NT], C.r4b[0:4, 0:NT], AF.Ln,
                                         bias=eps_tile(C, 1e-12)[0:4, :], scale=1.0),
                  waits=[ev_l2])
        ev4 = P.s(lambda s: s.activation(C.r4b[0:4, 0:NT], C.r4b[0:4, 0:NT], AF.Exp,
                                         scale=-0.5), waits=[ev3])
        C.r4b_st = P.dma("y", rows[16:20, t0:t0 + NT], C.r4b[0:4, 0:NT], waits=[ev4])
        ev_l3 = P.dma("y", C.r4a[0:4, 0:NT], rows[0:4, t0:t0 + NT],
                      waits=[ev2, P.last("ac"), st1])
        ev5 = P.v(lambda v: v.tensor_tensor(C.r4a[0:4, 0:NT], C.r4a[0:4, 0:NT],
                                            C.r4b[0:4, 0:NT], OP.mult),
                  waits=[ev_l3, ev4])
        C.r4a_st = P.dma("y", rows[20:24, t0:t0 + NT], C.r4a[0:4, 0:NT], waits=[ev5])
    for ti in range(NTILES):
        t0 = ti * NT
        ev_a = P.dma("g", C.r4c[0:4, 0:NT], rows[0:4, t0:t0 + NT],
                     waits=[P.last("dv"), P.last("mm")] + P.queue_events("qy"))
        ev_b = P.dma("g", C.r4c[4:8, 0:NT], rows[12:16, t0:t0 + NT])
        ev_c2 = P.dma("g", C.r4d[0:4, 0:NT], rows[16:20, t0:t0 + NT])
        ev_d = P.dma("g", C.r4d[4:8, 0:NT], rows[20:24, t0:t0 + NT])
        for s4 in range(4):
            cu = ti * 4 + s4
            evt = P.t(lambda t, s4=s4: t.transpose(
                C.pt[0][:, 0:8], C.r4c[0:8, s4 * 128:(s4 + 1) * 128], C.ident[0:8, 0:8]),
                waits=[ev_a, ev_b, C.pt_ev[0]])
            eve = P.v(lambda v, cu=cu: v.tensor_copy(C.scal[:, cu * 16:cu * 16 + 8],
                                                     C.pt[0][:, 0:8]),
                      waits=[evt])
            C.pt_ev[0] = eve
            evt2 = P.t(lambda t, s4=s4: t.transpose(
                C.pt[0][:, 0:8], C.r4d[0:8, s4 * 128:(s4 + 1) * 128], C.ident[0:8, 0:8]),
                waits=[ev_c2, ev_d, eve, C.pt_ev[0]])
            eve2 = P.v(lambda v, cu=cu: v.tensor_copy(C.scal[:, cu * 16 + 8:cu * 16 + 16],
                                                      C.pt[0][:, 0:8]),
                       waits=[evt2])
            C.pt_ev[0] = eve2
    ev_scal = P.last("dv")
    _emit_delta(C, bufs, ev_scal)


def _emit_delta(C, bufs, ev_scal):
    P = C.P
    qb, kb_, vb_ = bufs["qb"], bufs["kb"], bufs["vb"]
    dltb = bufs["dltb"]

    def wAh(h, off, n=128):
        return C.wA[:, h * 4096 + off: h * 4096 + off + n]

    def wBh(h, off, n=128):
        return C.wB[:, h * 4096 + off: h * 4096 + off + n]

    RB1, RC1 = 0, 128
    RB = {2: 384, 4: 512, 8: 640, 16: 768, 32: 896, 64: 256}
    RR = 0

    def Whi(h, s):
        return C.wB[:, h * 4096 + 2 * s: h * 4096 + 2 * s + 128]

    def Wlo(h, s):
        return C.wB[:, h * 4096 + 2 * s + 128: h * 4096 + 2 * s + 256]

    def Wsc(h):
        return C.wB[:, h * 4096 + 2048: h * 4096 + 2304].bitcast(F32)

    def evac_hl(h, s, pbh, waits):
        eh = P.s(lambda s_, h=h, s=s, pbh=pbh: s_.copy(Whi(h, s), pbh[:, 0:128]),
                 waits=waits)
        return P.v(lambda v, h=h, s=s, pbh=pbh: v.tensor_tensor(
            Wlo(h, s), pbh[:, 0:128], Whi(h, s), OP.subtract), waits=[eh])

    def evac_hi(h, s, pbh, waits):
        return P.s(lambda s_, h=h, s=s, pbh=pbh: s_.copy(Whi(h, s), pbh[:, 0:128]),
                   waits=waits)

    for h in range(4):
        P.v(lambda v, h=h: v.memset(C.S[h][:], 0.0))
        P.v(lambda v, h=h: v.memset(wAh(h, 2048, 512), 0.0))
    ev_s0 = P.last("dv")
    ld_guard = [ev_s0] * 4
    C.o_dma = [None] * 4
    scan_prev = [ev_s0] * 4
    pt_ch = [C.pt_ev[0], C.pt_ev[1]]

    for cu in range(64):
        tt0 = cu * 128

        def col(h, r):
            return C.scal[:, cu * 16 + r * 4 + h: cu * 16 + r * 4 + h + 1]

        ev_ld = []
        for h in range(4):
            e1 = P.dma("y", wAh(h, 0, 256).rearrange("p (c n) -> p c n", c=2),
                       qb[256 * h:256 * (h + 1), tt0:tt0 + 128]
                       .rearrange("(c p) n -> p c n", p=128),
                       waits=[ld_guard[h]])
            e2 = P.dma_tr("s", wAh(h, 256, 256),
                          kb_[256 * h:256 * (h + 1), tt0:tt0 + 128],
                          waits=[ld_guard[h]])
            e3 = P.dma_tr("s", wAh(h, 512, 256),
                          vb_[256 * h:256 * (h + 1), tt0:tt0 + 128],
                          waits=[ld_guard[h]])
            ev_ld.append((e1, e2, e3))
        ev_tm = []
        for h in range(4):
            e1, e2, e3 = ev_ld[h]
            a = P.v(lambda v, h=h, sc=col(h, 2): v.tensor_scalar(
                wAh(h, 768, 256), wAh(h, 256, 256), sc, None, op0=OP.mult),
                waits=[e2, ev_scal])
            b = P.v(lambda v, h=h, sc=col(h, 3): v.tensor_scalar(
                wAh(h, 1024, 256), wAh(h, 256, 256), sc, None, op0=OP.mult))
            c = P.v(lambda v, h=h, sc=col(h, 0): v.tensor_scalar(
                wAh(h, 1280, 256), wAh(h, 512, 256), sc, None, op0=OP.mult),
                waits=[e3])
            ev_tm.append((a, b, c))
        ev_fm = []
        for h in range(4):
            ptb = C.pt[h // 2]
            evs = []
            for (src_off, dst_off) in [(768, 1536), (1024, 1792)]:
                for half in range(2):
                    evt = P.t(lambda t, h=h, src_off=src_off, half=half, ptb=ptb: t.transpose(
                        ptb[:, 0:128], wAh(h, src_off + half * 128, 128), C.ident[:]),
                        waits=[ev_tm[h][0], ev_tm[h][1], pt_ch[h // 2]])
                    eve = P.v(lambda v, h=h, dst_off=dst_off, half=half, ptb=ptb: v.tensor_copy(
                        wAh(h, dst_off + half * 128, 128), ptb[:, 0:128]),
                        waits=[evt])
                    pt_ch[h // 2] = eve
                    evs.append(eve)
            ev_fm.append(evs[-1])
        pb4 = C.pb[4]

        def pb43():
            return pb4[:, 0:512].rearrange("p (h c) -> p h c", h=4)

        def W3(s, lo=False):
            base = 2 * s + (128 if lo else 0)
            return C.wB[:].rearrange("p (h c) -> p h c", h=4)[:, :, base:base + 128]

        def wsc3():
            return C.wsc4[:].rearrange("p (h c) -> p h c", h=4)

        def evac3(s, mm_ev):
            eh = P.s(lambda s_, s=s: s_.copy(W3(s), pb43()), waits=[mm_ev])
            el = P.v(lambda v, s=s: v.tensor_tensor(W3(s, True), pb43(), W3(s),
                                                    OP.subtract), waits=[eh])
            C.pb_ev[4] = el
            return el

        mA = None
        for h in range(4):
            for dc in range(2):
                first = (h == 0 and dc == 0)
                mA = P.t(lambda t, h=h, dc=dc, first=first: t.matmul(
                    pb4[:, h * 128:(h + 1) * 128],
                    wAh(h, 1792 + dc * 128, 128), wAh(h, 1536 + dc * 128, 128),
                    start=first, stop=(h == 3 and dc == 1), skip_group_check=True),
                    waits=(ev_fm + [C.pb_ev[4]]) if first else [])
        pm = None
        for h in range(4):
            pm = P.v(lambda v, h=h: v.tensor_tensor(
                C.wsc4[:, h * 128:(h + 1) * 128], pb4[:, h * 128:(h + 1) * 128],
                C.maskL[:], OP.mult), waits=[mA] if h == 0 else [])
        ehB = P.s(lambda s_: s_.copy(W3(RB1), wsc3()), waits=[pm])
        elB = P.v(lambda v: v.tensor_tensor(W3(RB1, True), wsc3(), W3(RB1),
                                            OP.subtract), waits=[ehB])
        mB = None
        for h in range(4):
            for dc in range(2):
                first = (h == 0 and dc == 0)
                mB = P.t(lambda t, h=h, dc=dc, first=first: t.matmul(
                    pb4[:, h * 128:(h + 1) * 128],
                    wAh(h, 1536 + dc * 128, 128), wAh(h, 1792 + dc * 128, 128),
                    start=first, stop=(h == 3 and dc == 1), skip_group_check=True),
                    waits=[pm] if first else [])
        pm = None
        for h in range(4):
            pm = P.v(lambda v, h=h: v.tensor_tensor(
                C.wsc4[:, h * 128:(h + 1) * 128], pb4[:, h * 128:(h + 1) * 128],
                C.maskU[:], OP.mult), waits=[mB] if h == 0 else [])
        ehC = P.s(lambda s_: s_.copy(W3(RC1), wsc3()), waits=[pm])
        elC = P.v(lambda v: v.tensor_tensor(W3(RC1, True), wsc3(), W3(RC1),
                                            OP.subtract), waits=[ehC])
        C.pb_ev[4] = pm
        S0, S1, S2, S3, S4, S5, S6, S7 = 0, 128, 256, 384, 512, 640, 768, 896
        sq_seq = [(S0, S1, S2),
                  (S1, S0, S3),
                  (S3, S2, S0),
                  (S2, S3, S4),
                  (S4, S0, S2),
                  (S0, S4, S5),
                  (S5, S2, S0),
                  (S2, S5, S6),
                  (S6, S0, S2),
                  (S0, S6, S7),
                  (S2, S7, S2)]
        prev = elC
        for (lo_, ro_, oo_) in sq_seq:
            mm = None
            for h in range(4):
                reg = pb4[:, h * 128:(h + 1) * 128]
                first = (h == 0)
                mm = P.t(lambda t, h=h, lo_=lo_, ro_=ro_, reg=reg, first=first: t.matmul(
                    reg, Whi(h, lo_), Whi(h, ro_),
                    start=first, stop=False, skip_group_check=True),
                    waits=[prev, C.pb_ev[4]] if first else [])
                mm = P.t(lambda t, h=h, lo_=lo_, ro_=ro_, reg=reg: t.matmul(
                    reg, Wlo(h, lo_), Whi(h, ro_),
                    start=False, stop=False, skip_group_check=True))
                mm = P.t(lambda t, h=h, lo_=lo_, ro_=ro_, reg=reg: t.matmul(
                    reg, Whi(h, lo_), Wlo(h, ro_),
                    start=False, stop=(h == 3), skip_group_check=True))
            prev = evac3(oo_, mm)
        mm = None
        for h in range(4):
            reg = pb4[:, h * 128:(h + 1) * 128]
            first = (h == 0)
            mm = P.t(lambda t, h=h, reg=reg, first=first: t.matmul(
                reg, C.ident[:], C.ident[:], start=first, stop=False,
                skip_group_check=True),
                waits=[prev, C.pb_ev[4]] if first else [])
            mm = P.t(lambda t, h=h, reg=reg: t.matmul(
                reg, C.ident[:], Whi(h, RC1), start=False, stop=False,
                skip_group_check=True))
            mm = P.t(lambda t, h=h, reg=reg: t.matmul(
                reg, C.ident[:], Wlo(h, RC1), start=False, stop=False,
                skip_group_check=True))
            mm = P.t(lambda t, h=h, reg=reg: t.matmul(
                reg, Whi(h, RB[2]), C.ident[:], start=False, stop=False,
                skip_group_check=True))
            mm = P.t(lambda t, h=h, reg=reg: t.matmul(
                reg, Wlo(h, RB[2]), C.ident[:], start=False, stop=False,
                skip_group_check=True))
            mm = P.t(lambda t, h=h, reg=reg: t.matmul(
                reg, Whi(h, RB[2]), Whi(h, RC1), start=False, stop=False,
                skip_group_check=True))
            mm = P.t(lambda t, h=h, reg=reg: t.matmul(
                reg, Wlo(h, RB[2]), Whi(h, RC1), start=False, stop=False,
                skip_group_check=True))
            mm = P.t(lambda t, h=h, reg=reg: t.matmul(
                reg, Whi(h, RB[2]), Wlo(h, RC1), start=False, stop=(h == 3),
                skip_group_check=True))
        prev = evac3(RR, mm)
        for lvl in (4, 8, 16, 32, 64):
            mm = None
            for h in range(4):
                reg = pb4[:, h * 128:(h + 1) * 128]
                first = (h == 0)
                mm = P.t(lambda t, h=h, reg=reg, first=first: t.matmul(
                    reg, C.ident[:], Whi(h, RR), start=first, stop=False,
                    skip_group_check=True),
                    waits=[prev, C.pb_ev[4]] if first else [])
                mm = P.t(lambda t, h=h, reg=reg: t.matmul(
                    reg, C.ident[:], Wlo(h, RR), start=False, stop=False,
                    skip_group_check=True))
                mm = P.t(lambda t, h=h, lvl=lvl, reg=reg: t.matmul(
                    reg, Whi(h, RB[lvl]), Whi(h, RR), start=False, stop=False,
                    skip_group_check=True))
                mm = P.t(lambda t, h=h, lvl=lvl, reg=reg: t.matmul(
                    reg, Wlo(h, RB[lvl]), Whi(h, RR), start=False, stop=False,
                    skip_group_check=True))
                mm = P.t(lambda t, h=h, lvl=lvl, reg=reg: t.matmul(
                    reg, Whi(h, RB[lvl]), Wlo(h, RR), start=False, stop=(h == 3),
                    skip_group_check=True))
            prev = evac3(RR, mm)
        prev_ev = [prev] * 4
        ev_cl = []
        for h in range(4):
            pbh = C.pb[h]
            mu = P.t(lambda t, h=h, pbh=pbh: t.matmul(
                pbh[:, 128:384], Whi(h, RR), wAh(h, 1280, 256),
                start=True, stop=True),
                waits=[prev_ev[h], ev_tm[h][2], C.pb_ev[h]])
            eu = P.v(lambda v, h=h, pbh=pbh: v.tensor_copy(wBh(h, 3200, 256),
                                                           pbh[:, 128:384]),
                     waits=[mu])
            mw = None
            for half in range(2):
                mw = P.t(lambda t, h=h, half=half, pbh=pbh: t.matmul(
                    pbh[:, 0:128], wAh(h, 1024 + half * 128, 128), Whi(h, RR),
                    start=True, stop=True),
                    waits=[eu, C.pb_ev[h]])
                ew = P.v(lambda v, h=h, half=half, pbh=pbh: v.tensor_copy(
                    wBh(h, 3456 + half * 128, 128), pbh[:, 0:128]), waits=[mw])
                C.pb_ev[h] = ew
            ma = None
            for dc in range(2):
                ma = P.t(lambda t, h=h, dc=dc, pbh=pbh: t.matmul(
                    pbh[:, 0:128], wAh(h, 1536 + dc * 128, 128), wAh(h, dc * 128, 128),
                    start=(dc == 0), stop=(dc == 1)),
                    waits=[C.pb_ev[h], ev_ld[h][0]])
            ea = P.v(lambda v, h=h, pbh=pbh: v.tensor_tensor(wBh(h, 3072), pbh[:, 0:128],
                                                             C.maskT[:], OP.mult),
                     waits=[ma])
            C.pb_ev[h] = ea
            ev_cl.append(ea)
        for h in range(4):
            pbh = C.pb[h]
            mp = None
            for dc in range(2):
                mp = P.t(lambda t, h=h, dc=dc, pbh=pbh: t.matmul(
                    pbh[:, 128:384], wBh(h, 3456 + dc * 128, 128),
                    wAh(h, 2048 + dc * 256, 256),
                    start=(dc == 0), stop=(dc == 1)),
                    waits=[ev_cl[h], scan_prev[h], C.pb_ev[h]])
            ui = P.v(lambda v, h=h, pbh=pbh: v.tensor_tensor(
                wBh(h, 3712, 256), wBh(h, 3200, 256), pbh[:, 128:384], OP.subtract),
                waits=[mp])
            mo = None
            for dc in range(2):
                mo = P.t(lambda t, h=h, dc=dc, pbh=pbh: t.matmul(
                    pbh[:, 128:384], wAh(h, dc * 128, 128), wAh(h, 2048 + dc * 256, 256),
                    start=(dc == 0), stop=False),
                    waits=[ui, C.pb_ev[h]])
            mo = P.t(lambda t, h=h, pbh=pbh: t.matmul(
                pbh[:, 128:384], wBh(h, 3072), wBh(h, 3712, 256),
                start=False, stop=True))
            eo = P.v(lambda v, h=h, pbh=pbh, sc=col(h, 1): v.tensor_scalar(
                C.sqt[:, h * 256:(h + 1) * 256], pbh[:, 128:384], sc, None,
                op0=OP.mult), waits=[mo, C.o_dma[h]])
            C.pb_ev[h] = eo
            C.o_dma[h] = P.dma("y", dltb[tt0:tt0 + 128, 256 * h:256 * (h + 1)],
                               C.sqt[:, h * 256:(h + 1) * 256], waits=[eo])
            sev = None
            for dc in range(2):
                ms = P.t(lambda t, h=h, dc=dc, pbh=pbh: t.matmul(
                    pbh[:, 128:384], wAh(h, 768 + dc * 128, 128), wBh(h, 3712, 256),
                    start=True, stop=True),
                    waits=[eo if dc == 0 else sev, C.pb_ev[h]])
                es = P.v(lambda v, h=h, dc=dc, pbh=pbh: v.tensor_tensor(
                    C.S[h][:, dc * 256:(dc + 1) * 256], C.S[h][:, dc * 256:(dc + 1) * 256],
                    pbh[:, 128:384], OP.add), waits=[ms])
                sev = P.v(lambda v, h=h, dc=dc: v.tensor_copy(
                    wAh(h, 2048 + dc * 256, 256), C.S[h][:, dc * 256:(dc + 1) * 256]))
                C.pb_ev[h] = sev
            scan_prev[h] = sev
            ld_guard[h] = sev
    C.pt_ev[0] = pt_ch[0]
    C.pt_ev[1] = pt_ch[1]
    C.wA_b.read(P.last("mm"))
    C.wA_b.read(P.last("dv"))
    C.wB_b.read(P.last("mm"))
    C.wB_b.read(P.last("dv"))


def emit_l1attn_p3(C, W, xsrc, lnsrc, xdst, lndst, bufs):
    P = C.P
    dltb, h1b = bufs["dltb"], bufs["h1b"]
    ev_w1 = P.dma("g", C.wA[:, 0:7 * 2160].rearrange("p (c n) -> p c n", c=7),
                  W["rw1"][0:896].rearrange("(c p) n -> p c n", p=128),
                  waits=C.wA_b.war())
    C.wA_b.wrote(ev_w1)
    ev_w1b = P.dma("g", C.wB[:, 0:2160], W["rw1"][896:1024], waits=C.wB_b.war())
    C.wB_b.wrote(ev_w1b)
    ev_w1x = P.dma("g", C.rw1x[:], W["rw1"][1024:1080])
    ev_rb1 = P.dma("y", C.rb1h[:], W["rb1"])
    gh_slots = [None] * 2
    for ti in range(NTILES):
        t0 = ti * NT
        lxi = ti % 4
        lxb = C.lx[lxi]
        ev_lx = P.dma("y", fm_sbuf(lxb), fm_dram(lnsrc, t0, NT), waits=C.lx_b[lxi].war())
        C.lx_b[lxi].wrote(ev_lx)
        ev_f56 = []
        for s4 in range(4):
            cu = ti * 4 + s4
            gsl = (s4 % 2) * 4096
            tt0 = cu * 128
            lds = [P.dma("y", C.gh[:, gsl + 2048:gsl + 3072],
                         dltb[tt0:tt0 + 128, :], waits=[gh_slots[s4 % 2]])]
            for (i, buf) in [(0, bufs["lob"]), (1, bufs["midb"]), (3, bufs["vb"])]:
                lds.append(P.dma_tr("s", C.gh[:, gsl + i * 1024:gsl + (i + 1) * 1024],
                                    buf[:, tt0:tt0 + 128], waits=[gh_slots[s4 % 2]]))
            evf = None
            for (q, i) in [(0, 0), (2, 1), (4, 2), (6, 3)]:
                o3 = C.gh[:, gsl + i * 1024:gsl + (i + 1) * 1024].rearrange(
                    "p (h d) -> p h d", d=256)
                evf = P.v(lambda v, q=q, o3=o3: v.tensor_reduce(
                    C.p16[:, q * 2:q * 2 + 4] if False else C.f32sc[:, 0:4],
                    o3, axis=AX.X, op=OP.add), waits=lds + [evf])
                evf = P.v(lambda v, q=q: v.tensor_scalar(
                    C.f56[:, q * 4:(q + 1) * 4], C.f32sc[:, 0:4], 1.0 / 256, None,
                    op0=OP.mult))
                evf = P.v(lambda v, o3=o3: v.tensor_tensor(
                    C.tc[:, 0:1024], o3.rearrange("p h d -> p (h d)"),
                    o3.rearrange("p h d -> p (h d)"), OP.mult))
                evf = P.v(lambda v: v.tensor_reduce(
                    C.f32sc[:, 4:8], C.tc[:, 0:1024].rearrange("p (h d) -> p h d", d=256),
                    axis=AX.X, op=OP.add))
                evf = P.v(lambda v, q=q: v.tensor_tensor(
                    C.f32sc[:, 8:12], C.f56[:, q * 4:(q + 1) * 4],
                    C.f56[:, q * 4:(q + 1) * 4], OP.mult))
                evf = P.v(lambda v, q=q: v.scalar_tensor_tensor(
                    C.f32sc[:, 8:12], C.f32sc[:, 8:12], -256.0 / 255.0,
                    C.f32sc[:, 4:8], op0=OP.mult, op1=OP.bypass))
                evf = P.v(lambda v, q=q: v.scalar_tensor_tensor(
                    C.f56[:, (q + 1) * 4:(q + 2) * 4], C.f32sc[:, 4:8], 1.0 / 255.0,
                    C.f32sc[:, 8:12], op0=OP.mult, op1=OP.add))
            pairq = 8
            for (i, j) in [(0, 1), (0, 2), (0, 3), (1, 2), (1, 3), (2, 3)]:
                oi = C.gh[:, gsl + i * 1024:gsl + (i + 1) * 1024]
                oj = C.gh[:, gsl + j * 1024:gsl + (j + 1) * 1024]
                evf = P.v(lambda v, oi=oi, oj=oj: v.tensor_tensor(
                    C.tc[:, 0:1024], oi, oj, OP.mult))
                evf = P.v(lambda v: v.tensor_reduce(
                    C.f32sc[:, 0:4], C.tc[:, 0:1024].rearrange("p (h d) -> p h d", d=256),
                    axis=AX.X, op=OP.add))
                evf = P.v(lambda v, pairq=pairq: v.tensor_scalar(
                    C.f56[:, pairq * 4:(pairq + 1) * 4], C.f32sc[:, 0:4], 1.0 / 256,
                    None, op0=OP.mult))
                pairq += 1
            gh_slots[s4 % 2] = evf
            evt = P.t(lambda t: t.transpose(C.pt[0][0:56, 0:128], C.f56[:, 0:56],
                                            C.ident[:]),
                      waits=[evf, C.pt_ev[0]])
            eve = P.v(lambda v, s4=s4: v.tensor_copy(
                C.f56T[0:56, s4 * 128:(s4 + 1) * 128], C.pt[0][0:56, 0:128]),
                waits=[evt])
            C.pt_ev[0] = eve
            ev_f56.append(eve)
        if not hasattr(C, "h1_dma"):
            C.h1_dma = [None, None]
        for j in range(17):
            bk = j % 2
            M = 128 if j < 16 else 112
            evh = None
            for c in range(NCH):
                loc = (C.wA, c * 2160 + j * 128) if c < 7 else (C.wB, (c - 7) * 0 + j * 128)
                evh = P.t(lambda t, c=c, j=j, bk=bk, loc=loc, lxb=lxb, M=M: t.matmul(
                    C.pb[bk][0:M, :], loc[0][:, loc[1]:loc[1] + M],
                    lxb[:, c * NT:(c + 1) * NT],
                    start=(c == 0), stop=False),
                    waits=[ev_w1, ev_w1b, ev_lx, C.pb_ev[bk]])
            evh = P.t(lambda t, j=j, bk=bk, M=M: t.matmul(
                C.pb[bk][0:M, :], C.rw1x[0:56, j * 128:j * 128 + M],
                C.f56T[0:56, :], start=False, stop=True),
                waits=ev_f56 + [ev_w1x])
            eg = P.s(lambda s, j=j, bk=bk, M=M: s.activation(
                C.sqt[0:M, (j % 2) * NT:(j % 2 + 1) * NT],
                C.pb[bk][0:M, :], AF.Silu, bias=C.rb1h[0:M, j:j + 1], scale=1.0),
                waits=[evh, ev_rb1, C.h1_dma[j % 2]])
            C.pb_ev[bk] = eg
            ed = P.dma("y", h1b[j * 128:j * 128 + M, t0:t0 + NT],
                       C.sqt[0:M, (j % 2) * NT:(j % 2 + 1) * NT], waits=[eg])
            C.h1_dma[j % 2] = ed
        C.lx_b[lxi].read(P.last("mm"))
    C.wA_b.read(P.last("mm"))
    C.wB_b.read(P.last("mm"))

    ev_wo = P.dma("g", C.wB[:, 0:8192].rearrange("p (c n) -> p c n", c=8),
                  W["wo"].rearrange("(c p) n -> p c n", p=128), waits=C.wB_b.war())
    C.wB_b.wrote(ev_wo)
    ev_w2 = P.dma("g", C.rw2s[:, 0:256].rearrange("p (c n) -> p c n", c=16),
                  W["rw2"][0:2048].rearrange("(c p) n -> p c n", p=128),
                  waits=C.wA_b.war())
    ev_w2b = P.dma("g", C.rw2s[0:112, 256:272], W["rw2"][2048:2160])
    C.wA_b.wrote(ev_w2b)
    ev_rb2l = P.dma("y", C.murow[0:1, 0:16], W["rb2"])
    ev_rb2 = emit_bcast_row(C, C.murow[0:1, 0:16], C.pb[4][:, 0:16],
                            [ev_rb2l, C.pb_ev[4]])
    ev_rb2e = P.v(lambda v: v.tensor_copy(C.rb2b[:], C.pb[4][:, 0:16]), waits=[ev_rb2])
    C.pb_ev[4] = ev_rb2e
    ev_mx = P.dma("g", C.r4c[0:4, 0:256], W["mixw"],
                  waits=[C.h1_dma[0], C.h1_dma[1]] + C.wA_b.war())
    ev_on = P.dma("y", C.murow[0:1, 0:256], W["onormw"], waits=[ev_rb2e])
    ev_onb = emit_bcast_row(C, C.murow[0:1, 0:256], C.pb[4][:, 0:256],
                            [ev_on, C.pb_ev[4]])
    ev_one = P.s(lambda s: s.copy(C.wbc[:, 0:256], C.pb[4][:, 0:256]), waits=[ev_onb])
    C.pb_ev[4] = ev_one
    for h in range(4):
        mm = P.t(lambda t, h=h: t.matmul(C.pb[4][:, 0:256], C.oh4T[0:4, h * 128:(h + 1) * 128],
                                         C.r4c[0:4, 0:256], start=True, stop=True),
                 waits=[ev_mx, C.pb_ev[4]])
        eb = P.v(lambda v, h=h: v.tensor_copy(C.mwb[:, h * 256:(h + 1) * 256],
                                              C.pb[4][:, 0:256]), waits=[mm])
        ee = P.v(lambda v, h=h: v.tensor_tensor(C.mwo[:, h * 256:(h + 1) * 256],
                                                C.pb[4][:, 0:256], C.wbc[:, 0:256],
                                                OP.mult), waits=[ev_one])
        C.pb_ev[4] = ee
    ev_mwo = P.last("dv")
    gh_slots = [None] * 2
    for ti in range(NTILES):
        t0 = ti * NT
        lxi = ti % 4
        ev_h1 = P.dma("y", C.acc[:, 0:8192].rearrange("p (c n) -> p c n", c=16),
                      h1b[0:2048, t0:t0 + NT].rearrange("(c p) n -> p c n", p=128),
                      waits=C.acc_b.war())
        C.acc_b.wrote(ev_h1)
        ev_h1x = P.dma("y", C.h1x[0:112, 0:NT], h1b[2048:2160, t0:t0 + NT])
        ev_x = P.dma("y", fm_sbuf(C.xt), fm_dram(xsrc, t0, NT), waits=C.xt_b.war())
        C.xt_b.wrote(ev_x)
        for s4 in range(4):
            cu = ti * 4 + s4
            tt0 = cu * 128
            gsl = (s4 % 2) * 4096
            evl = None
            for j in range(17):
                if j < 16:
                    lhsT = C.acc[:, j * NT + s4 * 128: j * NT + s4 * 128 + 128]
                else:
                    lhsT = C.h1x[0:112, s4 * 128:s4 * 128 + 128]
                rr = C.rw2s[0:(128 if j < 16 else 112), j * 16:(j + 1) * 16]
                evl = P.t(lambda t, j=j, lhsT=lhsT, rr=rr: t.matmul(
                    C.pb[4][:, 0:16], lhsT, rr,
                    start=(j == 0), stop=(j == 16)),
                    waits=[ev_h1, ev_h1x, ev_w2, ev_w2b, C.pb_ev[4]])
            e1 = P.v(lambda v: v.tensor_tensor(C.p16[:], C.pb[4][:, 0:16], C.rb2b[:],
                                               OP.add),
                     waits=[evl, ev_rb2e, getattr(C, "p16_st", None)])
            C.pb_ev[4] = e1
            P.v(lambda v: v.tensor_reduce(C.f32sc[:, 0:4],
                                          C.p16[:].rearrange("p (h d) -> p h d", d=4),
                                          axis=AX.X, op=OP.max, negate=True))
            P.v(lambda v: v.tensor_tensor(C.p16[:].rearrange("p (h d) -> p h d", d=4),
                                          C.p16[:].rearrange("p (h d) -> p h d", d=4),
                                          C.f32sc[:, 0:4].broadcast_to((128, 4, 4)),
                                          OP.add))
            ex = P.s(lambda s: s.activation(C.p16[:], C.p16[:], AF.Exp, scale=1.0),
                     waits=[P.last("dv")])
            P.v(lambda v: v.tensor_reduce(C.f32sc[:, 0:4],
                                          C.p16[:].rearrange("p (h d) -> p h d", d=4),
                                          axis=AX.X, op=OP.add), waits=[ex])
            P.v(lambda v: v.reciprocal(C.f32sc[:, 4:8], C.f32sc[:, 0:4]))
            P.v(lambda v: v.tensor_tensor(C.p16[:].rearrange("p (h d) -> p h d", d=4),
                                          C.p16[:].rearrange("p (h d) -> p h d", d=4),
                                          C.f32sc[:, 4:8].broadcast_to((128, 4, 4)),
                                          OP.mult))
            ep = P.v(lambda v: v.tensor_scalar(C.p16[:], C.p16[:], 0.96, 0.01,
                                               op0=OP.mult, op1=OP.add))
            if "dbg3" in bufs:
                C.p16_st = P.dma("y", bufs["dbg3"][tt0:tt0 + 128, :], C.p16[:],
                                 waits=[ep])
            lds = [P.dma("y", C.gh[:, gsl + 2048:gsl + 3072],
                         dltb[tt0:tt0 + 128, :], waits=[gh_slots[s4 % 2]])]
            for (i, buf) in [(0, bufs["lob"]), (1, bufs["midb"]), (3, bufs["vb"])]:
                lds.append(P.dma_tr("s", C.gh[:, gsl + i * 1024:gsl + (i + 1) * 1024],
                                    buf[:, tt0:tt0 + 128], waits=[gh_slots[s4 % 2]]))
            mx0 = 8192 + (s4 % 2) * 1024
            evm = None
            for h in range(4):
                for i in range(4):
                    o_ = C.gh[:, gsl + i * 1024 + h * 256: gsl + i * 1024 + (h + 1) * 256]
                    md = C.acc[:, mx0 + h * 256: mx0 + (h + 1) * 256]
                    if i == 0:
                        evm = P.v(lambda v, h=h, o_=o_, md=md: v.tensor_scalar(
                            md, o_, C.p16[:, h * 4:h * 4 + 1], None, op0=OP.mult),
                            waits=lds + [ep])
                    else:
                        evm = P.v(lambda v, h=h, o_=o_, i=i, md=md: v.scalar_tensor_tensor(
                            md, o_, C.p16[:, h * 4 + i:h * 4 + i + 1],
                            md, op0=OP.mult, op1=OP.add))
            gh_slots[s4 % 2] = evm
            for h in range(4):
                md = C.acc[:, mx0 + h * 256: mx0 + (h + 1) * 256]
                P.v(lambda v, h=h, gsl=gsl, md=md: v.tensor_tensor(
                    C.gh[:, gsl + 0:gsl + 256], md, md, OP.mult))
                P.v(lambda v, h=h, gsl=gsl: v.tensor_reduce(
                    C.f32sc[:, h:h + 1], C.gh[:, gsl + 0:gsl + 256], axis=AX.X,
                    op=OP.add))
                P.v(lambda v, h=h, gsl=gsl, md=md: v.tensor_tensor(
                    C.gh[:, gsl + 0:gsl + 256], md,
                    C.mwb[:, h * 256:(h + 1) * 256], OP.mult), waits=[ev_mwo])
                P.v(lambda v, h=h, gsl=gsl: v.tensor_tensor(
                    C.gh[:, gsl + 256:gsl + 512], C.gh[:, gsl + 0:gsl + 256],
                    C.gh[:, gsl + 0:gsl + 256], OP.mult))
                P.v(lambda v, h=h, gsl=gsl: v.tensor_reduce(
                    C.f32sc[:, 4 + h:5 + h], C.gh[:, gsl + 256:gsl + 512], axis=AX.X,
                    op=OP.add))
            ec = P.v(lambda v: v.scalar_tensor_tensor(
                C.f32sc[:, 8:12], C.f32sc[:, 0:4], 1e-5, C.f32sc[:, 4:8],
                op0=OP.mult, op1=OP.add))
            el = P.s(lambda s: s.activation(C.f32sc[:, 8:12], C.f32sc[:, 8:12], AF.Ln,
                                            bias=eps_tile(C, 1e-10)[:], scale=1.0 / 256),
                     waits=[ec])
            er = P.s(lambda s: s.activation(C.f32sc[:, 8:12], C.f32sc[:, 8:12], AF.Exp,
                                            scale=-0.5), waits=[el])
            evo = None
            for h in range(4):
                md = C.acc[:, mx0 + h * 256: mx0 + (h + 1) * 256]
                P.v(lambda v, h=h, md=md: v.tensor_tensor(
                    C.tc[:, h * 256:(h + 1) * 256], md,
                    C.mwo[:, h * 256:(h + 1) * 256], OP.mult), waits=[er])
                evo = P.v(lambda v, h=h: v.tensor_scalar(
                    C.tc[:, h * 256:(h + 1) * 256], C.tc[:, h * 256:(h + 1) * 256],
                    C.f32sc[:, 8 + h:9 + h], None, op0=OP.mult))
            if "dbg4" in bufs:
                C.on_st = P.dma("y", bufs["dbg4"][tt0:tt0 + 128, :], C.tc[:, 0:1024],
                                waits=[evo, getattr(C, "on_st", None)])
            for c8 in range(8):
                evt = P.t(lambda t, c8=c8: t.transpose(
                    C.pt[0][:, 0:128], C.tc[:, c8 * 128:(c8 + 1) * 128], C.ident[:]),
                    waits=[evo, C.pt_ev[0], getattr(C, "on_st", None)])
                eve = P.v(lambda v, c8=c8, s4=s4: v.tensor_copy(
                    C.xbf[:, c8 * NT + s4 * 128: c8 * NT + s4 * 128 + 128],
                    C.pt[0][:, 0:128]), waits=[evt])
                C.pt_ev[0] = eve
        ev_on_t = P.last("dv")
        ev_new = None
        for f in range(NCH):
            bk = f % 2
            evz = None
            for c in range(NCH):
                evz = P.t(lambda t, c=c, f=f, bk=bk: t.matmul(
                    C.pb[bk][:, :], C.wB[:, c * 1024 + f * 128: c * 1024 + (f + 1) * 128],
                    C.xbf[:, c * NT:(c + 1) * NT],
                    start=(c == 0), stop=(c == NCH - 1)),
                    waits=[ev_wo, ev_on_t, C.pb_ev[bk]])
            ev_new = P.v(lambda v, f=f, bk=bk: v.tensor_tensor(
                C.xt[:, f * NT:(f + 1) * NT], C.xt[:, f * NT:(f + 1) * NT],
                C.pb[bk][:, :], OP.add), waits=[evz, ev_x])
            C.pb_ev[bk] = ev_new
        C.xt_b.wrote(P.last("dv"))
        emit_tail_tile(C, ti, P.last("dv"), xdst, lndst, store_x=True)
    C.wB_b.read(P.last("mm"))


def emit_l1attn(C, W, xsrc, lnsrc, xdst, lndst, bufs):
    emit_l1attn_p1(C, W, lnsrc, bufs)
    C.P.barrier()
    emit_l1attn_p2(C, W, bufs)
    C.P.barrier()
    emit_l1attn_p3(C, W, xsrc, lnsrc, xdst, lndst, bufs)
    C.P.barrier()



L1MACRO = 1024
L1NI = L1MACRO // NT


def emit_l1ffn(C, W, xsrc, lnsrc, xdst, wTdram, csdram):
    P = C.P
    emit_gates(C, W["gate_w"], W["gate_b"], lnsrc, wTdram, renorm=True, with_cs=csdram)
    if not hasattr(C, "l1f_b1"):
        C.l1f_b1 = C.sb("l1f_b1", [128, 8 * 32], F32)
        C.l1f_boh = C.sb("l1f_boh", [128, 8], F32)
    b1s = C.l1f_b1
    ev_b1 = P.dma("y", b1s[:].rearrange("p (e h) -> p e h", e=8),
                  W["b1q"].rearrange("e p h -> p e h"))
    ev_bo = P.dma("y", C.l1f_boh[:], W["bo_fm"])
    ev_boh = P.v(lambda v: v.tensor_scalar(C.l1f_boh[:], C.l1f_boh[:], 0.5, None,
                                           op0=OP.mult), waits=[ev_bo])
    ev_wo = P.dma("g", C.wB[:].rearrange("p (c n) -> p c n", c=16),
                  W["woT"].rearrange("(c p) n -> p c n", p=128), waits=C.wB_b.war())
    C.wB_b.wrote(ev_wo)

    nmac = L // L1MACRO
    wq_ev = [None, None]
    wq_rd = [None, None]
    qidx = 0
    for mi in range(nmac):
        m0 = mi * L1MACRO
        ev_wt = P.dma("y", C.wTm[0:8, 0:L1MACRO], wTdram[:, m0:m0 + L1MACRO],
                      waits=C.wTm_b.war() + [C.wTst_b.wr])
        C.wTm_b.wrote(ev_wt)
        for ii in range(L1NI):
            lxb = C.lx[ii]
            ev_lx = P.dma("y", fm_sbuf(lxb), fm_dram(lnsrc, m0 + ii * NT, NT),
                          waits=C.lx_b[ii].war())
            C.lx_b[ii].wrote(ev_lx)
        for e in range(8):
            for ii in range(L1NI):
                evgb = P.t(lambda t, e=e, ii=ii: t.matmul(
                    C.pb[2][:, :], C.oh8T[0:8, e * 128:(e + 1) * 128],
                    C.wTm[0:8, ii * NT:(ii + 1) * NT], start=True, stop=True),
                    waits=[ev_wt, C.pb_ev[2]])
                evgbc = P.s(lambda s, ii=ii: s.copy(C.wbc[:, 0:NT] if ii == 0 else C.kvs[:, 0:NT],
                                                    C.pb[2][:, :]),
                            waits=[evgb] + (C.wbc_b.war() if ii == 0 else C.kvs_b.war()))
                (C.wbc_b if ii == 0 else C.kvs_b).wrote(evgbc)
                C.wTm_b.read(evgb)
                C.pb_ev[2] = evgbc
            ev_comb = P.last("ac")
            for q in range(4):
                buf = qidx % 2
                woff = buf * 8192
                prev_glu = P.last("dv")
                ev_w = P.dma("g", C.wA[:, woff:woff + 8192].rearrange("p (c n) -> p c n", c=8),
                             W["w1q"][e, :, q * 1024:(q + 1) * 1024]
                             .rearrange("(c p) n -> p c n", p=128),
                             waits=[wq_rd[buf], wq_ev[buf]] + (C.wA_b.war() if qidx < 2 else []))
                wq_ev[buf] = ev_w
                for ii in range(L1NI):
                    lxb = C.lx[ii]
                    combt = C.wbc[:, 0:NT] if ii == 0 else C.kvs[:, 0:NT]
                    for j in range(8):
                        bk = j % 2
                        evh = None
                        for c in range(NCH):
                            evh = P.t(lambda t, c=c, j=j, bk=bk, woff=woff, lxb=lxb: t.matmul(
                                C.pb[bk][:, :],
                                C.wA[:, woff + c * 1024 + j * 128: woff + c * 1024 + (j + 1) * 128],
                                lxb[:, c * NT:(c + 1) * NT],
                                start=(c == 0), stop=(c == NCH - 1)),
                                waits=[ev_w, C.lx_b[ii].wr, C.pb_ev[bk]])
                        gslot = (ii * 8 + j) % 16
                        fn = AF.Identity if j < 4 else AF.Sigmoid
                        ee = P.s(lambda s, j=j, bk=bk, e=e, q=q, gslot=gslot, fn=fn: s.activation(
                            C.gh[:, gslot * NT:(gslot + 1) * NT], C.pb[bk][:, :],
                            fn, bias=b1s[:, e * 32 + q * 8 + j: e * 32 + q * 8 + j + 1],
                            scale=1.0), waits=[evh, ev_b1, prev_glu])
                        C.pb_ev[bk] = ee
                        if ii == L1NI - 1 and j == 7:
                            wq_rd[buf] = evh
                    ev_gl = P.last("ac")
                    for jj in range(4):
                        a_sl = C.gh[:, ((ii * 8 + jj) % 16) * NT: ((ii * 8 + jj) % 16 + 1) * NT]
                        s_sl = C.gh[:, ((ii * 8 + jj + 4) % 16) * NT: ((ii * 8 + jj + 4) % 16 + 1) * NT]
                        hc = 4 * q + jj
                        gsl = C.acc[:, hc * L1MACRO + ii * NT: hc * L1MACRO + (ii + 1) * NT]
                        P.v(lambda v, a_sl=a_sl, s_sl=s_sl: v.tensor_tensor(
                            C.scr[:, 0:NT], a_sl, s_sl, OP.mult),
                            waits=[ev_gl, ev_comb])
                        if e == 0:
                            P.v(lambda v, gsl=gsl, combt=combt: v.tensor_tensor(
                                gsl, C.scr[:, 0:NT], combt, OP.mult),
                                waits=C.acc_b.war() if (q == 0 and jj == 0 and ii == 0 and mi >= 0) else [])
                        else:
                            P.v(lambda v, combt=combt: v.tensor_tensor(
                                C.scr[:, 512:1024], C.scr[:, 0:NT], combt, OP.mult))
                            P.v(lambda v, gsl=gsl: v.tensor_tensor(
                                gsl, gsl, C.scr[:, 512:1024], OP.add))
                qidx += 1
            C.wbc_b.read(P.last("dv"))
            C.kvs_b.read(P.last("dv"))
        for ii in range(L1NI):
            C.lx_b[ii].read(P.last("mm"))
        C.acc_b.wrote(P.last("dv"))
        g_done = P.last("dv")
        for ii in range(L1NI):
            i0 = m0 + ii * NT
            ti = i0 // NT
            ev_cs = P.dma("y", C.csst[:], csdram[0:1, i0:i0 + NT],
                          waits=[C.csrow_b.wr])
            ev_csb = emit_bcast_row(C, C.csst[0:1, :], C.pb[4][:, :],
                                    [ev_cs, C.pb_ev[4]])
            ev_cse = P.s(lambda s: s.copy(C.wbc[:, 0:NT], C.pb[4][:, :]),
                         waits=[ev_csb] + C.wbc_b.war())
            C.wbc_b.wrote(ev_cse)
            C.pb_ev[4] = ev_cse
            ev_x = P.dma("y", fm_sbuf(C.xt), fm_dram(xsrc, i0, NT), waits=C.xt_b.war())
            C.xt_b.wrote(ev_x)
            ev_new = None
            for f in range(NCH):
                bk = 2 + f % 2
                evo = None
                for hc in range(16):
                    evo = P.t(lambda t, hc=hc, f=f, bk=bk, ii=ii: t.matmul(
                        C.pb[bk][:, :],
                        C.wB[:, hc * 1024 + f * 128: hc * 1024 + (f + 1) * 128],
                        C.acc[:, hc * L1MACRO + ii * NT: hc * L1MACRO + (ii + 1) * NT],
                        start=(hc == 0), stop=(hc == 15)),
                        waits=[ev_wo, g_done, C.pb_ev[bk]])
                C.acc_b.read(evo)
                P.v(lambda v, f=f, bk=bk: v.scalar_tensor_tensor(
                    C.xt[:, f * NT:(f + 1) * NT], C.pb[bk][:, :], 0.5,
                    C.xt[:, f * NT:(f + 1) * NT], op0=OP.mult, op1=OP.add),
                    waits=[evo, ev_x])
                ev_new = P.v(lambda v, f=f: v.scalar_tensor_tensor(
                    C.xt[:, f * NT:(f + 1) * NT], C.wbc[:, 0:NT],
                    C.l1f_boh[:, f:f + 1], C.xt[:, f * NT:(f + 1) * NT],
                    op0=OP.mult, op1=OP.add),
                    waits=[ev_cse, ev_boh])
                C.wbc_b.read(ev_new)
                C.pb_ev[bk] = ev_new
            C.xt_b.wrote(ev_new)
            emit_tail_tile(C, ti, ev_new, xdst, None, store_x=True)
    C.wB_b.read(P.last("mm"))



def _elu_chain(C, psum_ap, bias_tile_ap, out_ap, waits):
    P = C.P
    n = 512
    t0 = C.tc[:, 0:512]
    t1 = C.tc[:, 512:1024]
    P.v(lambda v: v.tensor_tensor(t0, psum_ap, bias_tile_ap, OP.add), waits=waits)
    P.v(lambda v: v.tensor_scalar(t1, t0, 0.0, None, op0=OP.min))
    e_exp = P.s(lambda s: s.activation(t1, t1, AF.Exp, scale=1.0), waits=[P.last("dv")])
    P.v(lambda v: v.tensor_scalar(t0, t0, 0.0, None, op0=OP.max), waits=[e_exp])
    return P.v(lambda v: v.scalar_tensor_tensor(out_ap, t0, -1.0, t1,
                                                op0=OP.add, op1=OP.add))


def emit_l0attn(C, W, xsrc, lnsrc, xdst, lndst, abuf, h2buf):
    P = C.P

    ev_sm = []
    for dst, key in [(C.l0a_bq, "bq_fm"), (C.l0a_bo, "bo_fm"), (C.l0a_g1, "g1_fm"),
                     (C.l0a_be1, "be1_fm"), (C.l0a_g2, "g2_fm"), (C.l0a_be2, "be2_fm"),
                     (C.l0a_b1, "b1_fm"), (C.l0a_b2, "b2_fm")]:
        ev_sm.append(P.dma("y", dst[:], W[key]))
    ev_hm = P.dma("y", C.hm16[:], W["hm16"])
    ev_hmT = P.dma("y", C.hm16T[:], W["hm16T"])

    for key, dstt in [("bk_row", C.bkb), ("bv_row", C.bvb)]:
        for h in range(2):
            ev_r = P.dma("y", C.murow[:], W[key][0:1, h * 512:(h + 1) * 512],
                         waits=[P.last("ac"), P.last("mm"), P.last("dv")])
            evb = emit_bcast_row(C, C.murow[0:1, :], C.pb[0][:, :], [ev_r, C.pb_ev[0]])
            ec = P.s(lambda s, h=h, dstt=dstt: s.copy(dstt[:, h * 512:(h + 1) * 512],
                                                      C.pb[0][:, :]),
                     waits=[evb])
            C.pb_ev[0] = ec
    ev_bias = P.last("ac")

    ev_wk = P.dma("g", C.wA[:, 0:8192].rearrange("p (c n) -> p c n", c=8),
                  W["wk"].rearrange("(c p) n -> p c n", p=128), waits=C.wA_b.war())
    ev_wv = P.dma("g", C.wA[:, 8192:16384].rearrange("p (c n) -> p c n", c=8),
                  W["wv"].rearrange("(c p) n -> p c n", p=128))
    C.wA_b.wrote(ev_wv)
    first_kv = True
    ev_kv = None
    prev_kv = None
    for ti in range(NTILES):
        t0 = ti * NT
        lxi = ti % 4
        lxb = C.lx[lxi]
        ev_lx = P.dma("y", fm_sbuf(lxb), fm_dram(lnsrc, t0, NT), waits=C.lx_b[lxi].war())
        C.lx_b[lxi].wrote(ev_lx)
        for s4 in range(4):
            toff = s4 * 128
            for half in range(2):
                bk = half
                evk = None
                for c in range(NCH):
                    evk = P.t(lambda t, c=c, half=half, bk=bk, lxb=lxb, toff=toff: t.matmul(
                        C.pb[bk][:, :], lxb[:, c * NT + toff: c * NT + toff + 128],
                        C.wA[:, c * 1024 + half * 512: c * 1024 + (half + 1) * 512],
                        start=(c == 0), stop=(c == NCH - 1)),
                        waits=[ev_wk, ev_lx, C.pb_ev[bk]])
                ee = _elu_chain(C, C.pb[bk][:, :], C.bkb[:, half * 512:(half + 1) * 512],
                                C.xbf[:, half * 512:(half + 1) * 512], [evk, ev_bias])
                C.pb_ev[bk] = ee
            P.v(lambda v: v.tensor_tensor(C.xbf[:, 1024:2048], C.xbf[:, 0:1024],
                                          C.xbf[:, 0:1024], OP.mult))
            P.v(lambda v: v.tensor_reduce(
                C.ss16[:], C.xbf[:, 1024:2048].rearrange("p (h d) -> p h d", d=64),
                axis=AX.X, op=OP.add))
            e_ln = P.s(lambda s: s.activation(C.rr16[:], C.ss16[:], AF.Ln,
                                              bias=eps_tile(C, 5.4e-19)[:], scale=1.0),
                       waits=[P.last("dv")])
            e_rs = P.s(lambda s: s.activation(C.rr16[:], C.rr16[:], AF.Exp, scale=-0.5),
                       waits=[e_ln])
            ekn = P.v(lambda v: v.tensor_tensor(
                C.gh[:, 0:1024].rearrange("p (h d) -> p h d", d=64),
                C.xbf[:, 0:1024].rearrange("p (h d) -> p h d", d=64),
                C.rr16[:].broadcast_to((128, 16, 64)), OP.mult),
                waits=[e_rs, prev_kv])
            for half in range(2):
                bk = 2 + half
                evv = None
                for c in range(NCH):
                    evv = P.t(lambda t, c=c, half=half, bk=bk, lxb=lxb, toff=toff: t.matmul(
                        C.pb[bk][:, :], lxb[:, c * NT + toff: c * NT + toff + 128],
                        C.wA[:, 8192 + c * 1024 + half * 512: 8192 + c * 1024 + (half + 1) * 512],
                        start=(c == 0), stop=(c == NCH - 1)),
                        waits=[ev_wv, ev_lx, C.pb_ev[bk]])
                ev_e = P.v(lambda v, half=half, bk=bk: v.tensor_tensor(
                    C.gh[:, 1024 + half * 512: 1024 + (half + 1) * 512],
                    C.pb[bk][:, :], C.bvb[:, half * 512:(half + 1) * 512], OP.add),
                    waits=[evv, ev_bias, prev_kv])
                C.pb_ev[bk] = ev_e
            ev_gh = P.last("dv")
            for p in range(8):
                bk = 4 + p // 4
                ev_kv = P.t(lambda t, p=p, bk=bk, fk=first_kv: t.matmul(
                    C.pb[bk][:, (p % 4) * 128:(p % 4) * 128 + 128],
                    C.gh[:, p * 128:(p + 1) * 128],
                    C.gh[:, 1024 + p * 128: 1024 + (p + 1) * 128],
                    start=(fk and p % 4 == 0), stop=(ti == NTILES - 1 and s4 == 3),
                    skip_group_check=True),
                    waits=[ekn, ev_gh] + ([C.pb_ev[4], C.pb_ev[5]] if first_kv else []))
            first_kv = False
            prev_kv = ev_kv
        C.lx_b[lxi].read(ev_kv)
    C.wA_b.read(P.last("mm"))
    for p in range(8):
        bk = 4 + p // 4
        P.s(lambda s, p=p, bk=bk: s.copy(C.kvs[0:64, p * 64:(p + 1) * 64],
                                         C.pb[bk][0:64, (p % 4) * 128:(p % 4) * 128 + 64]),
            waits=[ev_kv] + (C.kvs_b.war() if p == 0 else []))
        P.s(lambda s, p=p, bk=bk: s.copy(C.kvs[64:128, p * 64:(p + 1) * 64],
                                         C.pb[bk][64:128, (p % 4) * 128 + 64:(p % 4) * 128 + 128]))
    ev_kvs = P.last("ac")
    C.kvs_b.wrote(ev_kvs)
    C.pb_ev[4] = ev_kvs
    C.pb_ev[5] = ev_kvs

    ev_wq = P.dma("g", C.wB[:, 0:8192].rearrange("p (c n) -> p c n", c=8),
                  W["wq"].rearrange("(c p) n -> p c n", p=128), waits=C.wB_b.war())
    ev_wo = P.dma("g", C.wB[:, 8192:16384].rearrange("p (c n) -> p c n", c=8),
                  W["wo"].rearrange("(c p) n -> p c n", p=128))
    C.wB_b.wrote(ev_wo)
    for ti in range(NTILES):
        t0 = ti * NT
        lxi = ti % 4
        lxb = C.lx[lxi]
        ev_lx = P.dma("y", fm_sbuf(lxb), fm_dram(lnsrc, t0, NT), waits=C.lx_b[lxi].war())
        C.lx_b[lxi].wrote(ev_lx)
        for f in range(NCH):
            bk = f % 2
            evq = None
            for c in range(NCH):
                evq = P.t(lambda t, c=c, f=f, bk=bk, lxb=lxb: t.matmul(
                    C.pb[bk][:, :],
                    C.wB[:, c * 1024 + f * 128: c * 1024 + f * 128 + 128],
                    lxb[:, c * NT:(c + 1) * NT],
                    start=(c == 0), stop=(c == NCH - 1)),
                    waits=[ev_wq, ev_lx, C.pb_ev[bk]])
            t0b = C.tc[:, 0:512]
            t1b = C.tc[:, 512:1024]
            P.v(lambda v, bk=bk: v.tensor_copy(t0b, C.pb[bk][:, :]),
                waits=[evq] + ev_sm)
            e_cp = P.v(lambda v, f=f: v.tensor_scalar(
                t0b, t0b, C.l0a_bq[:, f:f + 1], None, op0=OP.add))
            C.pb_ev[bk] = e_cp
            P.v(lambda v: v.tensor_scalar(t1b, t0b, 0.0, None, op0=OP.min))
            e_exp = P.s(lambda s: s.activation(t1b, t1b, AF.Exp, scale=1.0),
                        waits=[P.last("dv")])
            P.v(lambda v: v.tensor_scalar(t0b, t0b, 0.0, None, op0=OP.max), waits=[e_exp])
            P.v(lambda v, f=f: v.scalar_tensor_tensor(C.xbf[:, f * NT:(f + 1) * NT],
                                                      t0b, -1.0, t1b, op0=OP.add, op1=OP.add),
                waits=[getattr(C, "pend_sqt", None)] if f == 0 else [])
        ev_eq = P.last("dv")
        P.v(lambda v: v.tensor_tensor(C.tc[:], C.xbf[:], C.xbf[:], OP.mult),
            waits=[ev_eq])
        ev_sq = P.last("dv")
        ev_ss = None
        for c in range(NCH):
            ev_ss = P.t(lambda t, c=c: t.matmul(
                C.pb[2][0:16, :], C.hm16[:, c * 16:(c + 1) * 16],
                C.tc[:, c * NT:(c + 1) * NT],
                start=(c == 0), stop=(c == NCH - 1)),
                waits=[ev_sq, ev_hm, C.pb_ev[2]])
        e_ln = P.s(lambda s: s.activation(C.rst16f[:, 0:NT], C.pb[2][0:16, :], AF.Ln,
                                          bias=eps_tile(C, 5.4e-19)[0:16, :], scale=1.0),
                   waits=[ev_ss])
        C.pb_ev[2] = e_ln
        e_rs = P.s(lambda s: s.activation(C.rst16[:, 0:NT], C.rst16f[:, 0:NT], AF.Exp,
                                          bias=eps_tile(C, -3.0 * float(np.log(2.0)))[0:16, :],
                                          scale=-0.5), waits=[e_ln])
        for c in range(NCH):
            evb = P.t(lambda t, c=c: t.matmul(
                C.pb[3][:, :], C.hm16T[0:16, c * 128:(c + 1) * 128],
                C.rst16[0:16, 0:NT], start=True, stop=True),
                waits=[e_rs, ev_hmT, C.pb_ev[3]])
            eq = P.v(lambda v, c=c: v.tensor_tensor(C.gh[:, c * NT:(c + 1) * NT],
                                                    C.xbf[:, c * NT:(c + 1) * NT],
                                                    C.pb[3][:, :], OP.mult),
                     waits=[evb, getattr(C, "last_ctx", None) if c == 0 else None])
            C.pb_ev[3] = eq
        ev_qn = P.last("dv")
        for c in range(NCH):
            bk = c % 2
            ec1 = P.t(lambda t, c=c, bk=bk: t.matmul(
                C.pb[bk][0:64, :], C.kvs[0:64, c * 64:(c + 1) * 64],
                C.gh[0:64, c * NT:(c + 1) * NT], start=True, stop=True),
                waits=[ev_qn, ev_kvs, C.pb_ev[bk]])
            C.last_ctx = ec1
            ec2 = P.t(lambda t, c=c, bk=bk: t.matmul(
                C.pb[bk][64:128, :], C.kvs[64:128, c * 64:(c + 1) * 64],
                C.gh[64:128, c * NT:(c + 1) * NT], start=True, stop=True))
            ee = P.s(lambda s, c=c, bk=bk: s.copy(C.tc[:, c * NT:(c + 1) * NT],
                                                  C.pb[bk][:, :]), waits=[ec2])
            C.pb_ev[bk] = ee
        ev_ctx = P.last("ac")
        for f in range(NCH):
            bk = f % 2
            evo = None
            for c in range(NCH):
                evo = P.t(lambda t, c=c, f=f, bk=bk: t.matmul(
                    C.pb[bk][:, :],
                    C.wB[:, 8192 + c * 1024 + f * 128: 8192 + c * 1024 + f * 128 + 128],
                    C.tc[:, c * NT:(c + 1) * NT],
                    start=(c == 0), stop=(c == NCH - 1)),
                    waits=[ev_wo, ev_ctx, C.pb_ev[bk]])
            ez = P.v(lambda v, f=f, bk=bk, lxb=lxb: v.scalar_tensor_tensor(
                C.xt[:, f * NT:(f + 1) * NT], C.pb[bk][:, :], C.l0a_bo[:, f:f + 1],
                lxb[:, f * NT:(f + 1) * NT], op0=OP.add, op1=OP.add),
                waits=[evo] + C.xt_b.war() if f == 0 else [evo])
            C.pb_ev[bk] = ez
        C.lx_b[lxi].read(P.last("dv"))
        C.xt_b.wrote(P.last("dv"))
        emit_ln_affine(C, ti, abuf, C.l0a_g1, C.l0a_be1, eps=1e-12)
    C.wB_b.read(P.last("mm"))
    C.kvs_b.read(P.last("mm"))

    ev_w1a = P.dma("g", C.wA[:].rearrange("p (c n) -> p c n", c=8),
                   W["w1"][:, 0:2048].rearrange("(c p) n -> p c n", p=128),
                   waits=C.wA_b.war())
    C.wA_b.wrote(ev_w1a)
    ev_w1b = P.dma("g", C.wB[:].rearrange("p (c n) -> p c n", c=8),
                   W["w1"][:, 2048:4096].rearrange("(c p) n -> p c n", p=128),
                   waits=C.wB_b.war())
    C.wB_b.wrote(ev_w1b)
    gh_ev = [None] * 16
    for ti in range(NTILES):
        t0 = ti * NT
        lxi = ti % 4
        lxb = C.lx[lxi]
        ev_a = P.dma("y", fm_sbuf(lxb), fm_dram(abuf, t0, NT), waits=C.lx_b[lxi].war())
        C.lx_b[lxi].wrote(ev_a)
        lastj = None
        for j in range(32):
            wsrc = C.wA if j < 16 else C.wB
            jj = j % 16
            bk = j % 2
            evh = None
            for c in range(NCH):
                evh = P.t(lambda t, c=c, jj=jj, bk=bk, wsrc=wsrc, lxb=lxb: t.matmul(
                    C.pb[bk][:, :], wsrc[:, c * 2048 + jj * 128: c * 2048 + jj * 128 + 128],
                    lxb[:, c * NT:(c + 1) * NT],
                    start=(c == 0), stop=(c == NCH - 1)),
                    waits=[ev_w1a if j < 16 else ev_w1b, ev_a, C.pb_ev[bk]])
            slot = j % 16
            eg = P.s(lambda s, j=j, bk=bk, slot=slot: s.activation(
                C.gh[:, slot * NT:(slot + 1) * NT], C.pb[bk][:, :], AF.Gelu,
                bias=C.l0a_b1[:, j:j + 1], scale=1.0),
                waits=[evh, gh_ev[slot]])
            C.pb_ev[bk] = eg
            ed = P.dma("y", h2buf[j * 128:(j + 1) * 128, t0:t0 + NT],
                       C.gh[:, slot * NT:(slot + 1) * NT], waits=[eg])
            gh_ev[slot] = ed
            lastj = evh
        C.lx_b[lxi].read(lastj)
    C.wA_b.read(P.last("mm"))
    C.wB_b.read(P.last("mm"))

    ev_w2a = P.dma("g", C.wA[:].rearrange("p (c n) -> p c n", c=16),
                   W["w2"][0:2048].rearrange("(c p) n -> p c n", p=128),
                   waits=C.wA_b.war())
    C.wA_b.wrote(ev_w2a)
    ev_w2b = P.dma("g", C.wB[:].rearrange("p (c n) -> p c n", c=16),
                   W["w2"][2048:4096].rearrange("(c p) n -> p c n", p=128),
                   waits=C.wB_b.war())
    C.wB_b.wrote(ev_w2b)
    for ti in range(NTILES):
        t0 = ti * NT
        lxi = ti % 4
        lxb = C.lx[lxi]
        ev_a = P.dma("y", fm_sbuf(lxb), fm_dram(abuf, t0, NT), waits=C.lx_b[lxi].war())
        C.lx_b[lxi].wrote(ev_a)
        ev_h2 = P.dma("y", C.acc[:].rearrange("p (c n) -> p c n", c=32),
                      h2buf[:, t0:t0 + NT].rearrange("(c p) n -> p c n", p=128),
                      waits=C.acc_b.war())
        C.acc_b.wrote(ev_h2)
        ev_x = P.dma("y", fm_sbuf(C.xt), fm_dram(xsrc, t0, NT), waits=C.xt_b.war())
        C.xt_b.wrote(ev_x)
        for f in range(NCH):
            bk = f % 2
            evz = None
            for j in range(32):
                wsrc = C.wA if j < 16 else C.wB
                jj = j % 16
                evz = P.t(lambda t, j=j, jj=jj, f=f, bk=bk, wsrc=wsrc: t.matmul(
                    C.pb[bk][:, :], wsrc[:, jj * 1024 + f * 128: jj * 1024 + f * 128 + 128],
                    C.acc[:, j * NT:(j + 1) * NT],
                    start=(j == 0), stop=(j == 31)),
                    waits=[ev_w2a if j < 16 else ev_w2b, ev_h2, C.pb_ev[bk]])
            ez = P.v(lambda v, f=f, bk=bk, lxb=lxb: v.scalar_tensor_tensor(
                C.xbf[:, f * NT:(f + 1) * NT], C.pb[bk][:, :], C.l0a_b2[:, f:f + 1],
                lxb[:, f * NT:(f + 1) * NT], op0=OP.add, op1=OP.add),
                waits=[evz, C.lx_b[lxi].wr])
            C.pb_ev[bk] = ez
            C.acc_b.read(evz)
        C.lx_b[lxi].read(P.last("dv"))
        ev_z2 = P.last("dv")
        ev_mu = None
        for c in range(NCH):
            ev_mu = P.t(lambda t, c=c: t.matmul(C.pb[4][0:1, :], C.ones128[:],
                                                C.xbf[:, c * NT:(c + 1) * NT],
                                                start=(c == 0), stop=(c == NCH - 1)),
                        waits=[ev_z2, C.pb_ev[4]])
        ev_mur = P.s(lambda s: s.activation(C.murow[:], C.pb[4][0:1, :], AF.Copy,
                                            scale=1.0 / D), waits=[ev_mu])
        ev_mub = emit_bcast_row(C, C.murow[0:1, :], C.pb[5][:, :], [ev_mur, C.pb_ev[5]])
        ev_tc = None
        for c in range(NCH):
            ev_tc = P.v(lambda v, c=c: v.tensor_tensor(C.tc[:, c * NT:(c + 1) * NT],
                                                       C.xbf[:, c * NT:(c + 1) * NT],
                                                       C.pb[5][:, :], OP.subtract),
                        waits=[ev_mub])
        ev_sq = None
        for c in range(NCH):
            ev_sq = P.v(lambda v, c=c: v.tensor_tensor(C.sqt[:, c * NT:(c + 1) * NT],
                                                       C.tc[:, c * NT:(c + 1) * NT],
                                                       C.tc[:, c * NT:(c + 1) * NT], OP.mult),
                        waits=[ev_tc])
        ev_var = None
        for c in range(NCH):
            ev_var = P.t(lambda t, c=c: t.matmul(C.pb[4][0:1, :], C.ones128[:],
                                                 C.sqt[:, c * NT:(c + 1) * NT],
                                                 start=(c == 0), stop=(c == NCH - 1)),
                         waits=[ev_sq, ev_mur])
        ev_ln = P.s(lambda s: s.activation(C.lnrow[:], C.pb[4][0:1, :], AF.Ln,
                                           bias=eps_tile(C, 1e-12)[0:1, :], scale=1.0 / D),
                    waits=[ev_var])
        C.pb_ev[4] = ev_ln
        ev_rstd = P.s(lambda s: s.activation(C.rstdrow[:], C.lnrow[:], AF.Exp, scale=-0.5),
                      waits=[ev_ln])
        ev_rb = emit_bcast_row(C, C.rstdrow[0:1, :], C.pb[5][:, :], [ev_rstd, ev_tc])
        ev_new = None
        for c in range(NCH):
            P.v(lambda v, c=c: v.tensor_tensor(C.sqt[:, c * NT:(c + 1) * NT],
                                               C.tc[:, c * NT:(c + 1) * NT],
                                               C.pb[5][:, :], OP.mult),
                waits=[ev_rb])
            P.v(lambda v, c=c: v.tensor_scalar(C.sqt[:, c * NT:(c + 1) * NT],
                                               C.sqt[:, c * NT:(c + 1) * NT],
                                               C.l0a_g2[:, c:c + 1], C.l0a_be2[:, c:c + 1],
                                               op0=OP.mult, op1=OP.add))
            ev_new = P.v(lambda v, c=c: v.tensor_tensor(C.xt[:, c * NT:(c + 1) * NT],
                                                        C.xt[:, c * NT:(c + 1) * NT],
                                                        C.sqt[:, c * NT:(c + 1) * NT], OP.add),
                         waits=[ev_x])
        C.pb_ev[5] = ev_new
        C.xt_b.wrote(ev_new)
        emit_tail_tile(C, ti, ev_new, xdst, lndst, store_x=True)
    C.wA_b.read(P.last("mm"))
    C.wB_b.read(P.last("mm"))


def emit_ln_affine(C, ti, dst, g_tile, b_tile, eps):
    P = C.P
    t0 = ti * NT
    src_ev = C.xt_b.wr
    ev_cast = P.v(lambda v: v.tensor_copy(C.xbf[:], C.xt[:]), waits=[src_ev])
    C.xt_b.read(ev_cast)
    ev_mu = None
    for c in range(NCH):
        ev_mu = P.t(lambda t, c=c: t.matmul(C.pb[4][0:1, :], C.ones128[:],
                                            C.xbf[:, c * NT:(c + 1) * NT],
                                            start=(c == 0), stop=(c == NCH - 1)),
                    waits=[ev_cast, C.pb_ev[4]])
    ev_mur = P.s(lambda s: s.activation(C.murow[:], C.pb[4][0:1, :], AF.Copy, scale=1.0 / D),
                 waits=[ev_mu])
    ev_mub = emit_bcast_row(C, C.murow[0:1, :], C.pb[5][:, :], [ev_mur, C.pb_ev[5]])
    ev_tc = None
    for c in range(NCH):
        ev_tc = P.v(lambda v, c=c: v.tensor_tensor(C.tc[:, c * NT:(c + 1) * NT],
                                                   C.xbf[:, c * NT:(c + 1) * NT],
                                                   C.pb[5][:, :], OP.subtract),
                    waits=[ev_mub])
    ev_sq = None
    for c in range(NCH):
        ev_sq = P.v(lambda v, c=c: v.tensor_tensor(C.sqt[:, c * NT:(c + 1) * NT],
                                                   C.tc[:, c * NT:(c + 1) * NT],
                                                   C.tc[:, c * NT:(c + 1) * NT], OP.mult),
                    waits=[ev_tc])
    ev_var = None
    for c in range(NCH):
        ev_var = P.t(lambda t, c=c: t.matmul(C.pb[4][0:1, :], C.ones128[:],
                                             C.sqt[:, c * NT:(c + 1) * NT],
                                             start=(c == 0), stop=(c == NCH - 1)),
                     waits=[ev_sq, ev_mur])
    ev_ln = P.s(lambda s: s.activation(C.lnrow[:], C.pb[4][0:1, :], AF.Ln,
                                       bias=eps_tile(C, eps)[0:1, :], scale=1.0 / D),
                waits=[ev_var])
    C.pb_ev[4] = ev_ln
    ev_rstd = P.s(lambda s: s.activation(C.rstdrow[:], C.lnrow[:], AF.Exp, scale=-0.5),
                  waits=[ev_ln])
    ev_rb = emit_bcast_row(C, C.rstdrow[0:1, :], C.pb[5][:, :], [ev_rstd, ev_tc])
    ev_o = None
    for c in range(NCH):
        P.v(lambda v, c=c: v.tensor_tensor(C.sqt[:, c * NT:(c + 1) * NT],
                                           C.tc[:, c * NT:(c + 1) * NT],
                                           C.pb[5][:, :], OP.mult),
            waits=[ev_rb])
        ev_o = P.v(lambda v, c=c: v.tensor_scalar(C.sqt[:, c * NT:(c + 1) * NT],
                                                  C.sqt[:, c * NT:(c + 1) * NT],
                                                  g_tile[:, c:c + 1], b_tile[:, c:c + 1],
                                                  op0=OP.mult, op1=OP.add))
    C.pb_ev[5] = ev_o
    ev_st = P.dma("y", fm_dram(dst, t0, NT), fm_sbuf(C.sqt), waits=[ev_o])
    C.pend_sqt = ev_st
    return ev_st



def _prep_host(inputs):
    H = {}
    H["c_ones1b"] = np.ones((1, 128), BF)
    x = inputs["L0_attn_x"]
    H["xT"] = [np.ascontiguousarray(x[b].T) for b in range(8)]
    H["l0f_gate_w"] = np.ascontiguousarray(inputs["L0_ffn_gate_w"])
    H["l0f_gate_b"] = np.ascontiguousarray(
        np.broadcast_to(inputs["L0_ffn_gate_b"].reshape(1, 8), (128, 8)))
    H["l0f_w1"] = np.ascontiguousarray(inputs["L0_ffn_w1"])
    H["l0f_b1"] = np.ascontiguousarray(
        inputs["L0_ffn_b1"].reshape(8, 16, 128).transpose(0, 2, 1))
    H["l0f_w2"] = np.ascontiguousarray(inputs["L0_ffn_w2"])
    H["l0f_b2"] = np.ascontiguousarray(inputs["L0_ffn_b2"])
    H["l1f_gate_w"] = np.ascontiguousarray(inputs["L1_ffn_Wg"].T)
    H["l1f_gate_b"] = np.zeros((128, 8), np.float32)
    w1t = inputs["L1_ffn_W1"].transpose(0, 2, 1)
    cols = []
    for q in range(4):
        cols.extend(range(q * 512, (q + 1) * 512))
        cols.extend(range(2048 + q * 512, 2048 + (q + 1) * 512))
    cols = np.array(cols)
    H["l1f_w1q"] = np.ascontiguousarray(w1t[:, :, cols])
    b1q = inputs["L1_ffn_b1"][:, cols]
    H["l1f_b1q"] = np.ascontiguousarray(b1q.reshape(8, 32, 128).transpose(0, 2, 1))
    H["l1f_woT"] = np.ascontiguousarray(inputs["L1_ffn_Wo"].T)
    H["l1f_bo_fm"] = np.ascontiguousarray(inputs["L1_ffn_bo"].reshape(8, 128).T)
    H["l1a_wq"] = np.ascontiguousarray(inputs["L1_attn_Wq"])
    H["l1a_wk"] = np.ascontiguousarray(inputs["L1_attn_Wk"])
    H["l1a_wv"] = np.ascontiguousarray(inputs["L1_attn_Wv"])
    H["l1a_wb"] = np.ascontiguousarray(inputs["L1_attn_Wb"])
    H["l1a_wo"] = np.ascontiguousarray(inputs["L1_attn_Wo"])

    def conv_layout(w):
        K_ = w.shape[1]
        r = w.reshape(8, 128, K_).transpose(1, 2, 0).reshape(128, K_ * 8)
        return np.ascontiguousarray(r.astype(BF))
    H["l1a_convq"] = conv_layout(inputs["L1_attn_conv_q_w"])
    H["l1a_convk"] = conv_layout(inputs["L1_attn_conv_k_w"])
    H["l1a_convv"] = conv_layout(inputs["L1_attn_conv_v_w"])
    H["l1a_localw"] = conv_layout(inputs["L1_attn_local_w"])
    H["l1a_midw"] = conv_layout(inputs["L1_attn_mid_w"])
    hm4 = np.zeros((128, 32), BF)
    for c in range(8):
        hm4[:, c * 4 + c // 2] = 1
    H["l1a_hm4"] = hm4
    oh4 = np.zeros((4, 512), BF)
    for h in range(4):
        oh4[h, h * 128:(h + 1) * 128] = 1
    H["l1a_oh4T"] = oh4
    H["l1a_maskL"] = (-np.tril(np.ones((128, 128), np.float32), -1)).astype(BF)
    H["l1a_maskU"] = (-np.triu(np.ones((128, 128), np.float32), 1)).astype(BF)
    H["l1a_maskT"] = np.triu(np.ones((128, 128), np.float32), 0).astype(BF)
    H["l1a_rw1"] = np.ascontiguousarray(inputs["L1_attn_r_w1"])
    rb1 = np.zeros(17 * 128, np.float32)
    rb1[0:2160] = inputs["L1_attn_r_b1"]
    H["l1a_rb1"] = np.ascontiguousarray(rb1.reshape(17, 128).T)
    H["l1a_rw2"] = np.ascontiguousarray(inputs["L1_attn_r_w2"])
    H["l1a_rb2"] = inputs["L1_attn_r_b2"].reshape(1, 16).copy()
    H["l1a_mixw"] = np.ascontiguousarray(inputs["L1_attn_mix_w"])
    H["l1a_onormw"] = inputs["L1_attn_onorm_w"].reshape(1, 256).copy()
    H["l0a_wq"] = np.ascontiguousarray(inputs["L0_attn_Wq"])
    H["l0a_wk"] = np.ascontiguousarray(inputs["L0_attn_Wk"])
    H["l0a_wv"] = np.ascontiguousarray(inputs["L0_attn_Wv"])
    H["l0a_wo"] = np.ascontiguousarray(inputs["L0_attn_Wo"])
    H["l0a_w1"] = np.ascontiguousarray(inputs["L0_attn_W1"])
    H["l0a_w2"] = np.ascontiguousarray(inputs["L0_attn_W2"])
    H["l0a_bk_row"] = inputs["L0_attn_bk"].reshape(1, 1024).copy()
    H["l0a_bv_row"] = inputs["L0_attn_bv"].reshape(1, 1024).copy()
    for nm, key in [("bq_fm", "L0_attn_bq"), ("bo_fm", "L0_attn_bo"),
                    ("g1_fm", "L0_attn_g1"), ("be1_fm", "L0_attn_be1"),
                    ("g2_fm", "L0_attn_g2"), ("be2_fm", "L0_attn_be2")]:
        H["l0a_" + nm] = np.ascontiguousarray(inputs[key].reshape(8, 128).T)
    H["l0a_b1_fm"] = np.ascontiguousarray(inputs["L0_attn_b1"].reshape(32, 128).T)
    H["l0a_b2_fm"] = np.ascontiguousarray(inputs["L0_attn_b2"].reshape(8, 128).T)
    hm16 = np.zeros((128, 8 * 16), BF)
    hm16T = np.zeros((16, 8 * 128), BF)
    for c in range(8):
        hm16[0:64, c * 16 + 2 * c] = 1
        hm16[64:128, c * 16 + 2 * c + 1] = 1
        hm16T[2 * c, c * 128: c * 128 + 64] = 1
        hm16T[2 * c + 1, c * 128 + 64: c * 128 + 128] = 1
    H["l0a_hm16"] = hm16
    H["l0a_hm16T"] = hm16T
    H["c_ones128"] = np.ones((128, 1), BF)
    H["c_ident"] = np.eye(128, dtype=BF)
    oh = np.zeros((8, 8 * 128), BF)
    for e in range(8):
        oh[e, e * 128:(e + 1) * 128] = 1
    H["c_oh8T"] = oh
    return H


def declare_consts(C, nc):
    names = [("ones128", C.ones128, "c_ones128", [128, 1], BF16),
             ("ones1_b", C.ones1_b, "c_ones1b", [1, 128], BF16),
             ("ident", C.ident, "c_ident", [128, 128], BF16),
             ("oh8T", C.oh8T, "c_oh8T", [8, 8 * 128], BF16)]
    evs = []
    for nm, tile, hkey, shape, dt in names:
        ap = nc.dram_tensor("in_" + hkey, shape, dt, kind="ExternalInput").ap()
        evs.append(C.P.dma("y", tile[:], ap))
    C.P.barrier()
    return [("in_" + k) for _, _, k, _, _ in names]


def build_partial(passes, hash_only=False):
    es = ExitStack()
    nc = bass.Bass("TRN2", target_bir_lowering=False)
    P = PB()
    C = make_ctx(es, nc, P)

    di = lambda name, shape, dt: nc.dram_tensor(name, shape, dt, kind="ExternalInput").ap()
    xT = di("in_xT", [D, L], F32)
    OUT = nc.dram_tensor("out_x", [D, L], F32, kind="ExternalOutput").ap()
    LNOUT = nc.dram_tensor("out_ln", [D, L], BF16, kind="ExternalOutput").ap()
    xbuf = nc.dram_tensor("xbuf", [D, L], F32, kind="Internal").ap()
    lnbuf = nc.dram_tensor("lnbuf", [D, L], BF16, kind="Internal").ap()
    wTbuf = nc.dram_tensor("wTbuf", [8, L], BF16, kind="Internal").ap()
    csbuf = nc.dram_tensor("csbuf", [1, L], F32, kind="Internal").ap()

    abuf = nc.dram_tensor("abuf", [D, L], BF16, kind="Internal").ap()
    h2buf = nc.dram_tensor("h2buf", [4096, L], BF16, kind="Internal").ap()
    a1kind = "ExternalOutput" if SNAPSHOT else "Internal"
    A1B = dict(dbg3=nc.dram_tensor("dbg3", [L, 16], F32, kind=a1kind).ap(),
               dbg4=nc.dram_tensor("dbg4", [L, D], BF16, kind=a1kind).ap(),
               qb=nc.dram_tensor("qbuf", [D, L], BF16, kind=a1kind).ap(),
               kb=nc.dram_tensor("kbuf", [D, L], BF16, kind=a1kind).ap(),
               vb=nc.dram_tensor("vbuf", [D, L], BF16, kind=a1kind).ap(),
               lob=nc.dram_tensor("lobuf", [D, L], BF16, kind=a1kind).ap(),
               midb=nc.dram_tensor("midbuf", [D, L], BF16, kind=a1kind).ap(),
               dltb=nc.dram_tensor("dltbuf", [L, D], BF16, kind=a1kind).ap(),
               h1b=nc.dram_tensor("h1buf", [2160, L], BF16, kind=a1kind).ap(),
               rows=nc.dram_tensor("rowbuf", [32, L], F32, kind=a1kind).ap())
    W1A = dict(wq=di("in_l1a_wq", [D, D], F32), wk=di("in_l1a_wk", [D, D], F32),
               wv=di("in_l1a_wv", [D, D], F32), wb=di("in_l1a_wb", [D, 4], F32),
               wo=di("in_l1a_wo", [D, D], F32),
               convq=di("in_l1a_convq", [128, 32], BF16),
               convk=di("in_l1a_convk", [128, 32], BF16),
               convv=di("in_l1a_convv", [128, 32], BF16),
               localw=di("in_l1a_localw", [128, 56], BF16),
               midw=di("in_l1a_midw", [128, 200], BF16),
               hm4=di("in_l1a_hm4", [128, 32], BF16),
               oh4T=di("in_l1a_oh4T", [4, 512], BF16),
               maskL=di("in_l1a_maskL", [128, 128], BF16),
               maskU=di("in_l1a_maskU", [128, 128], BF16),
               maskT=di("in_l1a_maskT", [128, 128], BF16),
               rw1=di("in_l1a_rw1", [1080, 2160], F32),
               rb1=di("in_l1a_rb1", [128, 17], F32),
               rw2=di("in_l1a_rw2", [2160, 16], F32),
               rb2=di("in_l1a_rb2", [1, 16], F32),
               mixw=di("in_l1a_mixw", [4, 256], F32),
               onormw=di("in_l1a_onormw", [1, 256], F32))
    W0A = dict(wq=di("in_l0a_wq", [D, D], F32), wk=di("in_l0a_wk", [D, D], F32),
               wv=di("in_l0a_wv", [D, D], F32), wo=di("in_l0a_wo", [D, D], F32),
               w1=di("in_l0a_w1", [D, 4096], F32), w2=di("in_l0a_w2", [4096, D], F32),
               bk_row=di("in_l0a_bk_row", [1, D], F32),
               bv_row=di("in_l0a_bv_row", [1, D], F32),
               bq_fm=di("in_l0a_bq_fm", [128, 8], F32),
               bo_fm=di("in_l0a_bo_fm", [128, 8], F32),
               g1_fm=di("in_l0a_g1_fm", [128, 8], F32),
               be1_fm=di("in_l0a_be1_fm", [128, 8], F32),
               g2_fm=di("in_l0a_g2_fm", [128, 8], F32),
               be2_fm=di("in_l0a_be2_fm", [128, 8], F32),
               b1_fm=di("in_l0a_b1_fm", [128, 32], F32),
               b2_fm=di("in_l0a_b2_fm", [128, 8], F32),
               hm16=di("in_l0a_hm16", [128, 128], BF16),
               hm16T=di("in_l0a_hm16T", [16, 1024], BF16))
    W1F = dict(gate_w=di("in_l1f_gate_w", [D, 8], F32),
               gate_b=di("in_l1f_gate_b", [128, 8], F32),
               w1q=di("in_l1f_w1q", [8, D, 4096], F32),
               b1q=di("in_l1f_b1q", [8, 128, 32], F32),
               woT=di("in_l1f_woT", [2048, D], F32),
               bo_fm=di("in_l1f_bo_fm", [128, 8], F32))
    W0F = dict(gate_w=di("in_l0f_gate_w", [D, 8], F32),
               gate_b=di("in_l0f_gate_b", [128, 8], F32),
               w1=di("in_l0f_w1", [8, D, 2048], F32),
               b1=di("in_l0f_b1", [8, 128, 16], F32),
               w2=di("in_l0f_w2", [8, 2048, D], F32),
               b2=di("in_l0f_b2", [8, D], F32))

    declare_consts(C, nc)
    import numpy as _np
    for v in (1e-5, 1e-12, 1e-9, 1e-10, 5.4e-19, -3.0 * float(_np.log(2.0))):
        eps_tile(C, v)
    emit_const_init(C)
    C.const_frozen = True
    P.barrier()

    def snap(tag, xs):
        if SNAPSHOT:
            sp = nc.dram_tensor("snap_" + tag, [D, L], F32,
                                kind="ExternalOutput").ap()
            P.dma("y", sp, xs)

    xsrc = xT
    if "init" in passes:
        emit_init_pass(C, xsrc, lnbuf)
        P.barrier()
    if "f0a" in passes:
        dst = OUT if passes[-1] == "f0a" else xbuf
        emit_l0ffn(C, W0F, xsrc, lnbuf, dst, LNOUT if passes[-1] == "f0a" else lnbuf, wTbuf)
        xsrc = dst
        P.barrier()
        snap("f0a", xsrc)
    if "a0" in passes:
        dst = OUT if passes[-1] == "a0" else xbuf
        emit_l0attn(C, W0A, xsrc, lnbuf, dst, LNOUT if passes[-1] == "a0" else lnbuf,
                    abuf, h2buf)
        xsrc = dst
        P.barrier()
        snap("a0", xsrc)
    if "f0b" in passes:
        dst = OUT if passes[-1] == "f0b" else xbuf
        emit_l0ffn(C, W0F, xsrc, lnbuf, dst, LNOUT if passes[-1] == "f0b" else lnbuf, wTbuf)
        xsrc = dst
        P.barrier()
        snap("f0b", xsrc)
    if "f1a" in passes:
        dst = OUT if passes[-1] == "f1a" else xbuf
        emit_l1ffn(C, W1F, xsrc, lnbuf, dst, wTbuf, csbuf)
        xsrc = dst
        P.barrier()
        snap("f1a", xsrc)
    if "lnpass" in passes:
        emit_init_pass(C, xsrc, LNOUT if passes[-1] == "lnpass" else lnbuf)
        P.barrier()
    if "a1" in passes:
        dst = OUT if passes[-1] == "a1" else xbuf
        emit_l1attn(C, W1A, xsrc, lnbuf, dst, LNOUT if passes[-1] == "a1" else lnbuf,
                    A1B)
        xsrc = dst
        P.barrier()
        snap("a1", xsrc)
    if "f1b" in passes:
        dst = OUT if passes[-1] == "f1b" else xbuf
        emit_l1ffn(C, W1F, xsrc, lnbuf, dst, wTbuf, csbuf)
        xsrc = dst
        P.barrier()

    P._emit("sync", None, P.queue_events("qy") + P.queue_events("qs")
            + P.queue_events("qg")
            + [P.last("dv"), P.last("ac"), P.last("mm"), P.last("po")],
            None, 1)

    global LAST_PB
    LAST_PB = P
    sems = {}
    for s in ("mm", "dv", "ac", "po", "sy"):
        sems[s] = es.enter_context(nc.semaphore(s))
    for base in ("qy", "qs", "qg"):
        for j in range(NPOOL):
            nm = f"{base}{j}"
            if nm in P.cnt:
                sems[nm] = es.enter_context(nc.semaphore(nm))
    block = es.enter_context(nc.Block())
    P.finalize(block, sems)
    return nc, es


def in_map_for_core(H, b, extra=()):
    m = {"in_xT": H["xT"][b]}
    for k in ["l0f_gate_w", "l0f_gate_b", "l0f_w1", "l0f_b1", "l0f_w2", "l0f_b2",
              "l1f_gate_w", "l1f_gate_b", "l1f_w1q", "l1f_b1q", "l1f_woT", "l1f_bo_fm",
              "l1a_wq", "l1a_wk", "l1a_wv", "l1a_wb", "l1a_wo", "l1a_convq",
              "l1a_convk", "l1a_convv", "l1a_localw", "l1a_midw", "l1a_hm4",
              "l1a_oh4T", "l1a_maskL", "l1a_maskU", "l1a_maskT", "l1a_rw1",
              "l1a_rb1", "l1a_rw2", "l1a_rb2", "l1a_mixw", "l1a_onormw",
              "l0a_wq", "l0a_wk", "l0a_wv", "l0a_wo", "l0a_w1", "l0a_w2",
              "l0a_bk_row", "l0a_bv_row", "l0a_bq_fm", "l0a_bo_fm", "l0a_g1_fm",
              "l0a_be1_fm", "l0a_g2_fm", "l0a_be2_fm", "l0a_b1_fm", "l0a_b2_fm",
              "l0a_hm16", "l0a_hm16T",
              "c_ones128", "c_ones1b", "c_ident", "c_oh8T"]:
        m["in_" + k] = H[k]
    return m


SNAPSHOT = False

FULL_PASSES = ["init", "f0a", "a0", "f0b", "f1a", "lnpass", "a1", "f1b"]


def kernel(**inputs):
    H = _prep_host(inputs)
    nc, es = build_partial(FULL_PASSES)
    with es:
        in_maps = [in_map_for_core(H, b) for b in range(8)]
        res = run_bass_kernel_spmd(nc, in_maps, core_ids=list(range(8)))
    out = np.stack([r["out_x"].T for r in res.results], 0)
    return out


# version 07 (final)
# speedup vs baseline: 1.2075x; 1.2075x over previous
import numpy as np
import ml_dtypes
from contextlib import ExitStack
import concourse.bass as bass
import concourse.mybir as mybir
from concourse.alu_op_type import AluOpType as OP
from concourse.bass_utils import run_bass_kernel_spmd

F32 = mybir.dt.float32
F32R = mybir.dt.float32r
BF16 = mybir.dt.bfloat16
AF = mybir.ActivationFunctionType
AX = mybir.AxisListType
BF = ml_dtypes.bfloat16

D = 1024
L = 8192
NCH = 8
NT = 512
NTILES = L // NT
MACRO = 2048
NMACRO = L // MACRO
NINNER = MACRO // NT

ENGS = ("sync", "tensor", "vector", "scalar", "gpsimd")
LAST_PB = None
AUTOCHAIN = {"vector": "dv", "scalar": "ac", "gpsimd": "po"}


NPOOL = 32


class PB:
    def __init__(self):
        self.ops = []
        self.cnt = {}
        self.qn = {"qy": 0, "qs": 0, "qg": 0}

    def _emit(self, eng, fn, waits, inc_sem, inc_amt):
        ws = []
        for w in waits:
            if w is None:
                continue
            if isinstance(w, list):
                ws.extend(x for x in w if x is not None)
            else:
                ws.append(w)
        if eng in AUTOCHAIN and fn is not None:
            sem = AUTOCHAIN[eng]
            prev = self.cnt.get(sem, 0)
            if prev > 0:
                ws.append((sem, prev))
            if inc_sem is None:
                inc_sem, inc_amt = sem, 1
        ev = None
        if inc_sem is not None:
            self.cnt[inc_sem] = self.cnt.get(inc_sem, 0) + inc_amt
            ev = (inc_sem, self.cnt[inc_sem])
        self.ops.append((eng, fn, ws, (inc_sem, inc_amt) if inc_sem else None))
        return ev

    def t(self, fn, waits=()):
        return self._emit("tensor", fn, waits, "mm", 1)

    def v(self, fn, waits=()):
        return self._emit("vector", fn, waits, None, 1)

    def s(self, fn, waits=()):
        return self._emit("scalar", fn, waits, None, 1)

    def g(self, fn, waits=()):
        return self._emit("gpsimd", fn, waits, None, 1)

    def _dma_emit(self, queue, fn, waits):
        eng = {"y": "sync", "s": "scalar", "g": "gpsimd"}[queue]
        base = {"y": "qy", "s": "qs", "g": "qg"}[queue]
        i = self.qn[base]
        self.qn[base] = i + 1
        sem = f"{base}{i % NPOOL}"
        ws = list(waits)
        if i >= NPOOL:
            ws.append((sem, 16 * (i // NPOOL)))
        return self._emit(eng, fn, ws, sem, 16)

    def dma(self, queue, out, in_, waits=()):
        fn = lambda e, out=out, in_=in_: e.dma_start(out=out, in_=in_)
        return self._dma_emit(queue, fn, waits)

    def dma_tr(self, queue, out, in_, waits=()):
        fn = lambda e, out=out, in_=in_: e.dma_start_transpose(out, in_)
        return self._dma_emit(queue, fn, waits)

    def last(self, sem):
        c = self.cnt.get(sem, 0)
        return (sem, c) if c > 0 else None

    def queue_events(self, base):
        return [(s, c) for s, c in self.cnt.items() if s.startswith(base) and c > 0]

    def barrier(self):
        evs = [(s, c) for s, c in self.cnt.items()]
        for eng in ENGS:
            self._emit(eng, None, evs, None, 1)

    def finalize(self, block, sems):
        for eng_name in ENGS:
            deco = getattr(block, eng_name)

            def make(eng_name):
                def body(eng):
                    for op_eng, fn, waits, inc in self.ops:
                        if op_eng != eng_name:
                            continue
                        best = {}
                        for s, v in waits:
                            if best.get(s, -1) < v:
                                best[s] = v
                        for s, v in best.items():
                            eng.wait_ge(sems[s], v)
                        if fn is None:
                            continue
                        inst = fn(eng)
                        if inc is not None:
                            inst.then_inc(sems[inc[0]], inc[1])
                return body

            deco(make(eng_name))


class Buf:
    __slots__ = ("wr", "rd")

    def __init__(self):
        self.wr = None
        self.rd = []

    def war(self):
        return self.rd + ([self.wr] if self.wr else [])

    def wrote(self, ev):
        self.wr = ev
        self.rd = []

    def read(self, ev):
        self.rd.append(ev)


class Ctx:
    pass


def fm_dram(ap, t0, n):
    return ap[:, t0:t0 + n].rearrange("(c p) n -> p c n", p=128)


def fm_sbuf(tile, nch=NCH, nt=NT):
    return tile[:, 0:nch * nt].rearrange("p (c n) -> p c n", c=nch)


def make_ctx(es, nc, P):
    C = Ctx()
    C.nc = nc
    C.P = P

    def sb(name, shape, dt):
        return es.enter_context(nc.sbuf_tensor(name, shape, dt))
    C.sb = sb

    C.pb = [es.enter_context(nc.psum_tensor(f"pb{i}", [128, 512], F32)) for i in range(6)]
    C.pt = [es.enter_context(nc.psum_tensor(f"pt{i}", [128, 1024], BF16)) for i in range(2)]
    C.pb_ev = [None] * 6
    C.pt_ev = [None] * 2

    C.ones128 = sb("ones128", [128, 1], BF16)
    C.ident = sb("ident", [128, 128], BF16)
    C.oh8T = sb("oh8T", [8, 8 * 128], BF16)
    C.eps_tiles = {}
    C.const_defs = []

    C.xt = sb("xt", [128, NCH * NT], F32)
    C.xt_b = Buf()
    C.lx = [sb(f"lx{i}", [128, NCH * NT], BF16) for i in range(4)]
    C.lx_b = [Buf() for _ in range(4)]
    C.xnew = C.xt
    C.xnew_b = C.xt_b
    C.xbf = sb("xbf", [128, NCH * NT], BF16)
    C.tc = sb("tc", [128, NCH * NT], BF16)
    C.sqt = C.xbf
    C.murow = sb("murow", [1, NT], F32)
    C.bch = sb("bch", [1, NT], BF16)
    C.bcl = sb("bcl", [1, NT], BF16)
    C.ones1_b = sb("ones1_b", [1, 128], BF16)
    C.lnrow = C.murow
    C.rstdrow = sb("rstdrow", [1, NT], F32)

    C.wA = sb("wA", [128, 8 * 2048], BF16)
    C.wA_b = Buf()
    C.wB = sb("wB", [128, 16 * 1024], BF16)
    C.wB_b = Buf()
    C.gh = sb("gh", [128, 16 * NT], BF16)
    C.acc = sb("acc", [128, NCH * MACRO], BF16)
    C.acc_b = Buf()
    C.scr = C.tc
    C.wTm = sb("wTm", [8, MACRO], BF16)
    C.wTm_b = Buf()
    C.wTst = sb("wTst", [8, NT], BF16)
    C.wTst_b = Buf()
    C.csrow_b = Buf()
    C.gbrow = sb("gbrow", [1, 8], F32)
    C.gbb = sb("gbb", [128, 8], F32)
    C.g_sc = sb("g_sc", [128, 8], F32)
    C.g_sc2 = sb("g_sc2", [128, 8], F32)
    C.g_m = sb("g_m", [128, 1], F32)
    C.g_m2 = sb("g_m2", [128, 1], F32)
    C.g_s = sb("g_s", [128, 1], F32)
    C.g_w = sb("g_w", [128, 8], BF16)
    C.g_wcs = sb("g_wcs", [128, 8], BF16)
    C.wbc = sb("wbc", [128, NT], BF16)
    C.wbc_b = Buf()
    C.kvs = sb("kvs", [128, 512], BF16)
    C.kvs_b = Buf()
    C.bkb = C.gh[:, 6144:7168]
    C.bvb = C.gh[:, 7168:8192]
    C.rst16 = C.gh[0:16, 4096:4608]
    C.wsc4 = sb("wsc4", [128, 512], F32)
    C.rst16f = sb("rst16f", [16, NT], F32)
    C.csst = C.rst16f[0:1, :]
    C.ss16 = sb("ss16", [128, 16], F32)
    C.rr16 = sb("rr16", [128, 16], F32)
    C.l1a_cq = sb("l1a_cq", [128, 4 * 8], BF16)
    C.l1a_ck = sb("l1a_ck", [128, 4 * 8], BF16)
    C.l1a_cv = sb("l1a_cv", [128, 4 * 8], BF16)
    C.l1a_lw = sb("l1a_lw", [128, 7 * 8], BF16)
    C.l1a_mw = sb("l1a_mw", [128, 25 * 8], BF16)
    C.hm4 = sb("hm4", [128, 8 * 4], BF16)
    C.oh4T = sb("oh4T", [4, 4 * 128], BF16)
    C.maskL = sb("maskL", [128, 128], BF16)
    C.maskU = sb("maskU", [128, 128], BF16)
    C.maskT = sb("maskT", [128, 128], BF16)
    C.r4a = C.xt[0:4, 3072:3584]
    C.r4b = C.xt[0:4, 3584:4096]
    C.r4c = C.sqt[0:8, 0:512]
    C.r4d = C.sqt[0:8, 512:1024]
    C.scal = C.xt[:, 2048:3072]
    C.S = [C.xt[:, h * 512:(h + 1) * 512] for h in range(4)]
    C.rw1x = C.acc[0:56, 0:2160]
    C.f56T = C.kvs[0:56, :]
    C.rw2s = C.wA[0:128, 2048:2320]
    C.h1x = C.wA[:, 2320:2832]
    C.mwb = C.wA[:, 0:1024]
    C.mwo = C.wA[:, 1024:2048]
    C.rb1h = sb("rb1h", [128, 17], F32)
    C.rb2b = sb("rb2b", [128, 16], F32)
    C.p16 = sb("p16", [128, 16], F32)
    C.f56 = sb("f56", [128, 64], BF16)
    C.f32sc = sb("f32sc", [128, 16], F32)
    C.hm16 = sb("hm16", [128, 8 * 16], BF16)
    C.hm16T = sb("hm16T", [16, 8 * 128], BF16)
    C.l0a_bq = sb("l0a_bq", [128, 8], F32)
    C.l0a_bo = sb("l0a_bo", [128, 8], F32)
    C.l0a_g1 = sb("l0a_g1", [128, 8], F32)
    C.l0a_be1 = sb("l0a_be1", [128, 8], F32)
    C.l0a_g2 = sb("l0a_g2", [128, 8], F32)
    C.l0a_be2 = sb("l0a_be2", [128, 8], F32)
    C.l0a_b1 = sb("l0a_b1", [128, 32], F32)
    C.l0a_b2 = sb("l0a_b2", [128, 8], F32)
    return C


def emit_bcast_row(C, row_f32, psum_ap, waits):
    P = C.P
    n = row_f32.shape[-1]
    eh = P.v(lambda v: v.tensor_copy(C.bch[0:1, 0:n], row_f32), waits=waits)
    el = P.v(lambda v: v.tensor_tensor(C.bcl[0:1, 0:n], row_f32, C.bch[0:1, 0:n],
                                       OP.subtract))
    P.t(lambda t: t.matmul(psum_ap, C.ones1_b[:], C.bch[0:1, 0:n], start=True, stop=False),
        waits=[el] + waits)
    return P.t(lambda t: t.matmul(psum_ap, C.ones1_b[:], C.bcl[0:1, 0:n],
                                  start=False, stop=True))


def eps_tile(C, value):
    if value not in C.eps_tiles:
        assert not getattr(C, "const_frozen", False), f"const {value} not pre-registered"
        t = C.sb(f"cst{len(C.eps_tiles)}", [128, 1], F32)
        C.eps_tiles[value] = t
        C.const_defs.append((t, value))
    return C.eps_tiles[value]


def emit_const_init(C):
    P = C.P
    for t, val in C.const_defs:
        P.v(lambda v, t=t, val=val: v.memset(t[:], val))



def emit_tail_tile(C, ti, xnew_ev, xdst, lndst, store_x=True):
    P = C.P
    t0 = ti * NT
    if store_x:
        ev = P.dma("y", fm_dram(xdst, t0, NT), fm_sbuf(C.xnew), waits=[xnew_ev])
        C.xnew_b.read(ev)
    if lndst is None:
        return
    ev_cast = P.v(lambda v: v.tensor_copy(C.xbf[:], C.xnew[:]), waits=[xnew_ev])
    C.xnew_b.read(ev_cast)
    ev_mu = None
    for c in range(NCH):
        ev_mu = P.t(lambda t, c=c: t.matmul(C.pb[4][0:1, :], C.ones128[:],
                                            C.xbf[:, c * NT:(c + 1) * NT],
                                            start=(c == 0), stop=(c == NCH - 1)),
                    waits=[ev_cast, C.pb_ev[4]])
    ev_mur = P.s(lambda s: s.activation(C.murow[:], C.pb[4][0:1, :], AF.Copy, scale=1.0 / D),
                 waits=[ev_mu])
    ev_mub = emit_bcast_row(C, C.murow[0:1, :], C.pb[5][:, :],
                            [ev_mur, C.pb_ev[5]])
    ev_tc = None
    for c in range(NCH):
        ev_tc = P.v(lambda v, c=c: v.tensor_tensor(C.tc[:, c * NT:(c + 1) * NT],
                                                   C.xbf[:, c * NT:(c + 1) * NT],
                                                   C.pb[5][:, :], OP.subtract),
                    waits=[ev_mub])
    ev_sq = None
    for c in range(NCH):
        ev_sq = P.v(lambda v, c=c: v.tensor_tensor(C.sqt[:, c * NT:(c + 1) * NT],
                                                   C.tc[:, c * NT:(c + 1) * NT],
                                                   C.tc[:, c * NT:(c + 1) * NT], OP.mult),
                    waits=[ev_tc])
    ev_var = None
    for c in range(NCH):
        ev_var = P.t(lambda t, c=c: t.matmul(C.pb[4][0:1, :], C.ones128[:],
                                             C.sqt[:, c * NT:(c + 1) * NT],
                                             start=(c == 0), stop=(c == NCH - 1)),
                     waits=[ev_sq, ev_mur])
    ev_ln = P.s(lambda s: s.activation(C.lnrow[:], C.pb[4][0:1, :], AF.Ln,
                                       bias=eps_tile(C, 1e-5)[0:1, :], scale=1.0 / D),
                waits=[ev_var])
    C.pb_ev[4] = ev_ln
    ev_rstd = P.s(lambda s: s.activation(C.rstdrow[:], C.lnrow[:], AF.Exp, scale=-0.5),
                  waits=[ev_ln])
    ev_rb = emit_bcast_row(C, C.rstdrow[0:1, :], C.pb[5][:, :],
                           [ev_rstd, ev_tc])
    lxi = ti % 4
    lxb = C.lx[lxi]
    ev_lnx = None
    for c in range(NCH):
        ev_lnx = P.v(lambda v, c=c, lxb=lxb: v.tensor_tensor(
            lxb[:, c * NT:(c + 1) * NT], C.tc[:, c * NT:(c + 1) * NT],
            C.pb[5][:, :], OP.mult),
            waits=[ev_rb] + C.lx_b[lxi].war())
    C.pb_ev[5] = ev_lnx
    C.lx_b[lxi].wrote(ev_lnx)
    ev_st = P.dma("y", fm_dram(lndst, t0, NT), fm_sbuf(lxb), waits=[ev_lnx])
    C.lx_b[lxi].read(ev_st)


def emit_init_pass(C, xsrc, lndst):
    P = C.P
    for ti in range(NTILES):
        ev_ld = P.dma("y", fm_sbuf(C.xnew), fm_dram(xsrc, ti * NT, NT),
                      waits=C.xnew_b.war())
        C.xnew_b.wrote(ev_ld)
        emit_tail_tile(C, ti, ev_ld, None, lndst, store_x=False)



def emit_gates(C, gate_w_dram, gate_b_dram, lnsrc, wTdram, renorm, with_cs=None):
    P = C.P
    ev_gw = P.dma("g", C.wA[:, 0:64].rearrange("p (c e) -> p c e", c=8),
                  gate_w_dram.rearrange("(c p) e -> p c e", p=128),
                  waits=C.wA_b.war())
    C.wA_b.wrote(ev_gw)
    ev_gbb = P.dma("y", C.gbb[:], gate_b_dram)
    last_mm = None
    prev_pe_reads = []
    for ti in range(NTILES):
        t0 = ti * NT
        lxi = ti % 4
        lxb = C.lx[lxi]
        ev_lx = P.dma("y", fm_sbuf(lxb), fm_dram(lnsrc, t0, NT),
                      waits=C.lx_b[lxi].war())
        C.lx_b[lxi].wrote(ev_lx)
        for sub in range(NT // 128):
            tt = t0 + sub * 128
            evm = None
            for c in range(NCH):
                evm = P.t(lambda t, c=c, sub=sub, lxb=lxb: t.matmul(
                    C.pb[4][:, 0:8],
                    lxb[:, c * NT + sub * 128: c * NT + sub * 128 + 128],
                    C.wA[:, c * 8:(c + 1) * 8],
                    start=(c == 0), stop=(c == NCH - 1)),
                    waits=[ev_lx, ev_gw, C.pb_ev[4]])
            last_mm = evm
            e1 = P.v(lambda v: v.tensor_tensor(C.g_sc[:], C.pb[4][:, 0:8], C.gbb[:], OP.add),
                     waits=[evm, ev_gbb] + prev_pe_reads)
            prev_pe_reads = []
            C.pb_ev[4] = e1
            P.v(lambda v: v.tensor_reduce(C.g_m[:], C.g_sc[:], axis=AX.X,
                                          op=OP.max, negate=True))
            e3 = P.s(lambda s: s.activation(C.g_sc2[:], C.g_sc[:], AF.Exp,
                                            bias=C.g_m[:], scale=1.0),
                     waits=[P.last("dv")])
            P.v(lambda v: v.tensor_reduce(C.g_m[:], C.g_sc2[:], axis=AX.X, op=OP.add),
                waits=[e3])
            P.v(lambda v: v.reciprocal(C.g_m2[:], C.g_m[:]))
            P.v(lambda v: v.tensor_scalar(C.g_sc[:], C.g_sc2[:], C.g_m2[:], None,
                                          op0=OP.mult))
            P.v(lambda v: v.tensor_reduce(C.g_m[:], C.g_sc[:], axis=AX.X, op=OP.max))
            P.v(lambda v: v.tensor_scalar(C.g_sc2[:], C.g_sc[:], C.g_m[:], None,
                                          op0=OP.is_ge))
            P.v(lambda v: v.scalar_tensor_tensor(C.g_sc2[:], C.g_sc2[:], -1e30,
                                                 C.g_sc[:], op0=OP.mult, op1=OP.add))
            P.v(lambda v: v.tensor_reduce(C.g_m2[:], C.g_sc2[:], axis=AX.X, op=OP.max))
            P.v(lambda v: v.tensor_scalar(C.g_sc2[:], C.g_sc[:], C.g_m2[:], None,
                                          op0=OP.is_ge))
            ew = P.v(lambda v: v.tensor_tensor(C.g_sc[:], C.g_sc[:], C.g_sc2[:], OP.mult))
            if renorm:
                P.v(lambda v: v.tensor_reduce(C.g_m[:], C.g_sc[:], axis=AX.X, op=OP.add))
                P.v(lambda v: v.tensor_scalar(C.g_m2[:], C.g_m[:], 1e-9, None, op0=OP.add))
                P.v(lambda v: v.reciprocal(C.g_s[:], C.g_m2[:]))
                ew = P.v(lambda v: v.tensor_scalar(C.g_w[:], C.g_sc[:], C.g_s[:], None,
                                                   op0=OP.mult))
                if with_cs:
                    ecs = P.v(lambda v: v.tensor_tensor(
                        C.g_wcs[:], C.g_m[:].broadcast_to((128, 8)), C.g_s[:].broadcast_to((128, 8)),
                        OP.mult))
                    evt2 = P.t(lambda t: t.transpose(C.pt[1][0:8, 0:128], C.g_wcs[:],
                                                     C.ident[:]),
                               waits=[ecs, C.pt_ev[1]])
                    prev_pe_reads.append(evt2)
                    evc2 = P.s(lambda s, sub=sub: s.copy(C.csst[0:1, sub * 128:(sub + 1) * 128],
                                                         C.pt[1][0:1, 0:128]),
                               waits=[evt2] + ([C.csrow_b.wr] if sub == 0 else []))
                    C.pt_ev[1] = evc2
                    if sub == NT // 128 - 1:
                        evd = P.dma("y", with_cs[0:1, t0:t0 + NT], C.csst[:],
                                    waits=[evc2])
                        C.csrow_b.wrote(evd)
            else:
                ew = P.v(lambda v: v.tensor_copy(C.g_w[:], C.g_sc[:]))
            evt = P.t(lambda t: t.transpose(C.pt[0][0:8, 0:128], C.g_w[:], C.ident[:]),
                      waits=[ew, C.pt_ev[0]] + (C.wTst_b.rd if sub == 0 else []))
            prev_pe_reads.append(evt)
            evc = P.s(lambda s, sub=sub: s.copy(C.wTst[0:8, sub * 128:(sub + 1) * 128],
                                                C.pt[0][0:8, 0:128]),
                      waits=[evt])
            C.pt_ev[0] = evc
            if sub == NT // 128 - 1:
                evw = P.dma("y", wTdram[:, t0:t0 + NT], C.wTst[:], waits=[evc])
                C.wTst_b.rd = [evw]
                C.wTst_b.wr = evw
        C.lx_b[lxi].read(last_mm)
    C.wA_b.read(last_mm)



def emit_l0ffn(C, W, xsrc, lnsrc, xdst, lndst, wTdram):
    P = C.P
    emit_gates(C, W["gate_w"], W["gate_b"], lnsrc, wTdram, renorm=False)
    if not hasattr(C, "l0f_b1"):
        C.l0f_b1 = C.sb("l0f_b1", [128, 8 * 16], F32)
        C.l0f_b2 = C.sb("l0f_b2", [8, 1024], BF16)
    b1s, b2s = C.l0f_b1, C.l0f_b2
    ev_b1 = P.dma("y", b1s[:].rearrange("p (e h) -> p e h", e=8),
                  W["b1"].rearrange("e p h -> p e h"))
    ev_b2 = P.dma("g", b2s[:], W["b2"])

    for mi in range(NMACRO):
        m0 = mi * MACRO
        ev_wt = P.dma("y", C.wTm[:], wTdram[:, m0:m0 + MACRO],
                      waits=C.wTm_b.war() + [C.wTst_b.wr])
        C.wTm_b.wrote(ev_wt)
        for ii in range(NINNER):
            for c in range(NCH):
                bk = 4 + (c % 2)
                evi = P.t(lambda t, c=c, ii=ii, bk=bk: t.matmul(
                    C.pb[bk][:, :], b2s[0:8, c * 128:(c + 1) * 128],
                    C.wTm[0:8, ii * NT:(ii + 1) * NT],
                    start=True, stop=True),
                    waits=[ev_b2, ev_wt, C.pb_ev[bk]])
                eva = P.s(lambda s, c=c, ii=ii, bk=bk: s.copy(
                    C.acc[:, c * MACRO + ii * NT: c * MACRO + (ii + 1) * NT],
                    C.pb[bk][:, :]),
                    waits=[evi] + (C.acc_b.war() if (ii == 0 and c == 0) else []))
                C.pb_ev[bk] = eva
        C.acc_b.wrote(P.last("ac"))
        for e in range(8):
            ev_w1 = P.dma("g", C.wA[:].rearrange("p (c n) -> p c n", c=8),
                          W["w1"][e].rearrange("(c p) n -> p c n", p=128),
                          waits=C.wA_b.war())
            C.wA_b.wrote(ev_w1)
            ev_w2 = P.dma("g", C.wB[:].rearrange("p (c n) -> p c n", c=16),
                          W["w2"][e].rearrange("(c p) n -> p c n", p=128),
                          waits=C.wB_b.war())
            C.wB_b.wrote(ev_w2)
            for ii in range(NINNER):
                i0 = m0 + ii * NT
                lxb = C.lx[ii]
                if e == 0:
                    ev_lx = P.dma("y", fm_sbuf(lxb), fm_dram(lnsrc, i0, NT),
                                  waits=C.lx_b[ii].war())
                    C.lx_b[ii].wrote(ev_lx)
                ev_lx = C.lx_b[ii].wr
                ev_h_last = None
                evh = None
                for j in range(16):
                    bk = j % 2
                    for c in range(NCH):
                        evh = P.t(lambda t, j=j, c=c, bk=bk, lxb=lxb: t.matmul(
                            C.pb[bk][:, :],
                            C.wA[:, c * 2048 + j * 128: c * 2048 + (j + 1) * 128],
                            lxb[:, c * NT:(c + 1) * NT],
                            start=(c == 0), stop=(c == NCH - 1)),
                            waits=[ev_w1, ev_lx, C.pb_ev[bk]])
                    evr = P.s(lambda s, j=j, bk=bk, e=e: s.activation(
                        C.gh[:, j * NT:(j + 1) * NT], C.pb[bk][:, :], AF.Relu,
                        bias=b1s[:, e * 16 + j: e * 16 + j + 1], scale=1.0),
                        waits=[evh, ev_b1])
                    C.pb_ev[bk] = evr
                    ev_h_last = evr
                if ii == NINNER - 1:
                    C.wA_b.read(evh)
                if e == 7:
                    C.lx_b[ii].read(evh)
                evgb = P.t(lambda t, e=e, ii=ii: t.matmul(
                    C.pb[2][:, :], C.oh8T[0:8, e * 128:(e + 1) * 128],
                    C.wTm[0:8, ii * NT:(ii + 1) * NT], start=True, stop=True),
                    waits=[ev_wt, C.pb_ev[2]])
                evgbc = P.s(lambda s: s.copy(C.wbc[:], C.pb[2][:, :]),
                            waits=[evgb] + C.wbc_b.war())
                C.wbc_b.wrote(evgbc)
                C.pb_ev[2] = evgbc
                evo = None
                for c in range(NCH):
                    bk = 2 if c % 2 == 0 else 3
                    for j in range(16):
                        evo = P.t(lambda t, c=c, j=j, bk=bk: t.matmul(
                            C.pb[bk][:, :],
                            C.wB[:, j * 1024 + c * 128: j * 1024 + (c + 1) * 128],
                            C.gh[:, j * NT:(j + 1) * NT],
                            start=(j == 0), stop=(j == 15)),
                            waits=[ev_w2, ev_h_last, C.pb_ev[bk]])
                    accsl = C.acc[:, c * MACRO + ii * NT: c * MACRO + (ii + 1) * NT]
                    evt1 = P.v(lambda v, bk=bk: v.tensor_tensor(
                        C.scr[:, 0:NT], C.pb[bk][:, :], C.wbc[:], OP.mult),
                        waits=[evo, evgbc])
                    C.wbc_b.read(evt1)
                    P.v(lambda v, accsl=accsl: v.tensor_tensor(
                        accsl, accsl, C.scr[:, 0:NT], OP.add),
                        waits=[C.acc_b.wr])
                    C.pb_ev[bk] = evt1
                if ii == NINNER - 1:
                    C.wB_b.read(evo)
                    C.wTm_b.read(evgb)
        acc_done = P.last("dv")
        for ii in range(NINNER):
            i0 = m0 + ii * NT
            ev_x = P.dma("y", fm_sbuf(C.xt), fm_dram(xsrc, i0, NT),
                         waits=C.xt_b.war())
            C.xt_b.wrote(ev_x)
            ev_new = None
            for c in range(NCH):
                accsl = C.acc[:, c * MACRO + ii * NT: c * MACRO + (ii + 1) * NT]
                ev_new = P.v(lambda v, c=c, accsl=accsl: v.scalar_tensor_tensor(
                    C.xnew[:, c * NT:(c + 1) * NT], accsl, 0.5,
                    C.xt[:, c * NT:(c + 1) * NT], op0=OP.mult, op1=OP.add),
                    waits=[acc_done, ev_x] + C.xnew_b.war())
            C.xt_b.read(ev_new)
            C.acc_b.read(ev_new)
            C.xnew_b.wrote(ev_new)
            emit_tail_tile(C, i0 // NT, ev_new, xdst, lndst, store_x=True)




HL = 32


def _halo(buf, off):
    return buf[:, off:off + 8 * 544].rearrange("p (c n) -> p c n", c=8)


def emit_l1attn_p1(C, W, lnsrc, bufs):
    P = C.P
    rows = bufs["rows"]
    ev_c = []
    for dst, key in [(C.l1a_cq, "convq"), (C.l1a_ck, "convk"), (C.l1a_cv, "convv"),
                     (C.l1a_lw, "localw"), (C.l1a_mw, "midw"), (C.hm4, "hm4"),
                     (C.oh4T, "oh4T"), (C.maskL, "maskL"), (C.maskU, "maskU"),
                     (C.maskT, "maskT")]:
        ev_c.append(P.dma("y", dst[:], W[key]))
    ev_wq = P.dma("g", C.wA[:, 0:8192].rearrange("p (c n) -> p c n", c=8),
                  W["wq"].rearrange("(c p) n -> p c n", p=128), waits=C.wA_b.war())
    ev_wk = P.dma("g", C.wA[:, 8192:16384].rearrange("p (c n) -> p c n", c=8),
                  W["wk"].rearrange("(c p) n -> p c n", p=128))
    C.wA_b.wrote(ev_wk)
    ev_wv = P.dma("g", C.wB[:, 0:8192].rearrange("p (c n) -> p c n", c=8),
                  W["wv"].rearrange("(c p) n -> p c n", p=128), waits=C.wB_b.war())
    ev_wb = P.dma("g", C.wB[:, 8192:8224].rearrange("p (c n) -> p c n", c=8),
                  W["wb"].rearrange("(c p) n -> p c n", p=128))
    C.wB_b.wrote(ev_wb)
    ev_w = [ev_wq, ev_wk, ev_wv, ev_wb]
    P.v(lambda v: v.memset(C.gh[:, 0:4352], 0.0), waits=C.acc_b.war())
    z1 = P.v(lambda v: v.memset(C.acc[:, 0:13056], 0.0))
    C.acc_b.wrote(z1)
    prev_evac = [z1]
    prev_lm = []
    for ti in range(NTILES):
        t0 = ti * NT
        lxi = ti % 4
        lxb = C.lx[lxi]
        ev_lx = P.dma("y", fm_sbuf(lxb), fm_dram(lnsrc, t0, NT), waits=C.lx_b[lxi].war())
        C.lx_b[lxi].wrote(ev_lx)
        for (woff, wsrc, buf, boff) in [(0, C.wA, C.gh, 0), (8192, C.wA, C.acc, 8704),
                                        (0, C.wB, C.acc, 0)]:
            for f in range(NCH):
                bk = f % 2
                evp = None
                for c in range(NCH):
                    evp = P.t(lambda t, c=c, f=f, bk=bk, woff=woff, wsrc=wsrc, lxb=lxb: t.matmul(
                        C.pb[bk][:, :],
                        wsrc[:, woff + c * 1024 + f * 128: woff + c * 1024 + (f + 1) * 128],
                        lxb[:, c * NT:(c + 1) * NT],
                        start=(c == 0), stop=(c == NCH - 1)),
                        waits=ev_w + [ev_lx, C.pb_ev[bk]])
                ev_e = P.v(lambda v, f=f, bk=bk, buf=buf, boff=boff: v.tensor_copy(
                    buf[:, boff + f * 544 + HL: boff + f * 544 + HL + NT], C.pb[bk][:, :]),
                    waits=[evp] + prev_evac)
                prev_evac = []
                C.pb_ev[bk] = ev_e
        evb = None
        for c in range(NCH):
            evb = P.t(lambda t, c=c, lxb=lxb: t.matmul(
                C.pb[4][0:4, :], C.wB[:, 8192 + c * 4: 8192 + (c + 1) * 4],
                lxb[:, c * NT:(c + 1) * NT], start=(c == 0), stop=(c == NCH - 1)),
                waits=[ev_wb, ev_lx, C.pb_ev[4]])
        ev_t = P.s(lambda s: s.activation(C.r4a[0:4, 0:NT], C.pb[4][0:4, :], AF.Tanh,
                                          scale=0.5), waits=[evb, getattr(C, "r4a_st", None)])
        C.pb_ev[4] = ev_t
        ev_bt = P.v(lambda v: v.tensor_scalar(C.r4a[0:4, 0:NT], C.r4a[0:4, 0:NT], 0.5, 0.5,
                                              op0=OP.mult, op1=OP.add), waits=[ev_t])
        C.r4a_st = P.dma("y", rows[0:4, t0:t0 + NT], C.r4a[0:4, 0:NT], waits=[ev_bt])
        ctmp = C.xt[:, 0:2048].bitcast(BF16)
        ctmp3 = ctmp.rearrange("p (c n) -> p c n", c=8)

        def conv_lanes(src_, boff, wcol, K_, extra_waits, accv):
            lasts = []
            for (em, f0, f1) in [(P.v, 0, 5), (P.g, 5, 8)]:
                nf = f1 - f0
                ev = None
                for j in range(K_):
                    sh = HL - (K_ - 1) + j
                    wt = wcol[:, j * 8 + f0:j * 8 + f1].broadcast_to((128, nf, NT))
                    hv = _halo(src_, boff)[:, f0:f1, sh:sh + NT]
                    if j == 0:
                        ev = em(lambda e, hv=hv, wt=wt, f0=f0, f1=f1, accv=accv:
                                e.tensor_tensor(accv[:, f0:f1, :], hv, wt, OP.mult),
                                waits=extra_waits)
                    else:
                        em(lambda e, hv=hv, wt=wt, f0=f0, f1=f1: e.tensor_tensor(
                            ctmp3[:, f0:f1, :], hv, wt, OP.mult))
                        ev = em(lambda e, f0=f0, f1=f1, accv=accv: e.tensor_tensor(
                            accv[:, f0:f1, :], accv[:, f0:f1, :],
                            ctmp3[:, f0:f1, :], OP.add))
                lasts.append(ev)
            return lasts
        ev_ss_prev = None
        for (acb, boff, src_, wcol, dram, roff) in [
                (C.tc, 0, C.gh, C.l1a_cq, bufs["qb"], 4),
                (C.sqt, 8704, C.acc, C.l1a_ck, bufs["kb"], 8),
                (C.tc, 0, C.acc, C.l1a_cv, bufs["vb"], None)]:
            acv = acb[:].rearrange("p (c n) -> p c n", c=8)
            w0 = ev_c + prev_lm + ([ev_ss_prev] if ev_ss_prev else []) \
                + [C.cvd_b[id(acb)].wr if hasattr(C, "cvd_b") and id(acb) in C.cvd_b else None] \
                + [P.last("dv")]
            w0 = [w for w in w0 if w is not None]
            prev_lm = []
            lv = conv_lanes(src_, boff, wcol, 4, w0, acv)
            if not hasattr(C, "cvd_b"):
                C.cvd_b = {}
            if roff is not None:
                ev_si = P.s(lambda s, acb=acb: s.activation(acb[:], acb[:], AF.Silu,
                                                            scale=1.0), waits=lv)
                C.cvd_b[id(acb)] = Buf()
                C.cvd_b[id(acb)].wrote(
                    P.dma("y", fm_dram(dram, t0, NT),
                          acb[:].rearrange("p (c n) -> p c n", c=8), waits=[ev_si]))
                ev_sq = P.v(lambda v, acb=acb: v.tensor_tensor(ctmp, acb[:], acb[:],
                                                               OP.mult), waits=[ev_si])
                ev_ss = None
                for c in range(NCH):
                    ev_ss = P.t(lambda t, c=c: t.matmul(
                        C.pb[5][0:4, :], C.hm4[:, c * 4:(c + 1) * 4],
                        ctmp[:, c * NT:(c + 1) * NT],
                        start=(c == 0), stop=(c == NCH - 1)),
                        waits=[ev_sq, C.pb_ev[5]] + ev_c)
                ev_ss_prev = ev_ss
                ev_sse = P.v(lambda v: v.tensor_copy(C.r4b[0:4, 0:NT], C.pb[5][0:4, :]),
                             waits=[ev_ss, getattr(C, "r4b_st", None)])
                C.pb_ev[5] = ev_sse
                C.r4b_st = P.dma("y", rows[roff:roff + 4, t0:t0 + NT], C.r4b[0:4, 0:NT],
                                 waits=[ev_sse])
            else:
                ev_si = P.s(lambda s, acb=acb: s.activation(
                    _halo(C.acc, 4352)[:, :, HL:HL + NT],
                    acb[:].rearrange("p (c n) -> p c n", c=8), AF.Silu,
                    scale=1.0), waits=lv + [P.last("dv"), getattr(C, "vp_st", None)])
                C.vp_st = P.dma("y", fm_dram(dram, t0, NT),
                                _halo(C.acc, 4352)[:, :, HL:HL + NT], waits=[ev_si])
                ev_ss_prev = ev_si
        ev_vp = P.last("ac")
        for (acb, wcol, K_, dram) in [(C.sqt, C.l1a_lw, 7, bufs["lob"]),
                                      (C.tc, C.l1a_mw, 25, bufs["midb"])]:
            acv = acb[:].rearrange("p (c n) -> p c n", c=8)
            w0 = ev_c + prev_lm + ([ev_ss_prev] if ev_ss_prev else []) \
                + [C.cvd_b[id(acb)].wr if id(acb) in C.cvd_b else None] \
                + [ev_vp, P.last("dv"), P.last("po")]
            w0 = [w for w in w0 if w is not None]
            prev_lm = []
            ev_ss_prev = None
            lv = conv_lanes(C.acc, 4352, wcol, K_, w0, acv)
            C.cvd_b[id(acb)] = Buf()
            stlm = P.dma("y", fm_dram(dram, t0, NT),
                         acb[:].rearrange("p (c n) -> p c n", c=8), waits=lv)
            C.cvd_b[id(acb)].wrote(stlm)
            prev_lm = lv + ([stlm] if acb is C.tc else [])
        halo_w = [P.last("po")]
        for (buf, off) in [(C.gh, 0), (C.acc, 8704), (C.acc, 0), (C.acc, 4352)]:
            P.v(lambda v, buf=buf, off=off: v.tensor_copy(
                _halo(buf, off)[:, :, 0:HL], _halo(buf, off)[:, :, NT:NT + HL]),
                waits=halo_w)
            halo_w = []
        prev_evac = [P.last("dv")]
        C.lx_b[lxi].read(P.last("mm"))
    C.wA_b.read(P.last("mm"))
    C.wB_b.read(P.last("mm"))
    C.acc_b.wrote(P.last("dv"))


def emit_l1attn_p2(C, W, bufs):
    P = C.P
    rows = bufs["rows"]
    for ti in range(NTILES):
        t0 = ti * NT
        ev_l = P.dma("y", C.r4a[0:4, 0:NT], rows[4:8, t0:t0 + NT],
                     waits=[P.last("dv"), P.last("ac"), getattr(C, "r4a_st", None)])
        ev1 = P.s(lambda s: s.activation(C.r4a[0:4, 0:NT], C.r4a[0:4, 0:NT], AF.Ln,
                                         bias=eps_tile(C, 1e-12)[0:4, :], scale=1.0),
                  waits=[ev_l])
        ev2 = P.s(lambda s: s.activation(C.r4a[0:4, 0:NT], C.r4a[0:4, 0:NT], AF.Exp,
                                         scale=-0.5), waits=[ev1])
        st1 = P.dma("y", rows[12:16, t0:t0 + NT], C.r4a[0:4, 0:NT], waits=[ev2])
        ev_l2 = P.dma("y", C.r4b[0:4, 0:NT], rows[8:12, t0:t0 + NT],
                      waits=[P.last("ac"), ev2, getattr(C, "r4b_st", None)])
        ev3 = P.s(lambda s: s.activation(C.r4b[0:4, 0:NT], C.r4b[0:4, 0:NT], AF.Ln,
                                         bias=eps_tile(C, 1e-12)[0:4, :], scale=1.0),
                  waits=[ev_l2])
        ev4 = P.s(lambda s: s.activation(C.r4b[0:4, 0:NT], C.r4b[0:4, 0:NT], AF.Exp,
                                         scale=-0.5), waits=[ev3])
        C.r4b_st = P.dma("y", rows[16:20, t0:t0 + NT], C.r4b[0:4, 0:NT], waits=[ev4])
        ev_l3 = P.dma("y", C.r4a[0:4, 0:NT], rows[0:4, t0:t0 + NT],
                      waits=[ev2, P.last("ac"), st1])
        ev5 = P.v(lambda v: v.tensor_tensor(C.r4a[0:4, 0:NT], C.r4a[0:4, 0:NT],
                                            C.r4b[0:4, 0:NT], OP.mult),
                  waits=[ev_l3, ev4])
        C.r4a_st = P.dma("y", rows[20:24, t0:t0 + NT], C.r4a[0:4, 0:NT], waits=[ev5])
    for ti in range(NTILES):
        t0 = ti * NT
        ev_a = P.dma("g", C.r4c[0:4, 0:NT], rows[0:4, t0:t0 + NT],
                     waits=[P.last("dv"), P.last("mm")] + P.queue_events("qy"))
        ev_b = P.dma("g", C.r4c[4:8, 0:NT], rows[12:16, t0:t0 + NT])
        ev_c2 = P.dma("g", C.r4d[0:4, 0:NT], rows[16:20, t0:t0 + NT])
        ev_d = P.dma("g", C.r4d[4:8, 0:NT], rows[20:24, t0:t0 + NT])
        for s4 in range(4):
            cu = ti * 4 + s4
            evt = P.t(lambda t, s4=s4: t.transpose(
                C.pt[0][:, 0:8], C.r4c[0:8, s4 * 128:(s4 + 1) * 128], C.ident[0:8, 0:8]),
                waits=[ev_a, ev_b, C.pt_ev[0]])
            eve = P.v(lambda v, cu=cu: v.tensor_copy(C.scal[:, cu * 16:cu * 16 + 8],
                                                     C.pt[0][:, 0:8]),
                      waits=[evt])
            C.pt_ev[0] = eve
            evt2 = P.t(lambda t, s4=s4: t.transpose(
                C.pt[0][:, 0:8], C.r4d[0:8, s4 * 128:(s4 + 1) * 128], C.ident[0:8, 0:8]),
                waits=[ev_c2, ev_d, eve, C.pt_ev[0]])
            eve2 = P.v(lambda v, cu=cu: v.tensor_copy(C.scal[:, cu * 16 + 8:cu * 16 + 16],
                                                      C.pt[0][:, 0:8]),
                       waits=[evt2])
            C.pt_ev[0] = eve2
    ev_scal = P.last("dv")
    _emit_delta(C, bufs, ev_scal)


def _emit_delta(C, bufs, ev_scal):
    P = C.P
    qb, kb_, vb_ = bufs["qb"], bufs["kb"], bufs["vb"]
    dltb = bufs["dltb"]

    def wAh(h, off, n=128):
        return C.wA[:, h * 4096 + off: h * 4096 + off + n]

    def wBh(h, off, n=128):
        return C.wB[:, h * 4096 + off: h * 4096 + off + n]

    RB1, RC1 = 0, 128
    RB = {2: 384, 4: 512, 8: 640, 16: 768, 32: 896, 64: 256}
    RR = 0

    def Whi(h, s):
        return C.wB[:, h * 4096 + 2 * s: h * 4096 + 2 * s + 128]

    def Wlo(h, s):
        return C.wB[:, h * 4096 + 2 * s + 128: h * 4096 + 2 * s + 256]

    def Wsc(h):
        return C.wB[:, h * 4096 + 2048: h * 4096 + 2304].bitcast(F32)

    def evac_hl(h, s, pbh, waits):
        eh = P.s(lambda s_, h=h, s=s, pbh=pbh: s_.copy(Whi(h, s), pbh[:, 0:128]),
                 waits=waits)
        return P.v(lambda v, h=h, s=s, pbh=pbh: v.tensor_tensor(
            Wlo(h, s), pbh[:, 0:128], Whi(h, s), OP.subtract), waits=[eh])

    def evac_hi(h, s, pbh, waits):
        return P.s(lambda s_, h=h, s=s, pbh=pbh: s_.copy(Whi(h, s), pbh[:, 0:128]),
                   waits=waits)

    for h in range(4):
        P.v(lambda v, h=h: v.memset(C.S[h][:], 0.0))
        P.v(lambda v, h=h: v.memset(wAh(h, 2048, 512), 0.0))
    ev_s0 = P.last("dv")
    ld_guard = [ev_s0] * 4
    C.o_dma = [None] * 4
    scan_prev = [ev_s0] * 4
    pt_ch = [C.pt_ev[0], C.pt_ev[1]]

    for cu in range(64):
        tt0 = cu * 128

        def col(h, r):
            return C.scal[:, cu * 16 + r * 4 + h: cu * 16 + r * 4 + h + 1]

        ev_ld = []
        for h in range(4):
            e1 = P.dma("y", wAh(h, 0, 256).rearrange("p (c n) -> p c n", c=2),
                       qb[256 * h:256 * (h + 1), tt0:tt0 + 128]
                       .rearrange("(c p) n -> p c n", p=128),
                       waits=[ld_guard[h]])
            e2 = P.dma_tr("s", wAh(h, 256, 256),
                          kb_[256 * h:256 * (h + 1), tt0:tt0 + 128],
                          waits=[ld_guard[h]])
            e3 = P.dma_tr("s", wAh(h, 512, 256),
                          vb_[256 * h:256 * (h + 1), tt0:tt0 + 128],
                          waits=[ld_guard[h]])
            ev_ld.append((e1, e2, e3))
        ev_tm = []
        for h in range(4):
            e1, e2, e3 = ev_ld[h]
            a = P.v(lambda v, h=h, sc=col(h, 2): v.tensor_scalar(
                wAh(h, 768, 256), wAh(h, 256, 256), sc, None, op0=OP.mult),
                waits=[e2, ev_scal])
            b = P.v(lambda v, h=h, sc=col(h, 3): v.tensor_scalar(
                wAh(h, 1024, 256), wAh(h, 256, 256), sc, None, op0=OP.mult))
            c = P.v(lambda v, h=h, sc=col(h, 0): v.tensor_scalar(
                wAh(h, 1280, 256), wAh(h, 512, 256), sc, None, op0=OP.mult),
                waits=[e3])
            ev_tm.append((a, b, c))
        ev_fm = []
        for h in range(4):
            ptb = C.pt[h // 2]
            evs = []
            for (src_off, dst_off) in [(768, 1536), (1024, 1792)]:
                for half in range(2):
                    evt = P.t(lambda t, h=h, src_off=src_off, half=half, ptb=ptb: t.transpose(
                        ptb[:, 0:128], wAh(h, src_off + half * 128, 128), C.ident[:]),
                        waits=[ev_tm[h][0], ev_tm[h][1], pt_ch[h // 2]])
                    eve = P.v(lambda v, h=h, dst_off=dst_off, half=half, ptb=ptb: v.tensor_copy(
                        wAh(h, dst_off + half * 128, 128), ptb[:, 0:128]),
                        waits=[evt])
                    pt_ch[h // 2] = eve
                    evs.append(eve)
            ev_fm.append(evs[-1])
        pb4 = C.pb[4]

        def pb43():
            return pb4[:, 0:512].rearrange("p (h c) -> p h c", h=4)

        def W3(s, lo=False):
            base = 2 * s + (128 if lo else 0)
            return C.wB[:].rearrange("p (h c) -> p h c", h=4)[:, :, base:base + 128]

        def wsc3():
            return C.wsc4[:].rearrange("p (h c) -> p h c", h=4)

        def evac3(s, mm_ev):
            eh = P.s(lambda s_, s=s: s_.copy(W3(s), pb43()), waits=[mm_ev])
            el = P.v(lambda v, s=s: v.tensor_tensor(W3(s, True), pb43(), W3(s),
                                                    OP.subtract), waits=[eh])
            C.pb_ev[4] = el
            return el

        mA = None
        for h in range(4):
            for dc in range(2):
                first = (h == 0 and dc == 0)
                mA = P.t(lambda t, h=h, dc=dc, first=first: t.matmul(
                    pb4[:, h * 128:(h + 1) * 128],
                    wAh(h, 1792 + dc * 128, 128), wAh(h, 1536 + dc * 128, 128),
                    start=first, stop=(h == 3 and dc == 1), skip_group_check=True),
                    waits=(ev_fm + [C.pb_ev[4]]) if first else [])
        pm = None
        for h in range(4):
            pm = P.v(lambda v, h=h: v.tensor_tensor(
                C.wsc4[:, h * 128:(h + 1) * 128], pb4[:, h * 128:(h + 1) * 128],
                C.maskL[:], OP.mult), waits=[mA] if h == 0 else [])
        ehB = P.s(lambda s_: s_.copy(W3(RB1), wsc3()), waits=[pm])
        elB = P.v(lambda v: v.tensor_tensor(W3(RB1, True), wsc3(), W3(RB1),
                                            OP.subtract), waits=[ehB])
        mB = None
        for h in range(4):
            for dc in range(2):
                first = (h == 0 and dc == 0)
                mB = P.t(lambda t, h=h, dc=dc, first=first: t.matmul(
                    pb4[:, h * 128:(h + 1) * 128],
                    wAh(h, 1536 + dc * 128, 128), wAh(h, 1792 + dc * 128, 128),
                    start=first, stop=(h == 3 and dc == 1), skip_group_check=True),
                    waits=[pm] if first else [])
        pm = None
        for h in range(4):
            pm = P.v(lambda v, h=h: v.tensor_tensor(
                C.wsc4[:, h * 128:(h + 1) * 128], pb4[:, h * 128:(h + 1) * 128],
                C.maskU[:], OP.mult), waits=[mB] if h == 0 else [])
        ehC = P.s(lambda s_: s_.copy(W3(RC1), wsc3()), waits=[pm])
        elC = P.v(lambda v: v.tensor_tensor(W3(RC1, True), wsc3(), W3(RC1),
                                            OP.subtract), waits=[ehC])
        C.pb_ev[4] = pm
        S0, S1, S2, S3, S4, S5, S6, S7 = 0, 128, 256, 384, 512, 640, 768, 896
        sq_seq = [(S0, S1, S2),
                  (S1, S0, S3),
                  (S3, S2, S0),
                  (S2, S3, S4),
                  (S4, S0, S2),
                  (S0, S4, S5),
                  (S5, S2, S0),
                  (S2, S5, S6),
                  (S6, S0, S2),
                  (S0, S6, S7),
                  (S2, S7, S2)]
        prev = elC
        for (lo_, ro_, oo_) in sq_seq:
            mm = None
            for h in range(4):
                reg = pb4[:, h * 128:(h + 1) * 128]
                first = (h == 0)
                mm = P.t(lambda t, h=h, lo_=lo_, ro_=ro_, reg=reg, first=first: t.matmul(
                    reg, Whi(h, lo_), Whi(h, ro_),
                    start=first, stop=False, skip_group_check=True),
                    waits=[prev, C.pb_ev[4]] if first else [])
                mm = P.t(lambda t, h=h, lo_=lo_, ro_=ro_, reg=reg: t.matmul(
                    reg, Wlo(h, lo_), Whi(h, ro_),
                    start=False, stop=False, skip_group_check=True))
                mm = P.t(lambda t, h=h, lo_=lo_, ro_=ro_, reg=reg: t.matmul(
                    reg, Whi(h, lo_), Wlo(h, ro_),
                    start=False, stop=(h == 3), skip_group_check=True))
            prev = evac3(oo_, mm)
        mm = None
        for h in range(4):
            reg = pb4[:, h * 128:(h + 1) * 128]
            first = (h == 0)
            mm = P.t(lambda t, h=h, reg=reg, first=first: t.matmul(
                reg, C.ident[:], C.ident[:], start=first, stop=False,
                skip_group_check=True),
                waits=[prev, C.pb_ev[4]] if first else [])
            mm = P.t(lambda t, h=h, reg=reg: t.matmul(
                reg, C.ident[:], Whi(h, RC1), start=False, stop=False,
                skip_group_check=True))
            mm = P.t(lambda t, h=h, reg=reg: t.matmul(
                reg, C.ident[:], Wlo(h, RC1), start=False, stop=False,
                skip_group_check=True))
            mm = P.t(lambda t, h=h, reg=reg: t.matmul(
                reg, Whi(h, RB[2]), C.ident[:], start=False, stop=False,
                skip_group_check=True))
            mm = P.t(lambda t, h=h, reg=reg: t.matmul(
                reg, Wlo(h, RB[2]), C.ident[:], start=False, stop=False,
                skip_group_check=True))
            mm = P.t(lambda t, h=h, reg=reg: t.matmul(
                reg, Whi(h, RB[2]), Whi(h, RC1), start=False, stop=False,
                skip_group_check=True))
            mm = P.t(lambda t, h=h, reg=reg: t.matmul(
                reg, Wlo(h, RB[2]), Whi(h, RC1), start=False, stop=False,
                skip_group_check=True))
            mm = P.t(lambda t, h=h, reg=reg: t.matmul(
                reg, Whi(h, RB[2]), Wlo(h, RC1), start=False, stop=(h == 3),
                skip_group_check=True))
        prev = evac3(RR, mm)
        for lvl in (4, 8, 16, 32, 64):
            mm = None
            for h in range(4):
                reg = pb4[:, h * 128:(h + 1) * 128]
                first = (h == 0)
                mm = P.t(lambda t, h=h, reg=reg, first=first: t.matmul(
                    reg, C.ident[:], Whi(h, RR), start=first, stop=False,
                    skip_group_check=True),
                    waits=[prev, C.pb_ev[4]] if first else [])
                mm = P.t(lambda t, h=h, reg=reg: t.matmul(
                    reg, C.ident[:], Wlo(h, RR), start=False, stop=False,
                    skip_group_check=True))
                mm = P.t(lambda t, h=h, lvl=lvl, reg=reg: t.matmul(
                    reg, Whi(h, RB[lvl]), Whi(h, RR), start=False, stop=False,
                    skip_group_check=True))
                mm = P.t(lambda t, h=h, lvl=lvl, reg=reg: t.matmul(
                    reg, Wlo(h, RB[lvl]), Whi(h, RR), start=False, stop=False,
                    skip_group_check=True))
                mm = P.t(lambda t, h=h, lvl=lvl, reg=reg: t.matmul(
                    reg, Whi(h, RB[lvl]), Wlo(h, RR), start=False, stop=(h == 3),
                    skip_group_check=True))
            prev = evac3(RR, mm)
        prev_ev = [prev] * 4
        ev_cl = []
        for h in range(4):
            pbh = C.pb[h]
            mu = P.t(lambda t, h=h, pbh=pbh: t.matmul(
                pbh[:, 128:384], Whi(h, RR), wAh(h, 1280, 256),
                start=True, stop=True),
                waits=[prev_ev[h], ev_tm[h][2], C.pb_ev[h]])
            eu = P.v(lambda v, h=h, pbh=pbh: v.tensor_copy(wBh(h, 3200, 256),
                                                           pbh[:, 128:384]),
                     waits=[mu])
            mw = None
            for half in range(2):
                mw = P.t(lambda t, h=h, half=half, pbh=pbh: t.matmul(
                    pbh[:, 0:128], wAh(h, 1024 + half * 128, 128), Whi(h, RR),
                    start=True, stop=True),
                    waits=[eu, C.pb_ev[h]])
                ew = P.v(lambda v, h=h, half=half, pbh=pbh: v.tensor_copy(
                    wBh(h, 3456 + half * 128, 128), pbh[:, 0:128]), waits=[mw])
                C.pb_ev[h] = ew
            ma = None
            for dc in range(2):
                ma = P.t(lambda t, h=h, dc=dc, pbh=pbh: t.matmul(
                    pbh[:, 0:128], wAh(h, 1536 + dc * 128, 128), wAh(h, dc * 128, 128),
                    start=(dc == 0), stop=(dc == 1)),
                    waits=[C.pb_ev[h], ev_ld[h][0]])
            ea = P.v(lambda v, h=h, pbh=pbh: v.tensor_tensor(wBh(h, 3072), pbh[:, 0:128],
                                                             C.maskT[:], OP.mult),
                     waits=[ma])
            C.pb_ev[h] = ea
            ev_cl.append(ea)
        for h in range(4):
            pbh = C.pb[h]
            mp = None
            for dc in range(2):
                mp = P.t(lambda t, h=h, dc=dc, pbh=pbh: t.matmul(
                    pbh[:, 128:384], wBh(h, 3456 + dc * 128, 128),
                    wAh(h, 2048 + dc * 256, 256),
                    start=(dc == 0), stop=(dc == 1)),
                    waits=[ev_cl[h], scan_prev[h], C.pb_ev[h]])
            ui = P.v(lambda v, h=h, pbh=pbh: v.tensor_tensor(
                wBh(h, 3712, 256), wBh(h, 3200, 256), pbh[:, 128:384], OP.subtract),
                waits=[mp])
            mo = None
            for dc in range(2):
                mo = P.t(lambda t, h=h, dc=dc, pbh=pbh: t.matmul(
                    pbh[:, 128:384], wAh(h, dc * 128, 128), wAh(h, 2048 + dc * 256, 256),
                    start=(dc == 0), stop=False),
                    waits=[ui, C.pb_ev[h]])
            mo = P.t(lambda t, h=h, pbh=pbh: t.matmul(
                pbh[:, 128:384], wBh(h, 3072), wBh(h, 3712, 256),
                start=False, stop=True))
            eo = P.v(lambda v, h=h, pbh=pbh, sc=col(h, 1): v.tensor_scalar(
                C.sqt[:, h * 256:(h + 1) * 256], pbh[:, 128:384], sc, None,
                op0=OP.mult), waits=[mo, C.o_dma[h]])
            C.pb_ev[h] = eo
            C.o_dma[h] = P.dma("y", dltb[tt0:tt0 + 128, 256 * h:256 * (h + 1)],
                               C.sqt[:, h * 256:(h + 1) * 256], waits=[eo])
            sev = None
            for dc in range(2):
                ms = P.t(lambda t, h=h, dc=dc, pbh=pbh: t.matmul(
                    pbh[:, 128:384], wAh(h, 768 + dc * 128, 128), wBh(h, 3712, 256),
                    start=True, stop=True),
                    waits=[eo if dc == 0 else sev, C.pb_ev[h]])
                es = P.v(lambda v, h=h, dc=dc, pbh=pbh: v.tensor_tensor(
                    C.S[h][:, dc * 256:(dc + 1) * 256], C.S[h][:, dc * 256:(dc + 1) * 256],
                    pbh[:, 128:384], OP.add), waits=[ms])
                sev = P.v(lambda v, h=h, dc=dc: v.tensor_copy(
                    wAh(h, 2048 + dc * 256, 256), C.S[h][:, dc * 256:(dc + 1) * 256]))
                C.pb_ev[h] = sev
            scan_prev[h] = sev
            ld_guard[h] = sev
    C.pt_ev[0] = pt_ch[0]
    C.pt_ev[1] = pt_ch[1]
    C.wA_b.read(P.last("mm"))
    C.wA_b.read(P.last("dv"))
    C.wB_b.read(P.last("mm"))
    C.wB_b.read(P.last("dv"))


def emit_l1attn_p3(C, W, xsrc, lnsrc, xdst, lndst, bufs):
    P = C.P
    dltb, h1b = bufs["dltb"], bufs["h1b"]
    ev_w1 = P.dma("g", C.wA[:, 0:7 * 2160].rearrange("p (c n) -> p c n", c=7),
                  W["rw1"][0:896].rearrange("(c p) n -> p c n", p=128),
                  waits=C.wA_b.war())
    C.wA_b.wrote(ev_w1)
    ev_w1b = P.dma("g", C.wB[:, 0:2160], W["rw1"][896:1024], waits=C.wB_b.war())
    C.wB_b.wrote(ev_w1b)
    ev_w1x = P.dma("g", C.rw1x[:], W["rw1"][1024:1080])
    ev_rb1 = P.dma("y", C.rb1h[:], W["rb1"])
    gh_slots = [None] * 2
    for ti in range(NTILES):
        t0 = ti * NT
        lxi = ti % 4
        lxb = C.lx[lxi]
        ev_lx = P.dma("y", fm_sbuf(lxb), fm_dram(lnsrc, t0, NT), waits=C.lx_b[lxi].war())
        C.lx_b[lxi].wrote(ev_lx)
        ev_f56 = []
        for s4 in range(4):
            cu = ti * 4 + s4
            gsl = (s4 % 2) * 4096
            tt0 = cu * 128
            lds = [P.dma("y", C.gh[:, gsl + 2048:gsl + 3072],
                         dltb[tt0:tt0 + 128, :], waits=[gh_slots[s4 % 2]])]
            for (i, buf) in [(0, bufs["lob"]), (1, bufs["midb"]), (3, bufs["vb"])]:
                lds.append(P.dma_tr("s", C.gh[:, gsl + i * 1024:gsl + (i + 1) * 1024],
                                    buf[:, tt0:tt0 + 128], waits=[gh_slots[s4 % 2]]))
            evf = None
            for (q, i) in [(0, 0), (2, 1), (4, 2), (6, 3)]:
                o3 = C.gh[:, gsl + i * 1024:gsl + (i + 1) * 1024].rearrange(
                    "p (h d) -> p h d", d=256)
                evf = P.v(lambda v, q=q, o3=o3: v.tensor_reduce(
                    C.p16[:, q * 2:q * 2 + 4] if False else C.f32sc[:, 0:4],
                    o3, axis=AX.X, op=OP.add), waits=lds + [evf])
                evf = P.v(lambda v, q=q: v.tensor_scalar(
                    C.f56[:, q * 4:(q + 1) * 4], C.f32sc[:, 0:4], 1.0 / 256, None,
                    op0=OP.mult))
                evf = P.v(lambda v, o3=o3: v.tensor_tensor(
                    C.tc[:, 0:1024], o3.rearrange("p h d -> p (h d)"),
                    o3.rearrange("p h d -> p (h d)"), OP.mult))
                evf = P.v(lambda v: v.tensor_reduce(
                    C.f32sc[:, 4:8], C.tc[:, 0:1024].rearrange("p (h d) -> p h d", d=256),
                    axis=AX.X, op=OP.add))
                evf = P.v(lambda v, q=q: v.tensor_tensor(
                    C.f32sc[:, 8:12], C.f56[:, q * 4:(q + 1) * 4],
                    C.f56[:, q * 4:(q + 1) * 4], OP.mult))
                evf = P.v(lambda v, q=q: v.scalar_tensor_tensor(
                    C.f32sc[:, 8:12], C.f32sc[:, 8:12], -256.0 / 255.0,
                    C.f32sc[:, 4:8], op0=OP.mult, op1=OP.bypass))
                evf = P.v(lambda v, q=q: v.scalar_tensor_tensor(
                    C.f56[:, (q + 1) * 4:(q + 2) * 4], C.f32sc[:, 4:8], 1.0 / 255.0,
                    C.f32sc[:, 8:12], op0=OP.mult, op1=OP.add))
            pairq = 8
            for (i, j) in [(0, 1), (0, 2), (0, 3), (1, 2), (1, 3), (2, 3)]:
                oi = C.gh[:, gsl + i * 1024:gsl + (i + 1) * 1024]
                oj = C.gh[:, gsl + j * 1024:gsl + (j + 1) * 1024]
                evf = P.v(lambda v, oi=oi, oj=oj: v.tensor_tensor(
                    C.tc[:, 0:1024], oi, oj, OP.mult))
                evf = P.v(lambda v: v.tensor_reduce(
                    C.f32sc[:, 0:4], C.tc[:, 0:1024].rearrange("p (h d) -> p h d", d=256),
                    axis=AX.X, op=OP.add))
                evf = P.v(lambda v, pairq=pairq: v.tensor_scalar(
                    C.f56[:, pairq * 4:(pairq + 1) * 4], C.f32sc[:, 0:4], 1.0 / 256,
                    None, op0=OP.mult))
                pairq += 1
            gh_slots[s4 % 2] = evf
            evt = P.t(lambda t: t.transpose(C.pt[0][0:56, 0:128], C.f56[:, 0:56],
                                            C.ident[:]),
                      waits=[evf, C.pt_ev[0]])
            eve = P.v(lambda v, s4=s4: v.tensor_copy(
                C.f56T[0:56, s4 * 128:(s4 + 1) * 128], C.pt[0][0:56, 0:128]),
                waits=[evt])
            C.pt_ev[0] = eve
            ev_f56.append(eve)
        if not hasattr(C, "h1_dma"):
            C.h1_dma = [None, None]
        for j in range(17):
            bk = j % 2
            M = 128 if j < 16 else 112
            evh = None
            for c in range(NCH):
                loc = (C.wA, c * 2160 + j * 128) if c < 7 else (C.wB, (c - 7) * 0 + j * 128)
                evh = P.t(lambda t, c=c, j=j, bk=bk, loc=loc, lxb=lxb, M=M: t.matmul(
                    C.pb[bk][0:M, :], loc[0][:, loc[1]:loc[1] + M],
                    lxb[:, c * NT:(c + 1) * NT],
                    start=(c == 0), stop=False),
                    waits=[ev_w1, ev_w1b, ev_lx, C.pb_ev[bk]])
            evh = P.t(lambda t, j=j, bk=bk, M=M: t.matmul(
                C.pb[bk][0:M, :], C.rw1x[0:56, j * 128:j * 128 + M],
                C.f56T[0:56, :], start=False, stop=True),
                waits=ev_f56 + [ev_w1x])
            eg = P.s(lambda s, j=j, bk=bk, M=M: s.activation(
                C.sqt[0:M, (j % 2) * NT:(j % 2 + 1) * NT],
                C.pb[bk][0:M, :], AF.Silu, bias=C.rb1h[0:M, j:j + 1], scale=1.0),
                waits=[evh, ev_rb1, C.h1_dma[j % 2]])
            C.pb_ev[bk] = eg
            ed = P.dma("y", h1b[j * 128:j * 128 + M, t0:t0 + NT],
                       C.sqt[0:M, (j % 2) * NT:(j % 2 + 1) * NT], waits=[eg])
            C.h1_dma[j % 2] = ed
        C.lx_b[lxi].read(P.last("mm"))
    C.wA_b.read(P.last("mm"))
    C.wB_b.read(P.last("mm"))

    ev_wo = P.dma("g", C.wB[:, 0:8192].rearrange("p (c n) -> p c n", c=8),
                  W["wo"].rearrange("(c p) n -> p c n", p=128), waits=C.wB_b.war())
    C.wB_b.wrote(ev_wo)
    ev_w2 = P.dma("g", C.rw2s[:, 0:256].rearrange("p (c n) -> p c n", c=16),
                  W["rw2"][0:2048].rearrange("(c p) n -> p c n", p=128),
                  waits=C.wA_b.war())
    ev_w2b = P.dma("g", C.rw2s[0:112, 256:272], W["rw2"][2048:2160])
    C.wA_b.wrote(ev_w2b)
    ev_rb2l = P.dma("y", C.murow[0:1, 0:16], W["rb2"])
    ev_rb2 = emit_bcast_row(C, C.murow[0:1, 0:16], C.pb[4][:, 0:16],
                            [ev_rb2l, C.pb_ev[4]])
    ev_rb2e = P.v(lambda v: v.tensor_copy(C.rb2b[:], C.pb[4][:, 0:16]), waits=[ev_rb2])
    C.pb_ev[4] = ev_rb2e
    ev_mx = P.dma("g", C.r4c[0:4, 0:256], W["mixw"],
                  waits=[C.h1_dma[0], C.h1_dma[1]] + C.wA_b.war())
    ev_on = P.dma("y", C.murow[0:1, 0:256], W["onormw"], waits=[ev_rb2e])
    ev_onb = emit_bcast_row(C, C.murow[0:1, 0:256], C.pb[4][:, 0:256],
                            [ev_on, C.pb_ev[4]])
    ev_one = P.s(lambda s: s.copy(C.wbc[:, 0:256], C.pb[4][:, 0:256]), waits=[ev_onb])
    C.pb_ev[4] = ev_one
    for h in range(4):
        mm = P.t(lambda t, h=h: t.matmul(C.pb[4][:, 0:256], C.oh4T[0:4, h * 128:(h + 1) * 128],
                                         C.r4c[0:4, 0:256], start=True, stop=True),
                 waits=[ev_mx, C.pb_ev[4]])
        eb = P.v(lambda v, h=h: v.tensor_copy(C.mwb[:, h * 256:(h + 1) * 256],
                                              C.pb[4][:, 0:256]), waits=[mm])
        ee = P.v(lambda v, h=h: v.tensor_tensor(C.mwo[:, h * 256:(h + 1) * 256],
                                                C.pb[4][:, 0:256], C.wbc[:, 0:256],
                                                OP.mult), waits=[ev_one])
        C.pb_ev[4] = ee
    ev_mwo = P.last("dv")
    gh_slots = [None] * 2
    for ti in range(NTILES):
        t0 = ti * NT
        lxi = ti % 4
        ev_h1 = P.dma("y", C.acc[:, 0:8192].rearrange("p (c n) -> p c n", c=16),
                      h1b[0:2048, t0:t0 + NT].rearrange("(c p) n -> p c n", p=128),
                      waits=C.acc_b.war())
        C.acc_b.wrote(ev_h1)
        ev_h1x = P.dma("y", C.h1x[0:112, 0:NT], h1b[2048:2160, t0:t0 + NT])
        ev_x = P.dma("y", fm_sbuf(C.xt), fm_dram(xsrc, t0, NT), waits=C.xt_b.war())
        C.xt_b.wrote(ev_x)
        for s4 in range(4):
            cu = ti * 4 + s4
            tt0 = cu * 128
            gsl = (s4 % 2) * 4096
            evl = None
            for j in range(17):
                if j < 16:
                    lhsT = C.acc[:, j * NT + s4 * 128: j * NT + s4 * 128 + 128]
                else:
                    lhsT = C.h1x[0:112, s4 * 128:s4 * 128 + 128]
                rr = C.rw2s[0:(128 if j < 16 else 112), j * 16:(j + 1) * 16]
                evl = P.t(lambda t, j=j, lhsT=lhsT, rr=rr: t.matmul(
                    C.pb[4][:, 0:16], lhsT, rr,
                    start=(j == 0), stop=(j == 16)),
                    waits=[ev_h1, ev_h1x, ev_w2, ev_w2b, C.pb_ev[4]])
            e1 = P.v(lambda v: v.tensor_tensor(C.p16[:], C.pb[4][:, 0:16], C.rb2b[:],
                                               OP.add),
                     waits=[evl, ev_rb2e, getattr(C, "p16_st", None)])
            C.pb_ev[4] = e1
            P.v(lambda v: v.tensor_reduce(C.f32sc[:, 0:4],
                                          C.p16[:].rearrange("p (h d) -> p h d", d=4),
                                          axis=AX.X, op=OP.max, negate=True))
            P.v(lambda v: v.tensor_tensor(C.p16[:].rearrange("p (h d) -> p h d", d=4),
                                          C.p16[:].rearrange("p (h d) -> p h d", d=4),
                                          C.f32sc[:, 0:4].broadcast_to((128, 4, 4)),
                                          OP.add))
            ex = P.s(lambda s: s.activation(C.p16[:], C.p16[:], AF.Exp, scale=1.0),
                     waits=[P.last("dv")])
            P.v(lambda v: v.tensor_reduce(C.f32sc[:, 0:4],
                                          C.p16[:].rearrange("p (h d) -> p h d", d=4),
                                          axis=AX.X, op=OP.add), waits=[ex])
            P.v(lambda v: v.reciprocal(C.f32sc[:, 4:8], C.f32sc[:, 0:4]))
            P.v(lambda v: v.tensor_tensor(C.p16[:].rearrange("p (h d) -> p h d", d=4),
                                          C.p16[:].rearrange("p (h d) -> p h d", d=4),
                                          C.f32sc[:, 4:8].broadcast_to((128, 4, 4)),
                                          OP.mult))
            ep = P.v(lambda v: v.tensor_scalar(C.p16[:], C.p16[:], 0.96, 0.01,
                                               op0=OP.mult, op1=OP.add))
            if "dbg3" in bufs:
                C.p16_st = P.dma("y", bufs["dbg3"][tt0:tt0 + 128, :], C.p16[:],
                                 waits=[ep])
            lds = [P.dma("y", C.gh[:, gsl + 2048:gsl + 3072],
                         dltb[tt0:tt0 + 128, :], waits=[gh_slots[s4 % 2]])]
            for (i, buf) in [(0, bufs["lob"]), (1, bufs["midb"]), (3, bufs["vb"])]:
                lds.append(P.dma_tr("s", C.gh[:, gsl + i * 1024:gsl + (i + 1) * 1024],
                                    buf[:, tt0:tt0 + 128], waits=[gh_slots[s4 % 2]]))
            mx0 = 8192 + (s4 % 2) * 1024
            evm = None
            for h in range(4):
                for i in range(4):
                    o_ = C.gh[:, gsl + i * 1024 + h * 256: gsl + i * 1024 + (h + 1) * 256]
                    md = C.acc[:, mx0 + h * 256: mx0 + (h + 1) * 256]
                    if i == 0:
                        evm = P.v(lambda v, h=h, o_=o_, md=md: v.tensor_scalar(
                            md, o_, C.p16[:, h * 4:h * 4 + 1], None, op0=OP.mult),
                            waits=lds + [ep])
                    else:
                        evm = P.v(lambda v, h=h, o_=o_, i=i, md=md: v.scalar_tensor_tensor(
                            md, o_, C.p16[:, h * 4 + i:h * 4 + i + 1],
                            md, op0=OP.mult, op1=OP.add))
            gh_slots[s4 % 2] = evm
            for h in range(4):
                md = C.acc[:, mx0 + h * 256: mx0 + (h + 1) * 256]
                P.v(lambda v, h=h, gsl=gsl, md=md: v.tensor_tensor(
                    C.gh[:, gsl + 0:gsl + 256], md, md, OP.mult))
                P.v(lambda v, h=h, gsl=gsl: v.tensor_reduce(
                    C.f32sc[:, h:h + 1], C.gh[:, gsl + 0:gsl + 256], axis=AX.X,
                    op=OP.add))
                P.v(lambda v, h=h, gsl=gsl, md=md: v.tensor_tensor(
                    C.gh[:, gsl + 0:gsl + 256], md,
                    C.mwb[:, h * 256:(h + 1) * 256], OP.mult), waits=[ev_mwo])
                P.v(lambda v, h=h, gsl=gsl: v.tensor_tensor(
                    C.gh[:, gsl + 256:gsl + 512], C.gh[:, gsl + 0:gsl + 256],
                    C.gh[:, gsl + 0:gsl + 256], OP.mult))
                P.v(lambda v, h=h, gsl=gsl: v.tensor_reduce(
                    C.f32sc[:, 4 + h:5 + h], C.gh[:, gsl + 256:gsl + 512], axis=AX.X,
                    op=OP.add))
            ec = P.v(lambda v: v.scalar_tensor_tensor(
                C.f32sc[:, 8:12], C.f32sc[:, 0:4], 1e-5, C.f32sc[:, 4:8],
                op0=OP.mult, op1=OP.add))
            el = P.s(lambda s: s.activation(C.f32sc[:, 8:12], C.f32sc[:, 8:12], AF.Ln,
                                            bias=eps_tile(C, 1e-10)[:], scale=1.0 / 256),
                     waits=[ec])
            er = P.s(lambda s: s.activation(C.f32sc[:, 8:12], C.f32sc[:, 8:12], AF.Exp,
                                            scale=-0.5), waits=[el])
            evo = None
            for h in range(4):
                md = C.acc[:, mx0 + h * 256: mx0 + (h + 1) * 256]
                P.v(lambda v, h=h, md=md: v.tensor_tensor(
                    C.tc[:, h * 256:(h + 1) * 256], md,
                    C.mwo[:, h * 256:(h + 1) * 256], OP.mult), waits=[er])
                evo = P.v(lambda v, h=h: v.tensor_scalar(
                    C.tc[:, h * 256:(h + 1) * 256], C.tc[:, h * 256:(h + 1) * 256],
                    C.f32sc[:, 8 + h:9 + h], None, op0=OP.mult))
            if "dbg4" in bufs:
                C.on_st = P.dma("y", bufs["dbg4"][tt0:tt0 + 128, :], C.tc[:, 0:1024],
                                waits=[evo, getattr(C, "on_st", None)])
            for c8 in range(8):
                evt = P.t(lambda t, c8=c8: t.transpose(
                    C.pt[0][:, 0:128], C.tc[:, c8 * 128:(c8 + 1) * 128], C.ident[:]),
                    waits=[evo, C.pt_ev[0], getattr(C, "on_st", None)])
                eve = P.v(lambda v, c8=c8, s4=s4: v.tensor_copy(
                    C.xbf[:, c8 * NT + s4 * 128: c8 * NT + s4 * 128 + 128],
                    C.pt[0][:, 0:128]), waits=[evt])
                C.pt_ev[0] = eve
        ev_on_t = P.last("dv")
        ev_new = None
        for f in range(NCH):
            bk = f % 2
            evz = None
            for c in range(NCH):
                evz = P.t(lambda t, c=c, f=f, bk=bk: t.matmul(
                    C.pb[bk][:, :], C.wB[:, c * 1024 + f * 128: c * 1024 + (f + 1) * 128],
                    C.xbf[:, c * NT:(c + 1) * NT],
                    start=(c == 0), stop=(c == NCH - 1)),
                    waits=[ev_wo, ev_on_t, C.pb_ev[bk]])
            ev_new = P.v(lambda v, f=f, bk=bk: v.tensor_tensor(
                C.xt[:, f * NT:(f + 1) * NT], C.xt[:, f * NT:(f + 1) * NT],
                C.pb[bk][:, :], OP.add), waits=[evz, ev_x])
            C.pb_ev[bk] = ev_new
        C.xt_b.wrote(P.last("dv"))
        emit_tail_tile(C, ti, P.last("dv"), xdst, lndst, store_x=True)
    C.wB_b.read(P.last("mm"))


def emit_l1attn(C, W, xsrc, lnsrc, xdst, lndst, bufs):
    emit_l1attn_p1(C, W, lnsrc, bufs)
    C.P.barrier()
    emit_l1attn_p2(C, W, bufs)
    C.P.barrier()
    emit_l1attn_p3(C, W, xsrc, lnsrc, xdst, lndst, bufs)
    C.P.barrier()



L1MACRO = 1024
L1NI = L1MACRO // NT


def emit_l1ffn(C, W, xsrc, lnsrc, xdst, wTdram, csdram):
    P = C.P
    emit_gates(C, W["gate_w"], W["gate_b"], lnsrc, wTdram, renorm=True, with_cs=csdram)
    if not hasattr(C, "l1f_b1"):
        C.l1f_b1 = C.sb("l1f_b1", [128, 8 * 32], F32)
        C.l1f_boh = C.sb("l1f_boh", [128, 8], F32)
    b1s = C.l1f_b1
    ev_b1 = P.dma("y", b1s[:].rearrange("p (e h) -> p e h", e=8),
                  W["b1q"].rearrange("e p h -> p e h"))
    ev_bo = P.dma("y", C.l1f_boh[:], W["bo_fm"])
    ev_boh = P.v(lambda v: v.tensor_scalar(C.l1f_boh[:], C.l1f_boh[:], 0.5, None,
                                           op0=OP.mult), waits=[ev_bo])
    ev_wo = P.dma("g", C.wB[:].rearrange("p (c n) -> p c n", c=16),
                  W["woT"].rearrange("(c p) n -> p c n", p=128), waits=C.wB_b.war())
    C.wB_b.wrote(ev_wo)

    nmac = L // L1MACRO
    wq_ev = [None, None]
    wq_rd = [None, None]
    qidx = 0
    for mi in range(nmac):
        m0 = mi * L1MACRO
        ev_wt = P.dma("y", C.wTm[0:8, 0:L1MACRO], wTdram[:, m0:m0 + L1MACRO],
                      waits=C.wTm_b.war() + [C.wTst_b.wr])
        C.wTm_b.wrote(ev_wt)
        for ii in range(L1NI):
            lxb = C.lx[ii]
            ev_lx = P.dma("y", fm_sbuf(lxb), fm_dram(lnsrc, m0 + ii * NT, NT),
                          waits=C.lx_b[ii].war())
            C.lx_b[ii].wrote(ev_lx)
        for e in range(8):
            for ii in range(L1NI):
                evgb = P.t(lambda t, e=e, ii=ii: t.matmul(
                    C.pb[2][:, :], C.oh8T[0:8, e * 128:(e + 1) * 128],
                    C.wTm[0:8, ii * NT:(ii + 1) * NT], start=True, stop=True),
                    waits=[ev_wt, C.pb_ev[2]])
                evgbc = P.s(lambda s, ii=ii: s.copy(C.wbc[:, 0:NT] if ii == 0 else C.kvs[:, 0:NT],
                                                    C.pb[2][:, :]),
                            waits=[evgb] + (C.wbc_b.war() if ii == 0 else C.kvs_b.war()))
                (C.wbc_b if ii == 0 else C.kvs_b).wrote(evgbc)
                C.wTm_b.read(evgb)
                C.pb_ev[2] = evgbc
            ev_comb = P.last("ac")
            for q in range(4):
                buf = qidx % 2
                woff = buf * 8192
                prev_glu = P.last("dv")
                ev_w = P.dma("g", C.wA[:, woff:woff + 8192].rearrange("p (c n) -> p c n", c=8),
                             W["w1q"][e, :, q * 1024:(q + 1) * 1024]
                             .rearrange("(c p) n -> p c n", p=128),
                             waits=[wq_rd[buf], wq_ev[buf]] + (C.wA_b.war() if qidx < 2 else []))
                wq_ev[buf] = ev_w
                for ii in range(L1NI):
                    lxb = C.lx[ii]
                    combt = C.wbc[:, 0:NT] if ii == 0 else C.kvs[:, 0:NT]
                    for j in range(8):
                        bk = j % 2
                        evh = None
                        for c in range(NCH):
                            evh = P.t(lambda t, c=c, j=j, bk=bk, woff=woff, lxb=lxb: t.matmul(
                                C.pb[bk][:, :],
                                C.wA[:, woff + c * 1024 + j * 128: woff + c * 1024 + (j + 1) * 128],
                                lxb[:, c * NT:(c + 1) * NT],
                                start=(c == 0), stop=(c == NCH - 1)),
                                waits=[ev_w, C.lx_b[ii].wr, C.pb_ev[bk]])
                        gslot = (ii * 8 + j) % 16
                        fn = AF.Identity if j < 4 else AF.Sigmoid
                        ee = P.s(lambda s, j=j, bk=bk, e=e, q=q, gslot=gslot, fn=fn: s.activation(
                            C.gh[:, gslot * NT:(gslot + 1) * NT], C.pb[bk][:, :],
                            fn, bias=b1s[:, e * 32 + q * 8 + j: e * 32 + q * 8 + j + 1],
                            scale=1.0), waits=[evh, ev_b1, prev_glu])
                        C.pb_ev[bk] = ee
                        if ii == L1NI - 1 and j == 7:
                            wq_rd[buf] = evh
                    ev_gl = P.last("ac")
                    for jj in range(4):
                        a_sl = C.gh[:, ((ii * 8 + jj) % 16) * NT: ((ii * 8 + jj) % 16 + 1) * NT]
                        s_sl = C.gh[:, ((ii * 8 + jj + 4) % 16) * NT: ((ii * 8 + jj + 4) % 16 + 1) * NT]
                        hc = 4 * q + jj
                        gsl = C.acc[:, hc * L1MACRO + ii * NT: hc * L1MACRO + (ii + 1) * NT]
                        P.v(lambda v, a_sl=a_sl, s_sl=s_sl: v.tensor_tensor(
                            C.scr[:, 0:NT], a_sl, s_sl, OP.mult),
                            waits=[ev_gl, ev_comb])
                        if e == 0:
                            P.v(lambda v, gsl=gsl, combt=combt: v.tensor_tensor(
                                gsl, C.scr[:, 0:NT], combt, OP.mult),
                                waits=C.acc_b.war() if (q == 0 and jj == 0 and ii == 0 and mi >= 0) else [])
                        else:
                            P.v(lambda v, combt=combt: v.tensor_tensor(
                                C.scr[:, 512:1024], C.scr[:, 0:NT], combt, OP.mult))
                            P.v(lambda v, gsl=gsl: v.tensor_tensor(
                                gsl, gsl, C.scr[:, 512:1024], OP.add))
                qidx += 1
            C.wbc_b.read(P.last("dv"))
            C.kvs_b.read(P.last("dv"))
        for ii in range(L1NI):
            C.lx_b[ii].read(P.last("mm"))
        C.acc_b.wrote(P.last("dv"))
        g_done = P.last("dv")
        for ii in range(L1NI):
            i0 = m0 + ii * NT
            ti = i0 // NT
            ev_cs = P.dma("y", C.csst[:], csdram[0:1, i0:i0 + NT],
                          waits=[C.csrow_b.wr])
            ev_csb = emit_bcast_row(C, C.csst[0:1, :], C.pb[4][:, :],
                                    [ev_cs, C.pb_ev[4]])
            ev_cse = P.s(lambda s: s.copy(C.wbc[:, 0:NT], C.pb[4][:, :]),
                         waits=[ev_csb] + C.wbc_b.war())
            C.wbc_b.wrote(ev_cse)
            C.pb_ev[4] = ev_cse
            ev_x = P.dma("y", fm_sbuf(C.xt), fm_dram(xsrc, i0, NT), waits=C.xt_b.war())
            C.xt_b.wrote(ev_x)
            ev_new = None
            for f in range(NCH):
                bk = 2 + f % 2
                evo = None
                for hc in range(16):
                    evo = P.t(lambda t, hc=hc, f=f, bk=bk, ii=ii: t.matmul(
                        C.pb[bk][:, :],
                        C.wB[:, hc * 1024 + f * 128: hc * 1024 + (f + 1) * 128],
                        C.acc[:, hc * L1MACRO + ii * NT: hc * L1MACRO + (ii + 1) * NT],
                        start=(hc == 0), stop=(hc == 15)),
                        waits=[ev_wo, g_done, C.pb_ev[bk]])
                C.acc_b.read(evo)
                P.v(lambda v, f=f, bk=bk: v.scalar_tensor_tensor(
                    C.xt[:, f * NT:(f + 1) * NT], C.pb[bk][:, :], 0.5,
                    C.xt[:, f * NT:(f + 1) * NT], op0=OP.mult, op1=OP.add),
                    waits=[evo, ev_x])
                ev_new = P.v(lambda v, f=f: v.scalar_tensor_tensor(
                    C.xt[:, f * NT:(f + 1) * NT], C.wbc[:, 0:NT],
                    C.l1f_boh[:, f:f + 1], C.xt[:, f * NT:(f + 1) * NT],
                    op0=OP.mult, op1=OP.add),
                    waits=[ev_cse, ev_boh])
                C.wbc_b.read(ev_new)
                C.pb_ev[bk] = ev_new
            C.xt_b.wrote(ev_new)
            emit_tail_tile(C, ti, ev_new, xdst, None, store_x=True)
    C.wB_b.read(P.last("mm"))



def _elu_chain(C, psum_ap, bias_tile_ap, out_ap, waits):
    P = C.P
    n = 512
    t0 = C.tc[:, 0:512]
    t1 = C.tc[:, 512:1024]
    P.v(lambda v: v.tensor_tensor(t0, psum_ap, bias_tile_ap, OP.add), waits=waits)
    P.v(lambda v: v.tensor_scalar(t1, t0, 0.0, None, op0=OP.min))
    e_exp = P.s(lambda s: s.activation(t1, t1, AF.Exp, scale=1.0), waits=[P.last("dv")])
    P.v(lambda v: v.tensor_scalar(t0, t0, 0.0, None, op0=OP.max), waits=[e_exp])
    return P.v(lambda v: v.scalar_tensor_tensor(out_ap, t0, -1.0, t1,
                                                op0=OP.add, op1=OP.add))


def emit_l0attn(C, W, xsrc, lnsrc, xdst, lndst, abuf, h2buf):
    P = C.P

    ev_sm = []
    for dst, key in [(C.l0a_bq, "bq_fm"), (C.l0a_bo, "bo_fm"), (C.l0a_g1, "g1_fm"),
                     (C.l0a_be1, "be1_fm"), (C.l0a_g2, "g2_fm"), (C.l0a_be2, "be2_fm"),
                     (C.l0a_b1, "b1_fm"), (C.l0a_b2, "b2_fm")]:
        ev_sm.append(P.dma("y", dst[:], W[key]))
    ev_hm = P.dma("y", C.hm16[:], W["hm16"])
    ev_hmT = P.dma("y", C.hm16T[:], W["hm16T"])

    for key, dstt in [("bk_row", C.bkb), ("bv_row", C.bvb)]:
        for h in range(2):
            ev_r = P.dma("y", C.murow[:], W[key][0:1, h * 512:(h + 1) * 512],
                         waits=[P.last("ac"), P.last("mm"), P.last("dv")])
            evb = emit_bcast_row(C, C.murow[0:1, :], C.pb[0][:, :], [ev_r, C.pb_ev[0]])
            ec = P.s(lambda s, h=h, dstt=dstt: s.copy(dstt[:, h * 512:(h + 1) * 512],
                                                      C.pb[0][:, :]),
                     waits=[evb])
            C.pb_ev[0] = ec
    ev_bias = P.last("ac")

    ev_wk = P.dma("g", C.wA[:, 0:8192].rearrange("p (c n) -> p c n", c=8),
                  W["wk"].rearrange("(c p) n -> p c n", p=128), waits=C.wA_b.war())
    ev_wv = P.dma("g", C.wA[:, 8192:16384].rearrange("p (c n) -> p c n", c=8),
                  W["wv"].rearrange("(c p) n -> p c n", p=128))
    C.wA_b.wrote(ev_wv)
    first_kv = True
    ev_kv = None
    prev_kv = None
    for ti in range(NTILES):
        t0 = ti * NT
        lxi = ti % 4
        lxb = C.lx[lxi]
        ev_lx = P.dma("y", fm_sbuf(lxb), fm_dram(lnsrc, t0, NT), waits=C.lx_b[lxi].war())
        C.lx_b[lxi].wrote(ev_lx)
        for s4 in range(4):
            toff = s4 * 128
            for half in range(2):
                bk = half
                evk = None
                for c in range(NCH):
                    evk = P.t(lambda t, c=c, half=half, bk=bk, lxb=lxb, toff=toff: t.matmul(
                        C.pb[bk][:, :], lxb[:, c * NT + toff: c * NT + toff + 128],
                        C.wA[:, c * 1024 + half * 512: c * 1024 + (half + 1) * 512],
                        start=(c == 0), stop=(c == NCH - 1)),
                        waits=[ev_wk, ev_lx, C.pb_ev[bk]])
                ee = _elu_chain(C, C.pb[bk][:, :], C.bkb[:, half * 512:(half + 1) * 512],
                                C.xbf[:, half * 512:(half + 1) * 512], [evk, ev_bias])
                C.pb_ev[bk] = ee
            P.v(lambda v: v.tensor_tensor(C.xbf[:, 1024:2048], C.xbf[:, 0:1024],
                                          C.xbf[:, 0:1024], OP.mult))
            P.v(lambda v: v.tensor_reduce(
                C.ss16[:], C.xbf[:, 1024:2048].rearrange("p (h d) -> p h d", d=64),
                axis=AX.X, op=OP.add))
            e_ln = P.s(lambda s: s.activation(C.rr16[:], C.ss16[:], AF.Ln,
                                              bias=eps_tile(C, 5.4e-19)[:], scale=1.0),
                       waits=[P.last("dv")])
            e_rs = P.s(lambda s: s.activation(C.rr16[:], C.rr16[:], AF.Exp, scale=-0.5),
                       waits=[e_ln])
            ekn = P.v(lambda v: v.tensor_tensor(
                C.gh[:, 0:1024].rearrange("p (h d) -> p h d", d=64),
                C.xbf[:, 0:1024].rearrange("p (h d) -> p h d", d=64),
                C.rr16[:].broadcast_to((128, 16, 64)), OP.mult),
                waits=[e_rs, prev_kv])
            for half in range(2):
                bk = 2 + half
                evv = None
                for c in range(NCH):
                    evv = P.t(lambda t, c=c, half=half, bk=bk, lxb=lxb, toff=toff: t.matmul(
                        C.pb[bk][:, :], lxb[:, c * NT + toff: c * NT + toff + 128],
                        C.wA[:, 8192 + c * 1024 + half * 512: 8192 + c * 1024 + (half + 1) * 512],
                        start=(c == 0), stop=(c == NCH - 1)),
                        waits=[ev_wv, ev_lx, C.pb_ev[bk]])
                ev_e = P.v(lambda v, half=half, bk=bk: v.tensor_tensor(
                    C.gh[:, 1024 + half * 512: 1024 + (half + 1) * 512],
                    C.pb[bk][:, :], C.bvb[:, half * 512:(half + 1) * 512], OP.add),
                    waits=[evv, ev_bias, prev_kv])
                C.pb_ev[bk] = ev_e
            ev_gh = P.last("dv")
            for p in range(8):
                bk = 4 + p // 4
                ev_kv = P.t(lambda t, p=p, bk=bk, fk=first_kv: t.matmul(
                    C.pb[bk][:, (p % 4) * 128:(p % 4) * 128 + 128],
                    C.gh[:, p * 128:(p + 1) * 128],
                    C.gh[:, 1024 + p * 128: 1024 + (p + 1) * 128],
                    start=(fk and p % 4 == 0), stop=(ti == NTILES - 1 and s4 == 3),
                    skip_group_check=True),
                    waits=[ekn, ev_gh] + ([C.pb_ev[4], C.pb_ev[5]] if first_kv else []))
            first_kv = False
            prev_kv = ev_kv
        C.lx_b[lxi].read(ev_kv)
    C.wA_b.read(P.last("mm"))
    for p in range(8):
        bk = 4 + p // 4
        P.s(lambda s, p=p, bk=bk: s.copy(C.kvs[0:64, p * 64:(p + 1) * 64],
                                         C.pb[bk][0:64, (p % 4) * 128:(p % 4) * 128 + 64]),
            waits=[ev_kv] + (C.kvs_b.war() if p == 0 else []))
        P.s(lambda s, p=p, bk=bk: s.copy(C.kvs[64:128, p * 64:(p + 1) * 64],
                                         C.pb[bk][64:128, (p % 4) * 128 + 64:(p % 4) * 128 + 128]))
    ev_kvs = P.last("ac")
    C.kvs_b.wrote(ev_kvs)
    C.pb_ev[4] = ev_kvs
    C.pb_ev[5] = ev_kvs

    ev_wq = P.dma("g", C.wB[:, 0:8192].rearrange("p (c n) -> p c n", c=8),
                  W["wq"].rearrange("(c p) n -> p c n", p=128), waits=C.wB_b.war())
    ev_wo = P.dma("g", C.wB[:, 8192:16384].rearrange("p (c n) -> p c n", c=8),
                  W["wo"].rearrange("(c p) n -> p c n", p=128))
    C.wB_b.wrote(ev_wo)
    for ti in range(NTILES):
        t0 = ti * NT
        lxi = ti % 4
        lxb = C.lx[lxi]
        ev_lx = P.dma("y", fm_sbuf(lxb), fm_dram(lnsrc, t0, NT), waits=C.lx_b[lxi].war())
        C.lx_b[lxi].wrote(ev_lx)
        for f in range(NCH):
            bk = f % 2
            evq = None
            for c in range(NCH):
                evq = P.t(lambda t, c=c, f=f, bk=bk, lxb=lxb: t.matmul(
                    C.pb[bk][:, :],
                    C.wB[:, c * 1024 + f * 128: c * 1024 + f * 128 + 128],
                    lxb[:, c * NT:(c + 1) * NT],
                    start=(c == 0), stop=(c == NCH - 1)),
                    waits=[ev_wq, ev_lx, C.pb_ev[bk]])
            t0b = C.tc[:, 0:512]
            t1b = C.tc[:, 512:1024]
            P.v(lambda v, bk=bk: v.tensor_copy(t0b, C.pb[bk][:, :]),
                waits=[evq] + ev_sm)
            e_cp = P.v(lambda v, f=f: v.tensor_scalar(
                t0b, t0b, C.l0a_bq[:, f:f + 1], None, op0=OP.add))
            C.pb_ev[bk] = e_cp
            P.v(lambda v: v.tensor_scalar(t1b, t0b, 0.0, None, op0=OP.min))
            e_exp = P.s(lambda s: s.activation(t1b, t1b, AF.Exp, scale=1.0),
                        waits=[P.last("dv")])
            P.v(lambda v: v.tensor_scalar(t0b, t0b, 0.0, None, op0=OP.max), waits=[e_exp])
            P.v(lambda v, f=f: v.scalar_tensor_tensor(C.xbf[:, f * NT:(f + 1) * NT],
                                                      t0b, -1.0, t1b, op0=OP.add, op1=OP.add),
                waits=[getattr(C, "pend_sqt", None)] if f == 0 else [])
        ev_eq = P.last("dv")
        P.v(lambda v: v.tensor_tensor(C.tc[:], C.xbf[:], C.xbf[:], OP.mult),
            waits=[ev_eq])
        ev_sq = P.last("dv")
        ev_ss = None
        for c in range(NCH):
            ev_ss = P.t(lambda t, c=c: t.matmul(
                C.pb[2][0:16, :], C.hm16[:, c * 16:(c + 1) * 16],
                C.tc[:, c * NT:(c + 1) * NT],
                start=(c == 0), stop=(c == NCH - 1)),
                waits=[ev_sq, ev_hm, C.pb_ev[2]])
        e_ln = P.s(lambda s: s.activation(C.rst16f[:, 0:NT], C.pb[2][0:16, :], AF.Ln,
                                          bias=eps_tile(C, 5.4e-19)[0:16, :], scale=1.0),
                   waits=[ev_ss])
        C.pb_ev[2] = e_ln
        e_rs = P.s(lambda s: s.activation(C.rst16[:, 0:NT], C.rst16f[:, 0:NT], AF.Exp,
                                          bias=eps_tile(C, -3.0 * float(np.log(2.0)))[0:16, :],
                                          scale=-0.5), waits=[e_ln])
        for c in range(NCH):
            evb = P.t(lambda t, c=c: t.matmul(
                C.pb[3][:, :], C.hm16T[0:16, c * 128:(c + 1) * 128],
                C.rst16[0:16, 0:NT], start=True, stop=True),
                waits=[e_rs, ev_hmT, C.pb_ev[3]])
            eq = P.v(lambda v, c=c: v.tensor_tensor(C.gh[:, c * NT:(c + 1) * NT],
                                                    C.xbf[:, c * NT:(c + 1) * NT],
                                                    C.pb[3][:, :], OP.mult),
                     waits=[evb, getattr(C, "last_ctx", None) if c == 0 else None])
            C.pb_ev[3] = eq
        ev_qn = P.last("dv")
        for c in range(NCH):
            bk = c % 2
            ec1 = P.t(lambda t, c=c, bk=bk: t.matmul(
                C.pb[bk][0:64, :], C.kvs[0:64, c * 64:(c + 1) * 64],
                C.gh[0:64, c * NT:(c + 1) * NT], start=True, stop=True),
                waits=[ev_qn, ev_kvs, C.pb_ev[bk]])
            C.last_ctx = ec1
            ec2 = P.t(lambda t, c=c, bk=bk: t.matmul(
                C.pb[bk][64:128, :], C.kvs[64:128, c * 64:(c + 1) * 64],
                C.gh[64:128, c * NT:(c + 1) * NT], start=True, stop=True))
            ee = P.s(lambda s, c=c, bk=bk: s.copy(C.tc[:, c * NT:(c + 1) * NT],
                                                  C.pb[bk][:, :]), waits=[ec2])
            C.pb_ev[bk] = ee
        ev_ctx = P.last("ac")
        for f in range(NCH):
            bk = f % 2
            evo = None
            for c in range(NCH):
                evo = P.t(lambda t, c=c, f=f, bk=bk: t.matmul(
                    C.pb[bk][:, :],
                    C.wB[:, 8192 + c * 1024 + f * 128: 8192 + c * 1024 + f * 128 + 128],
                    C.tc[:, c * NT:(c + 1) * NT],
                    start=(c == 0), stop=(c == NCH - 1)),
                    waits=[ev_wo, ev_ctx, C.pb_ev[bk]])
            ez = P.v(lambda v, f=f, bk=bk, lxb=lxb: v.scalar_tensor_tensor(
                C.xt[:, f * NT:(f + 1) * NT], C.pb[bk][:, :], C.l0a_bo[:, f:f + 1],
                lxb[:, f * NT:(f + 1) * NT], op0=OP.add, op1=OP.add),
                waits=[evo] + C.xt_b.war() if f == 0 else [evo])
            C.pb_ev[bk] = ez
        C.lx_b[lxi].read(P.last("dv"))
        C.xt_b.wrote(P.last("dv"))
        emit_ln_affine(C, ti, abuf, C.l0a_g1, C.l0a_be1, eps=1e-12)
    C.wB_b.read(P.last("mm"))
    C.kvs_b.read(P.last("mm"))

    ev_w1a = P.dma("g", C.wA[:].rearrange("p (c n) -> p c n", c=8),
                   W["w1"][:, 0:2048].rearrange("(c p) n -> p c n", p=128),
                   waits=C.wA_b.war())
    C.wA_b.wrote(ev_w1a)
    ev_w1b = P.dma("g", C.wB[:].rearrange("p (c n) -> p c n", c=8),
                   W["w1"][:, 2048:4096].rearrange("(c p) n -> p c n", p=128),
                   waits=C.wB_b.war())
    C.wB_b.wrote(ev_w1b)
    gh_ev = [None] * 16
    for ti in range(NTILES):
        t0 = ti * NT
        lxi = ti % 4
        lxb = C.lx[lxi]
        ev_a = P.dma("y", fm_sbuf(lxb), fm_dram(abuf, t0, NT), waits=C.lx_b[lxi].war())
        C.lx_b[lxi].wrote(ev_a)
        lastj = None
        for j in range(32):
            wsrc = C.wA if j < 16 else C.wB
            jj = j % 16
            bk = j % 2
            evh = None
            for c in range(NCH):
                evh = P.t(lambda t, c=c, jj=jj, bk=bk, wsrc=wsrc, lxb=lxb: t.matmul(
                    C.pb[bk][:, :], wsrc[:, c * 2048 + jj * 128: c * 2048 + jj * 128 + 128],
                    lxb[:, c * NT:(c + 1) * NT],
                    start=(c == 0), stop=(c == NCH - 1)),
                    waits=[ev_w1a if j < 16 else ev_w1b, ev_a, C.pb_ev[bk]])
            slot = j % 16
            eg = P.s(lambda s, j=j, bk=bk, slot=slot: s.activation(
                C.gh[:, slot * NT:(slot + 1) * NT], C.pb[bk][:, :], AF.Gelu,
                bias=C.l0a_b1[:, j:j + 1], scale=1.0),
                waits=[evh, gh_ev[slot]])
            C.pb_ev[bk] = eg
            ed = P.dma("y", h2buf[j * 128:(j + 1) * 128, t0:t0 + NT],
                       C.gh[:, slot * NT:(slot + 1) * NT], waits=[eg])
            gh_ev[slot] = ed
            lastj = evh
        C.lx_b[lxi].read(lastj)
    C.wA_b.read(P.last("mm"))
    C.wB_b.read(P.last("mm"))

    ev_w2a = P.dma("g", C.wA[:].rearrange("p (c n) -> p c n", c=16),
                   W["w2"][0:2048].rearrange("(c p) n -> p c n", p=128),
                   waits=C.wA_b.war())
    C.wA_b.wrote(ev_w2a)
    ev_w2b = P.dma("g", C.wB[:].rearrange("p (c n) -> p c n", c=16),
                   W["w2"][2048:4096].rearrange("(c p) n -> p c n", p=128),
                   waits=C.wB_b.war())
    C.wB_b.wrote(ev_w2b)
    for ti in range(NTILES):
        t0 = ti * NT
        lxi = ti % 4
        lxb = C.lx[lxi]
        ev_a = P.dma("y", fm_sbuf(lxb), fm_dram(abuf, t0, NT), waits=C.lx_b[lxi].war())
        C.lx_b[lxi].wrote(ev_a)
        ev_h2 = P.dma("y", C.acc[:].rearrange("p (c n) -> p c n", c=32),
                      h2buf[:, t0:t0 + NT].rearrange("(c p) n -> p c n", p=128),
                      waits=C.acc_b.war())
        C.acc_b.wrote(ev_h2)
        ev_x = P.dma("y", fm_sbuf(C.xt), fm_dram(xsrc, t0, NT), waits=C.xt_b.war())
        C.xt_b.wrote(ev_x)
        for f in range(NCH):
            bk = f % 2
            evz = None
            for j in range(32):
                wsrc = C.wA if j < 16 else C.wB
                jj = j % 16
                evz = P.t(lambda t, j=j, jj=jj, f=f, bk=bk, wsrc=wsrc: t.matmul(
                    C.pb[bk][:, :], wsrc[:, jj * 1024 + f * 128: jj * 1024 + f * 128 + 128],
                    C.acc[:, j * NT:(j + 1) * NT],
                    start=(j == 0), stop=(j == 31)),
                    waits=[ev_w2a if j < 16 else ev_w2b, ev_h2, C.pb_ev[bk]])
            ez = P.v(lambda v, f=f, bk=bk, lxb=lxb: v.scalar_tensor_tensor(
                C.xbf[:, f * NT:(f + 1) * NT], C.pb[bk][:, :], C.l0a_b2[:, f:f + 1],
                lxb[:, f * NT:(f + 1) * NT], op0=OP.add, op1=OP.add),
                waits=[evz, C.lx_b[lxi].wr])
            C.pb_ev[bk] = ez
            C.acc_b.read(evz)
        C.lx_b[lxi].read(P.last("dv"))
        ev_z2 = P.last("dv")
        ev_mu = None
        for c in range(NCH):
            ev_mu = P.t(lambda t, c=c: t.matmul(C.pb[4][0:1, :], C.ones128[:],
                                                C.xbf[:, c * NT:(c + 1) * NT],
                                                start=(c == 0), stop=(c == NCH - 1)),
                        waits=[ev_z2, C.pb_ev[4]])
        ev_mur = P.s(lambda s: s.activation(C.murow[:], C.pb[4][0:1, :], AF.Copy,
                                            scale=1.0 / D), waits=[ev_mu])
        ev_mub = emit_bcast_row(C, C.murow[0:1, :], C.pb[5][:, :], [ev_mur, C.pb_ev[5]])
        ev_tc = None
        for c in range(NCH):
            ev_tc = P.v(lambda v, c=c: v.tensor_tensor(C.tc[:, c * NT:(c + 1) * NT],
                                                       C.xbf[:, c * NT:(c + 1) * NT],
                                                       C.pb[5][:, :], OP.subtract),
                        waits=[ev_mub])
        ev_sq = None
        for c in range(NCH):
            ev_sq = P.v(lambda v, c=c: v.tensor_tensor(C.sqt[:, c * NT:(c + 1) * NT],
                                                       C.tc[:, c * NT:(c + 1) * NT],
                                                       C.tc[:, c * NT:(c + 1) * NT], OP.mult),
                        waits=[ev_tc])
        ev_var = None
        for c in range(NCH):
            ev_var = P.t(lambda t, c=c: t.matmul(C.pb[4][0:1, :], C.ones128[:],
                                                 C.sqt[:, c * NT:(c + 1) * NT],
                                                 start=(c == 0), stop=(c == NCH - 1)),
                         waits=[ev_sq, ev_mur])
        ev_ln = P.s(lambda s: s.activation(C.lnrow[:], C.pb[4][0:1, :], AF.Ln,
                                           bias=eps_tile(C, 1e-12)[0:1, :], scale=1.0 / D),
                    waits=[ev_var])
        C.pb_ev[4] = ev_ln
        ev_rstd = P.s(lambda s: s.activation(C.rstdrow[:], C.lnrow[:], AF.Exp, scale=-0.5),
                      waits=[ev_ln])
        ev_rb = emit_bcast_row(C, C.rstdrow[0:1, :], C.pb[5][:, :], [ev_rstd, ev_tc])
        ev_new = None
        for c in range(NCH):
            P.v(lambda v, c=c: v.tensor_tensor(C.sqt[:, c * NT:(c + 1) * NT],
                                               C.tc[:, c * NT:(c + 1) * NT],
                                               C.pb[5][:, :], OP.mult),
                waits=[ev_rb])
            P.v(lambda v, c=c: v.tensor_scalar(C.sqt[:, c * NT:(c + 1) * NT],
                                               C.sqt[:, c * NT:(c + 1) * NT],
                                               C.l0a_g2[:, c:c + 1], C.l0a_be2[:, c:c + 1],
                                               op0=OP.mult, op1=OP.add))
            ev_new = P.v(lambda v, c=c: v.tensor_tensor(C.xt[:, c * NT:(c + 1) * NT],
                                                        C.xt[:, c * NT:(c + 1) * NT],
                                                        C.sqt[:, c * NT:(c + 1) * NT], OP.add),
                         waits=[ev_x])
        C.pb_ev[5] = ev_new
        C.xt_b.wrote(ev_new)
        emit_tail_tile(C, ti, ev_new, xdst, lndst, store_x=True)
    C.wA_b.read(P.last("mm"))
    C.wB_b.read(P.last("mm"))


def emit_ln_affine(C, ti, dst, g_tile, b_tile, eps):
    P = C.P
    t0 = ti * NT
    src_ev = C.xt_b.wr
    ev_cast = P.v(lambda v: v.tensor_copy(C.xbf[:], C.xt[:]), waits=[src_ev])
    C.xt_b.read(ev_cast)
    ev_mu = None
    for c in range(NCH):
        ev_mu = P.t(lambda t, c=c: t.matmul(C.pb[4][0:1, :], C.ones128[:],
                                            C.xbf[:, c * NT:(c + 1) * NT],
                                            start=(c == 0), stop=(c == NCH - 1)),
                    waits=[ev_cast, C.pb_ev[4]])
    ev_mur = P.s(lambda s: s.activation(C.murow[:], C.pb[4][0:1, :], AF.Copy, scale=1.0 / D),
                 waits=[ev_mu])
    ev_mub = emit_bcast_row(C, C.murow[0:1, :], C.pb[5][:, :], [ev_mur, C.pb_ev[5]])
    ev_tc = None
    for c in range(NCH):
        ev_tc = P.v(lambda v, c=c: v.tensor_tensor(C.tc[:, c * NT:(c + 1) * NT],
                                                   C.xbf[:, c * NT:(c + 1) * NT],
                                                   C.pb[5][:, :], OP.subtract),
                    waits=[ev_mub])
    ev_sq = None
    for c in range(NCH):
        ev_sq = P.v(lambda v, c=c: v.tensor_tensor(C.sqt[:, c * NT:(c + 1) * NT],
                                                   C.tc[:, c * NT:(c + 1) * NT],
                                                   C.tc[:, c * NT:(c + 1) * NT], OP.mult),
                    waits=[ev_tc])
    ev_var = None
    for c in range(NCH):
        ev_var = P.t(lambda t, c=c: t.matmul(C.pb[4][0:1, :], C.ones128[:],
                                             C.sqt[:, c * NT:(c + 1) * NT],
                                             start=(c == 0), stop=(c == NCH - 1)),
                     waits=[ev_sq, ev_mur])
    ev_ln = P.s(lambda s: s.activation(C.lnrow[:], C.pb[4][0:1, :], AF.Ln,
                                       bias=eps_tile(C, eps)[0:1, :], scale=1.0 / D),
                waits=[ev_var])
    C.pb_ev[4] = ev_ln
    ev_rstd = P.s(lambda s: s.activation(C.rstdrow[:], C.lnrow[:], AF.Exp, scale=-0.5),
                  waits=[ev_ln])
    ev_rb = emit_bcast_row(C, C.rstdrow[0:1, :], C.pb[5][:, :], [ev_rstd, ev_tc])
    ev_o = None
    for c in range(NCH):
        P.v(lambda v, c=c: v.tensor_tensor(C.sqt[:, c * NT:(c + 1) * NT],
                                           C.tc[:, c * NT:(c + 1) * NT],
                                           C.pb[5][:, :], OP.mult),
            waits=[ev_rb])
        ev_o = P.v(lambda v, c=c: v.tensor_scalar(C.sqt[:, c * NT:(c + 1) * NT],
                                                  C.sqt[:, c * NT:(c + 1) * NT],
                                                  g_tile[:, c:c + 1], b_tile[:, c:c + 1],
                                                  op0=OP.mult, op1=OP.add))
    C.pb_ev[5] = ev_o
    ev_st = P.dma("y", fm_dram(dst, t0, NT), fm_sbuf(C.sqt), waits=[ev_o])
    C.pend_sqt = ev_st
    return ev_st



def _prep_host(inputs):
    H = {}
    H["c_ones1b"] = np.ones((1, 128), BF)
    x = inputs["L0_attn_x"]
    H["xT"] = [np.ascontiguousarray(x[b].T) for b in range(8)]
    H["l0f_gate_w"] = np.ascontiguousarray(inputs["L0_ffn_gate_w"])
    H["l0f_gate_b"] = np.ascontiguousarray(
        np.broadcast_to(inputs["L0_ffn_gate_b"].reshape(1, 8), (128, 8)))
    H["l0f_w1"] = np.ascontiguousarray(inputs["L0_ffn_w1"])
    H["l0f_b1"] = np.ascontiguousarray(
        inputs["L0_ffn_b1"].reshape(8, 16, 128).transpose(0, 2, 1))
    H["l0f_w2"] = np.ascontiguousarray(inputs["L0_ffn_w2"])
    H["l0f_b2"] = np.ascontiguousarray(inputs["L0_ffn_b2"])
    H["l1f_gate_w"] = np.ascontiguousarray(inputs["L1_ffn_Wg"].T)
    H["l1f_gate_b"] = np.zeros((128, 8), np.float32)
    w1t = inputs["L1_ffn_W1"].transpose(0, 2, 1)
    cols = []
    for q in range(4):
        cols.extend(range(q * 512, (q + 1) * 512))
        cols.extend(range(2048 + q * 512, 2048 + (q + 1) * 512))
    cols = np.array(cols)
    H["l1f_w1q"] = np.ascontiguousarray(w1t[:, :, cols])
    b1q = inputs["L1_ffn_b1"][:, cols]
    H["l1f_b1q"] = np.ascontiguousarray(b1q.reshape(8, 32, 128).transpose(0, 2, 1))
    H["l1f_woT"] = np.ascontiguousarray(inputs["L1_ffn_Wo"].T)
    H["l1f_bo_fm"] = np.ascontiguousarray(inputs["L1_ffn_bo"].reshape(8, 128).T)
    H["l1a_wq"] = np.ascontiguousarray(inputs["L1_attn_Wq"])
    H["l1a_wk"] = np.ascontiguousarray(inputs["L1_attn_Wk"])
    H["l1a_wv"] = np.ascontiguousarray(inputs["L1_attn_Wv"])
    H["l1a_wb"] = np.ascontiguousarray(inputs["L1_attn_Wb"])
    H["l1a_wo"] = np.ascontiguousarray(inputs["L1_attn_Wo"])

    def conv_layout(w):
        K_ = w.shape[1]
        r = w.reshape(8, 128, K_).transpose(1, 2, 0).reshape(128, K_ * 8)
        return np.ascontiguousarray(r.astype(BF))
    H["l1a_convq"] = conv_layout(inputs["L1_attn_conv_q_w"])
    H["l1a_convk"] = conv_layout(inputs["L1_attn_conv_k_w"])
    H["l1a_convv"] = conv_layout(inputs["L1_attn_conv_v_w"])
    H["l1a_localw"] = conv_layout(inputs["L1_attn_local_w"])
    H["l1a_midw"] = conv_layout(inputs["L1_attn_mid_w"])
    hm4 = np.zeros((128, 32), BF)
    for c in range(8):
        hm4[:, c * 4 + c // 2] = 1
    H["l1a_hm4"] = hm4
    oh4 = np.zeros((4, 512), BF)
    for h in range(4):
        oh4[h, h * 128:(h + 1) * 128] = 1
    H["l1a_oh4T"] = oh4
    H["l1a_maskL"] = (-np.tril(np.ones((128, 128), np.float32), -1)).astype(BF)
    H["l1a_maskU"] = (-np.triu(np.ones((128, 128), np.float32), 1)).astype(BF)
    H["l1a_maskT"] = np.triu(np.ones((128, 128), np.float32), 0).astype(BF)
    H["l1a_rw1"] = np.ascontiguousarray(inputs["L1_attn_r_w1"])
    rb1 = np.zeros(17 * 128, np.float32)
    rb1[0:2160] = inputs["L1_attn_r_b1"]
    H["l1a_rb1"] = np.ascontiguousarray(rb1.reshape(17, 128).T)
    H["l1a_rw2"] = np.ascontiguousarray(inputs["L1_attn_r_w2"])
    H["l1a_rb2"] = inputs["L1_attn_r_b2"].reshape(1, 16).copy()
    H["l1a_mixw"] = np.ascontiguousarray(inputs["L1_attn_mix_w"])
    H["l1a_onormw"] = inputs["L1_attn_onorm_w"].reshape(1, 256).copy()
    H["l0a_wq"] = np.ascontiguousarray(inputs["L0_attn_Wq"])
    H["l0a_wk"] = np.ascontiguousarray(inputs["L0_attn_Wk"])
    H["l0a_wv"] = np.ascontiguousarray(inputs["L0_attn_Wv"])
    H["l0a_wo"] = np.ascontiguousarray(inputs["L0_attn_Wo"])
    H["l0a_w1"] = np.ascontiguousarray(inputs["L0_attn_W1"])
    H["l0a_w2"] = np.ascontiguousarray(inputs["L0_attn_W2"])
    H["l0a_bk_row"] = inputs["L0_attn_bk"].reshape(1, 1024).copy()
    H["l0a_bv_row"] = inputs["L0_attn_bv"].reshape(1, 1024).copy()
    for nm, key in [("bq_fm", "L0_attn_bq"), ("bo_fm", "L0_attn_bo"),
                    ("g1_fm", "L0_attn_g1"), ("be1_fm", "L0_attn_be1"),
                    ("g2_fm", "L0_attn_g2"), ("be2_fm", "L0_attn_be2")]:
        H["l0a_" + nm] = np.ascontiguousarray(inputs[key].reshape(8, 128).T)
    H["l0a_b1_fm"] = np.ascontiguousarray(inputs["L0_attn_b1"].reshape(32, 128).T)
    H["l0a_b2_fm"] = np.ascontiguousarray(inputs["L0_attn_b2"].reshape(8, 128).T)
    hm16 = np.zeros((128, 8 * 16), BF)
    hm16T = np.zeros((16, 8 * 128), BF)
    for c in range(8):
        hm16[0:64, c * 16 + 2 * c] = 1
        hm16[64:128, c * 16 + 2 * c + 1] = 1
        hm16T[2 * c, c * 128: c * 128 + 64] = 1
        hm16T[2 * c + 1, c * 128 + 64: c * 128 + 128] = 1
    H["l0a_hm16"] = hm16
    H["l0a_hm16T"] = hm16T
    H["c_ones128"] = np.ones((128, 1), BF)
    H["c_ident"] = np.eye(128, dtype=BF)
    oh = np.zeros((8, 8 * 128), BF)
    for e in range(8):
        oh[e, e * 128:(e + 1) * 128] = 1
    H["c_oh8T"] = oh
    return H


def declare_consts(C, nc):
    names = [("ones128", C.ones128, "c_ones128", [128, 1], BF16),
             ("ones1_b", C.ones1_b, "c_ones1b", [1, 128], BF16),
             ("ident", C.ident, "c_ident", [128, 128], BF16),
             ("oh8T", C.oh8T, "c_oh8T", [8, 8 * 128], BF16)]
    evs = []
    for nm, tile, hkey, shape, dt in names:
        ap = nc.dram_tensor("in_" + hkey, shape, dt, kind="ExternalInput").ap()
        evs.append(C.P.dma("y", tile[:], ap))
    C.P.barrier()
    return [("in_" + k) for _, _, k, _, _ in names]


def build_partial(passes, hash_only=False):
    es = ExitStack()
    nc = bass.Bass("TRN2", target_bir_lowering=False)
    P = PB()
    C = make_ctx(es, nc, P)

    di = lambda name, shape, dt: nc.dram_tensor(name, shape, dt, kind="ExternalInput").ap()
    xT = di("in_xT", [D, L], F32)
    OUT = nc.dram_tensor("out_x", [D, L], F32, kind="ExternalOutput").ap()
    LNOUT = nc.dram_tensor("out_ln", [D, L], BF16, kind="ExternalOutput").ap()
    xbuf = nc.dram_tensor("xbuf", [D, L], F32, kind="Internal").ap()
    lnbuf = nc.dram_tensor("lnbuf", [D, L], BF16, kind="Internal").ap()
    wTbuf = nc.dram_tensor("wTbuf", [8, L], BF16, kind="Internal").ap()
    csbuf = nc.dram_tensor("csbuf", [1, L], F32, kind="Internal").ap()

    abuf = nc.dram_tensor("abuf", [D, L], BF16, kind="Internal").ap()
    h2buf = nc.dram_tensor("h2buf", [4096, L], BF16, kind="Internal").ap()
    a1kind = "ExternalOutput" if SNAPSHOT else "Internal"
    A1B = dict(dbg3=nc.dram_tensor("dbg3", [L, 16], F32, kind=a1kind).ap(),
               dbg4=nc.dram_tensor("dbg4", [L, D], BF16, kind=a1kind).ap(),
               qb=nc.dram_tensor("qbuf", [D, L], BF16, kind=a1kind).ap(),
               kb=nc.dram_tensor("kbuf", [D, L], BF16, kind=a1kind).ap(),
               vb=nc.dram_tensor("vbuf", [D, L], BF16, kind=a1kind).ap(),
               lob=nc.dram_tensor("lobuf", [D, L], BF16, kind=a1kind).ap(),
               midb=nc.dram_tensor("midbuf", [D, L], BF16, kind=a1kind).ap(),
               dltb=nc.dram_tensor("dltbuf", [L, D], BF16, kind=a1kind).ap(),
               h1b=nc.dram_tensor("h1buf", [2160, L], BF16, kind=a1kind).ap(),
               rows=nc.dram_tensor("rowbuf", [32, L], F32, kind=a1kind).ap())
    W1A = dict(wq=di("in_l1a_wq", [D, D], F32), wk=di("in_l1a_wk", [D, D], F32),
               wv=di("in_l1a_wv", [D, D], F32), wb=di("in_l1a_wb", [D, 4], F32),
               wo=di("in_l1a_wo", [D, D], F32),
               convq=di("in_l1a_convq", [128, 32], BF16),
               convk=di("in_l1a_convk", [128, 32], BF16),
               convv=di("in_l1a_convv", [128, 32], BF16),
               localw=di("in_l1a_localw", [128, 56], BF16),
               midw=di("in_l1a_midw", [128, 200], BF16),
               hm4=di("in_l1a_hm4", [128, 32], BF16),
               oh4T=di("in_l1a_oh4T", [4, 512], BF16),
               maskL=di("in_l1a_maskL", [128, 128], BF16),
               maskU=di("in_l1a_maskU", [128, 128], BF16),
               maskT=di("in_l1a_maskT", [128, 128], BF16),
               rw1=di("in_l1a_rw1", [1080, 2160], F32),
               rb1=di("in_l1a_rb1", [128, 17], F32),
               rw2=di("in_l1a_rw2", [2160, 16], F32),
               rb2=di("in_l1a_rb2", [1, 16], F32),
               mixw=di("in_l1a_mixw", [4, 256], F32),
               onormw=di("in_l1a_onormw", [1, 256], F32))
    W0A = dict(wq=di("in_l0a_wq", [D, D], F32), wk=di("in_l0a_wk", [D, D], F32),
               wv=di("in_l0a_wv", [D, D], F32), wo=di("in_l0a_wo", [D, D], F32),
               w1=di("in_l0a_w1", [D, 4096], F32), w2=di("in_l0a_w2", [4096, D], F32),
               bk_row=di("in_l0a_bk_row", [1, D], F32),
               bv_row=di("in_l0a_bv_row", [1, D], F32),
               bq_fm=di("in_l0a_bq_fm", [128, 8], F32),
               bo_fm=di("in_l0a_bo_fm", [128, 8], F32),
               g1_fm=di("in_l0a_g1_fm", [128, 8], F32),
               be1_fm=di("in_l0a_be1_fm", [128, 8], F32),
               g2_fm=di("in_l0a_g2_fm", [128, 8], F32),
               be2_fm=di("in_l0a_be2_fm", [128, 8], F32),
               b1_fm=di("in_l0a_b1_fm", [128, 32], F32),
               b2_fm=di("in_l0a_b2_fm", [128, 8], F32),
               hm16=di("in_l0a_hm16", [128, 128], BF16),
               hm16T=di("in_l0a_hm16T", [16, 1024], BF16))
    W1F = dict(gate_w=di("in_l1f_gate_w", [D, 8], F32),
               gate_b=di("in_l1f_gate_b", [128, 8], F32),
               w1q=di("in_l1f_w1q", [8, D, 4096], F32),
               b1q=di("in_l1f_b1q", [8, 128, 32], F32),
               woT=di("in_l1f_woT", [2048, D], F32),
               bo_fm=di("in_l1f_bo_fm", [128, 8], F32))
    W0F = dict(gate_w=di("in_l0f_gate_w", [D, 8], F32),
               gate_b=di("in_l0f_gate_b", [128, 8], F32),
               w1=di("in_l0f_w1", [8, D, 2048], F32),
               b1=di("in_l0f_b1", [8, 128, 16], F32),
               w2=di("in_l0f_w2", [8, 2048, D], F32),
               b2=di("in_l0f_b2", [8, D], F32))

    declare_consts(C, nc)
    import numpy as _np
    for v in (1e-5, 1e-12, 1e-9, 1e-10, 5.4e-19, -3.0 * float(_np.log(2.0))):
        eps_tile(C, v)
    emit_const_init(C)
    C.const_frozen = True
    P.barrier()

    def snap(tag, xs):
        if SNAPSHOT:
            sp = nc.dram_tensor("snap_" + tag, [D, L], F32,
                                kind="ExternalOutput").ap()
            P.dma("y", sp, xs)

    xsrc = xT
    if "init" in passes:
        emit_init_pass(C, xsrc, lnbuf)
        P.barrier()
    if "f0a" in passes:
        dst = OUT if passes[-1] == "f0a" else xbuf
        emit_l0ffn(C, W0F, xsrc, lnbuf, dst, LNOUT if passes[-1] == "f0a" else lnbuf, wTbuf)
        xsrc = dst
        P.barrier()
        snap("f0a", xsrc)
    if "a0" in passes:
        dst = OUT if passes[-1] == "a0" else xbuf
        emit_l0attn(C, W0A, xsrc, lnbuf, dst, LNOUT if passes[-1] == "a0" else lnbuf,
                    abuf, h2buf)
        xsrc = dst
        P.barrier()
        snap("a0", xsrc)
    if "f0b" in passes:
        dst = OUT if passes[-1] == "f0b" else xbuf
        emit_l0ffn(C, W0F, xsrc, lnbuf, dst, LNOUT if passes[-1] == "f0b" else lnbuf, wTbuf)
        xsrc = dst
        P.barrier()
        snap("f0b", xsrc)
    if "f1a" in passes:
        dst = OUT if passes[-1] == "f1a" else xbuf
        emit_l1ffn(C, W1F, xsrc, lnbuf, dst, wTbuf, csbuf)
        xsrc = dst
        P.barrier()
        snap("f1a", xsrc)
    if "lnpass" in passes:
        emit_init_pass(C, xsrc, LNOUT if passes[-1] == "lnpass" else lnbuf)
        P.barrier()
    if "a1" in passes:
        dst = OUT if passes[-1] == "a1" else xbuf
        emit_l1attn(C, W1A, xsrc, lnbuf, dst, LNOUT if passes[-1] == "a1" else lnbuf,
                    A1B)
        xsrc = dst
        P.barrier()
        snap("a1", xsrc)
    if "f1b" in passes:
        dst = OUT if passes[-1] == "f1b" else xbuf
        emit_l1ffn(C, W1F, xsrc, lnbuf, dst, wTbuf, csbuf)
        xsrc = dst
        P.barrier()

    P._emit("sync", None, P.queue_events("qy") + P.queue_events("qs")
            + P.queue_events("qg")
            + [P.last("dv"), P.last("ac"), P.last("mm"), P.last("po")],
            None, 1)

    global LAST_PB
    LAST_PB = P
    sems = {}
    for s in ("mm", "dv", "ac", "po", "sy"):
        sems[s] = es.enter_context(nc.semaphore(s))
    for base in ("qy", "qs", "qg"):
        for j in range(NPOOL):
            nm = f"{base}{j}"
            if nm in P.cnt:
                sems[nm] = es.enter_context(nc.semaphore(nm))
    block = es.enter_context(nc.Block())
    P.finalize(block, sems)
    return nc, es


def in_map_for_core(H, b, extra=()):
    m = {"in_xT": H["xT"][b]}
    for k in ["l0f_gate_w", "l0f_gate_b", "l0f_w1", "l0f_b1", "l0f_w2", "l0f_b2",
              "l1f_gate_w", "l1f_gate_b", "l1f_w1q", "l1f_b1q", "l1f_woT", "l1f_bo_fm",
              "l1a_wq", "l1a_wk", "l1a_wv", "l1a_wb", "l1a_wo", "l1a_convq",
              "l1a_convk", "l1a_convv", "l1a_localw", "l1a_midw", "l1a_hm4",
              "l1a_oh4T", "l1a_maskL", "l1a_maskU", "l1a_maskT", "l1a_rw1",
              "l1a_rb1", "l1a_rw2", "l1a_rb2", "l1a_mixw", "l1a_onormw",
              "l0a_wq", "l0a_wk", "l0a_wv", "l0a_wo", "l0a_w1", "l0a_w2",
              "l0a_bk_row", "l0a_bv_row", "l0a_bq_fm", "l0a_bo_fm", "l0a_g1_fm",
              "l0a_be1_fm", "l0a_g2_fm", "l0a_be2_fm", "l0a_b1_fm", "l0a_b2_fm",
              "l0a_hm16", "l0a_hm16T",
              "c_ones128", "c_ones1b", "c_ident", "c_oh8T"]:
        m["in_" + k] = H[k]
    return m


SNAPSHOT = False

FULL_PASSES = ["init", "f0a", "a0", "f0b", "f1a", "lnpass", "a1", "f1b"]


def kernel(**inputs):
    H = _prep_host(inputs)
    nc, es = build_partial(FULL_PASSES)
    with es:
        in_maps = [in_map_for_core(H, b) for b in range(8)]
        res = run_bass_kernel_spmd(nc, in_maps, core_ids=list(range(8)))
    out = np.stack([r["out_x"].T for r in res.results], 0)
    return out
